# version 16
# baseline (speedup 1.0000x reference)
from contextlib import ExitStack
from dataclasses import dataclass, field

import numpy as np
import ml_dtypes

import concourse.bass as bass
import concourse.mybir as mybir
from concourse.bass_utils import run_bass_kernel_spmd

F32 = mybir.dt.float32
BF16 = mybir.dt.bfloat16
AF = mybir.ActivationFunctionType
ALU = mybir.AluOpType
AX = mybir.AxisListType
BFNP = ml_dtypes.bfloat16

EXP_BIAS = -48.0
BIS_LO, BIS_HI = -16.0, 16.0
N_CORES = 8
ENGINES = ("vector", "scalar", "tensor", "gpsimd", "sync")


@dataclass
class Cfg:
    b: int = 2
    seq: int = 2048
    d: int = 1024
    h: int = 16
    dh: int = 64
    mem: int = 16
    topk: int = 64
    e: int = 16
    f: int = 2048
    cap: int = 256
    fsh: int = 2048
    bis_sched: tuple = ("vector",) * 13

    @property
    def tt(self): return self.seq // 128
    @property
    def dch(self): return self.d // 128
    @property
    def jp(self): return ((self.mem + self.seq + 127) // 128) * 128
    @property
    def jcn(self): return self.jp // 128
    @property
    def fch(self): return self.f // 128
    @property
    def fshch(self): return self.fsh // 128
    @property
    def capch(self): return max(1, self.cap // 128)
    @property
    def mr_rounds(self): return self.cap // 8

    def cw(self, g):
        return min((8 * g + 7) // 128 + 1, self.seq // 128)

    def cw_t(self, t):
        return self.cw(t * 16 + 15)

    @property
    def memc(self):
        return self.seq // 128

    def jlist(self, g):
        return list(range(self.cw(g))) + [self.memc]


class Sched:
    NRING = 8
    ERING = 4

    def __init__(self, nc, ctx):
        self.nc = nc
        self.steps = []
        self.last = {}
        self.bar = {}
        self.sems = {}
        for e in ENGINES:
            for i in range(self.ERING):
                self.sems[f"{e}{i}"] = ctx.enter_context(
                    nc.semaphore(f"sem_{e}{i}"))
        for q in ("hdma", "gdma"):
            for i in range(self.NRING):
                self.sems[f"{q}{i}"] = ctx.enter_context(
                    nc.semaphore(f"sem_{q}{i}"))
        self.dma_idx = {"hdma": 0, "gdma": 0}
        self.eng_idx = dict.fromkeys(ENGINES, 0)
        self.ring_prev = {}

    def step(self, eng, fn, deps=()):
        sid = len(self.steps)
        slot = self.eng_idx[eng] % self.ERING
        self.eng_idx[eng] += 1
        dl = [d for d in deps if d is not None]
        if self.bar.get(eng) is not None:
            dl.append(self.bar[eng])
        self.steps.append((eng, fn, dl, False, None, slot))
        self.last[eng] = sid
        return sid

    def dma(self, eng, fn, deps=()):
        sid = len(self.steps)
        q = "gdma" if eng == "gpsimd" else "hdma"
        slot = self.dma_idx[q] % self.NRING
        self.dma_idx[q] += 1
        dl = [d for d in deps if d is not None]
        prev = self.ring_prev.get((q, slot))
        if prev is not None:
            dl.append(prev)
        if self.bar.get(eng) is not None:
            dl.append(self.bar[eng])
        self.steps.append((eng, fn, dl, True, q, slot))
        self.last[eng] = sid
        self.last[f"{q}{slot}"] = sid
        self.ring_prev[(q, slot)] = sid
        return sid

    def barrier(self):
        snap = dict(self.last)
        newbar = {}
        for e in ENGINES:
            deps = [v for k, v in snap.items() if v is not None]
            slot = self.eng_idx[e] % self.ERING
            self.eng_idx[e] += 1
            sid = len(self.steps)
            self.steps.append((e, lambda x: x.nop(), deps, False, None, slot))
            self.last[e] = sid
            newbar[e] = sid
        self.bar = newbar

    def emit(self):
        done = {}
        cnt = {}
        for i, (eng, fn, deps, dma, q, slot) in enumerate(self.steps):
            if dma:
                key = f"{q}{slot}"
                cnt[key] = cnt.get(key, 0) + 16
                done[i] = (key, cnt[key])
            else:
                key = f"{eng}{slot}"
                cnt[key] = cnt.get(key, 0) + 1
                done[i] = (key, cnt[key])
        per_engine = {e: [] for e in ENGINES}
        for i, st in enumerate(self.steps):
            per_engine[st[0]].append((i, st))

        def section(ename):
            def body(e):
                waited = {}
                for i, (eng, fn, deps, dma, q, slot) in per_engine[ename]:
                    for d in deps:
                        s, v = done[d]
                        if waited.get(s, -1) >= v:
                            continue
                        e.wait_ge(self.sems[s], v)
                        waited[s] = v
                    instr = fn(e)
                    if instr is None:
                        instr = e.nop()
                    s, v = done[i]
                    instr.then_inc(self.sems[s], 16 if dma else 1)
            return body

        finals = dict(cnt)

        def sync_section(e):
            section("sync")(e)
            for s, v in finals.items():
                e.wait_ge(self.sems[s], v)

        with self.nc.Block() as blk:
            blk.vector(section("vector"))
            blk.scalar(section("scalar"))
            blk.tensor(section("tensor"))
            blk.gpsimd(section("gpsimd"))
            blk.sync(sync_section)


def host_constants(cfg):
    c = {}
    c["identb"] = np.eye(128, dtype=BFNP)
    c["identf"] = np.eye(128, dtype=np.float32)
    e8 = np.zeros((8, 128), np.float32)
    for r in range(8):
        e8[r, r * 16:(r + 1) * 16] = 1.0
    c["e8c"] = e8.astype(BFNP)
    c["iota_cap"] = np.tile(np.arange(1, cfg.cap + 1, dtype=np.float32),
                            (128, 1)).astype(BFNP)
    c["onesbf"] = np.ones((1, 128), BFNP)
    c["onesff"] = np.ones((16, 128), np.float32)
    c["ones8b"] = np.ones((8, 128), BFNP)
    return c


class Build:

    def __init__(self, cfg, debug=()):
        self.cfg = cfg
        self.debug = set(debug)
        self.nc = bass.Bass("TRN2", target_bir_lowering=False)
        self.dbg = {}

    def din(self, name, shape, dtype=F32):
        return self.nc.dram_tensor(name, list(shape), dtype,
                                   kind="ExternalInput").ap()

    def dout(self, name, shape, dtype=F32):
        return self.nc.dram_tensor(name, list(shape), dtype,
                                   kind="ExternalOutput").ap()

    def dtmp(self, name, shape, dtype=F32):
        return self.nc.dram_tensor(name, list(shape), dtype,
                                   kind="Internal").ap()

    def ddbg(self, name, shape, dtype=F32):
        t = self.dout("dbg_" + name, shape, dtype)
        self.dbg[name] = t
        return t

    def build(self):
        cg = self.cfg
        nc = self.nc
        B, SEQ, D, H, DH, MEM = cg.b, cg.seq, cg.d, cg.h, cg.dh, cg.mem
        E, F, CAP, FSH = cg.e, cg.f, cg.cap, cg.fsh
        TT, DCH, JP, JCN = cg.tt, cg.dch, cg.jp, cg.jcn
        FCH, FSHCH, CAPCH = cg.fch, cg.fshch, cg.capch
        SCH = SEQ // 128
        self.d_x = self.din("x", [B, SEQ, D])
        self.d_wq = self.din("wq", [D, D])
        self.d_wk = self.din("wk", [D, D])
        self.d_wv = self.din("wv", [D, D])
        self.d_wo = self.din("wo", [D, D])
        self.d_pre = self.din("pre", [H, H])
        self.d_post = self.din("post", [H, H])
        self.d_memk = self.din("memk", [H, DH, MEM])
        self.d_memv = self.din("memv", [H, DH, MEM])
        self.d_bo = self.din("bo", [1, D])
        self.d_gatewT = self.din("gatewT", [D, E])
        self.d_wgT = self.din("wgT", [E, D, F])
        self.d_wuT = self.din("wuT", [E, D, F])
        self.d_wdT = self.din("wdT", [E, F, D])
        self.d_wsgT = self.din("wsgT", [D, FSH])
        self.d_wsuT = self.din("wsuT", [D, FSH])
        self.d_wsdT = self.din("wsdT", [FSH, D])
        self.d_identb = self.din("identb", [128, 128], BF16)
        self.d_identf = self.din("identf", [128, 128], F32)
        self.d_e8c = self.din("e8c", [8, 128], BF16)
        self.d_iotacap = self.din("iota_cap", [128, CAP], BF16)
        self.d_onesbf = self.din("onesbf", [1, 128], BF16)
        self.d_onesff = self.din("onesff", [16, 128], F32)
        self.d_ones8b = self.din("ones8b", [8, 128], BF16)

        self.d_xnfm = self.dtmp("xnfm", [B, DCH, 128, SEQ], BF16)
        self.d_st = self.dtmp("stm", [B, E, SCH, 128, CAP], BF16)
        self.d_eo = self.dtmp("eo", [B, E, CAPCH * 128, D], BF16)
        self.d_g = self.dtmp("gbuf", [B, TT, 128, D], BF16)
        self.d_fsc = self.dtmp("fsc", [B, TT, 128, D], F32)
        self.d_qfm = self.dtmp("qfm", [B, DCH, 128, SEQ], BF16)
        self.d_out = self.dout("out", [B, SEQ, D])

        with ExitStack() as ctx:
            self.ctx = ctx
            self.sc = Sched(nc, ctx)
            import os
            stop = int(os.environ.get("KPHASES", "99"))
            self.alloc_persistent()
            self.ph_consts()
            self.sc.barrier()
            self._csc.close()
            with ExitStack() as xnsc:
                self.xntm = self.sb("xn_tm",
                                    [128, self.cfg.b * self.cfg.tt *
                                     self.cfg.d], BF16, xnsc)
                if stop >= 1:
                    self.ph_layernorm()
                    self.sc.barrier()
                if stop >= 2:
                    self.ph_routing()
                    self.sc.barrier()
                if stop >= 3:
                    self.ph_experts()
                    self.sc.barrier()
            if stop >= 4:
                self.ph_scatter()
                self.sc.barrier()
            if stop >= 5:
                self.ph_shared()
                self.sc.barrier()
            if stop >= 6:
                self.ph_attention()
            self.sc.emit()
        return nc

    def sb(self, name, shape, dtype=F32, scope=None):
        return (scope or self.ctx).enter_context(
            self.nc.sbuf_tensor("sb_" + name, list(shape), dtype))

    def ps(self, name, shape, dtype=F32, scope=None):
        return (scope or self.ctx).enter_context(
            self.nc.psum_tensor("ps_" + name, list(shape), dtype))

    def alloc_persistent(self):
        cg = self.cfg
        self.identb = self.sb("identb", [128, 128], BF16)
        self.e8c = self.sb("e8c", [8, 128], BF16)
        self.onesbf = self.sb("onesbf", [1, 128], BF16)
        self.onesff = self.sb("onesff", [16, 128], F32)
        self.ones8b = self.sb("ones8b", [8, 128], BF16)
        self.prepat = self.sb("prepat", [128, 128], BF16)
        self.cexp = self.sb("cexp", [128, 1], F32)
        self.postpat = self.sb("postpat", [128, 128], BF16)
        self.bo_rowb = self.sb("bo_rowb", [1, cg.d], BF16)
        self.wcol = self.sb("wcol", [128, cg.b * cg.e * cg.capch], F32)

    def xnt(self, b, t):
        o = (b * self.cfg.tt + t) * self.cfg.d
        return self.xntm[:, o:o + self.cfg.d]

    def ph_consts(self):
        sc, cg = self.sc, self.cfg
        csc = ExitStack()
        self._csc = csc
        ppf = self.sb("ppf", [128, 256], F32, csc)
        ld = []
        for dst, src in [(self.identb, self.d_identb),
                         (self.e8c, self.d_e8c),
                         (self.onesbf, self.d_onesbf),
                         (self.onesff, self.d_onesff),
                         (self.ones8b, self.d_ones8b)]:
            ld.append(sc.dma("sync", lambda sy, dst=dst, src=src:
                             sy.dma_start(out=dst[:], in_=src)))
        self.c_ld = [ld[0], None, ld[1], None, ld[2], ld[3], ld[4]]
        ld = self.c_ld
        self.s_bo_ld = sc.dma("gpsimd", lambda g: g.dma_start(
            out=self.bo_rowb[:], in_=self.d_bo))
        z0 = sc.step("vector", lambda v: v.memset(ppf[:], 0.0))
        self.s_cexp = sc.step("vector", lambda v: v.memset(
            self.cexp[:], EXP_BIAS))
        pp = []
        for i8 in range(8):
            pp.append(sc.dma("sync", lambda sy, i8=i8: sy.dma_start(
                out=ppf[i8 * 16:(i8 + 1) * 16, i8 * 16:(i8 + 1) * 16],
                in_=self.d_pre), deps=[z0]))
            pp.append(sc.dma("sync", lambda sy, i8=i8: sy.dma_start(
                out=ppf[i8 * 16:(i8 + 1) * 16,
                        128 + i8 * 16:128 + (i8 + 1) * 16],
                in_=self.d_post), deps=[z0]))
        self.s_pre = sc.step("vector", lambda v: v.tensor_copy(
            self.prepat[:], ppf[:, 0:128]), deps=pp)
        self.s_post = sc.step("vector", lambda v: v.tensor_copy(
            self.postpat[:], ppf[:, 128:256]), deps=pp)

    def ph_layernorm(self):
        sc, cg = self.sc, self.cfg
        B, TT, D, DCH = cg.b, cg.tt, cg.d, cg.dch
        with ExitStack() as lsc:
            ln_x = self.sb("ln_x", [128, 2 * D], F32, lsc)
            ln_sq = self.sb("ln_sq", [128, D], F32, lsc)
            ln_st = self.sb("ln_st", [128, 16], F32, lsc)
            ln_stg = self.sb("ln_stg", [128, 2 * 128], BF16, lsc)
            ln_ps = self.ps("ln_ps", [128, 2048], BF16, lsc)
            last_xn = [None, None]
            last_tr = [None, None]
            self.ln_fm = {}
            if "ln" in self.debug:
                d_dbgxn = self.ddbg("xn", [B, cg.seq, D])
            prev_sq = None
            for i in range(B * TT):
                b, t = divmod(i, TT)
                bi = i % 2
                xt = ln_x[:, bi * D:(bi + 1) * D]
                st = ln_st[:, bi * 8:bi * 8 + 8]
                s_ld = sc.dma("sync", lambda sy, b=b, t=t, xt=xt: sy.dma_start(
                    out=xt, in_=self.d_x[b, t * 128:(t + 1) * 128, :]),
                    deps=[last_xn[bi]])
                s_sum = sc.step("vector", lambda v, xt=xt, st=st: v.reduce_sum(
                    st[:, 0:1], xt, axis=AX.X), deps=[s_ld, last_xn[bi]])
                s_sq = sc.step("scalar", lambda s, xt=xt, st=st: s.activation(
                    ln_sq[:], xt, AF.Square, bias=0.0, scale=1.0,
                    accum_out=st[:, 1:2]), deps=[s_ld, prev_sq, last_xn[bi]])
                prev_sq = s_sq
                s_m = sc.step("vector", lambda v, st=st: v.tensor_single_scalar(
                    st[:, 2:3], st[:, 0:1], 1.0 / D, op=ALU.mult), deps=[s_sum])
                s_m2 = sc.step("vector", lambda v, st=st: v.tensor_single_scalar(
                    st[:, 3:4], st[:, 1:2], 1.0 / D, op=ALU.mult), deps=[s_sq])
                s_msq = sc.step("vector", lambda v, st=st: v.tensor_tensor(
                    st[:, 4:5], st[:, 2:3], st[:, 2:3], op=ALU.mult),
                    deps=[s_m])
                s_var = sc.step("vector", lambda v, st=st: v.scalar_tensor_tensor(
                    st[:, 5:6], st[:, 3:4], 1e-5, st[:, 4:5],
                    op0=ALU.add, op1=ALU.subtract), deps=[s_m2, s_msq])
                s_sd = sc.step("scalar", lambda s, st=st: s.activation(
                    st[:, 6:7], st[:, 5:6], AF.Sqrt, bias=0.0, scale=1.0),
                    deps=[s_var])
                s_r = sc.step("vector", lambda v, st=st: v.reciprocal(
                    st[:, 7:8], st[:, 6:7]), deps=[s_sd])
                s_xn = sc.step("vector", lambda v, xt=xt, st=st, b=b, t=t:
                               v.tensor_scalar(
                                   self.xnt(b, t), xt, st[:, 2:3], st[:, 7:8],
                                   op0=ALU.subtract, op1=ALU.mult),
                               deps=[s_r, s_ld])
                last_xn[bi] = s_xn
                if "ln" in self.debug:
                    sc.dma("sync", lambda sy, b=b, t=t: sy.dma_start(
                        out=d_dbgxn[b, t * 128:(t + 1) * 128, :],
                        in_=self.xnt(b, t)), deps=[s_xn])
                for dc in range(DCH):
                    pbi = (i * DCH + dc) % 2
                    s_tr = sc.step("tensor", lambda te, b=b, t=t, dc=dc, pbi=pbi:
                                   te.transpose(
                                       ln_ps[:, pbi * 1024:pbi * 1024 + 128],
                                       self.xnt(b, t)[:, dc * 128:(dc + 1) * 128],
                                       self.identb[:]),
                                   deps=[s_xn, self.c_ld[0], last_tr[pbi]])
                    s_ev = sc.step("scalar", lambda s, pbi=pbi: s.copy(
                        ln_stg[:, pbi * 128:(pbi + 1) * 128],
                        ln_ps[:, pbi * 1024:pbi * 1024 + 128]), deps=[s_tr])
                    s_stm = sc.dma("sync", lambda sy, b=b, t=t, dc=dc, pbi=pbi:
                                   sy.dma_start(
                                       out=self.d_xnfm[b, dc, :,
                                                       t * 128:(t + 1) * 128],
                                       in_=ln_stg[:, pbi * 128:(pbi + 1) * 128]),
                                   deps=[s_ev])
                    last_tr[pbi] = s_stm
                    self.ln_fm[(b, dc, t)] = s_stm
            self.ln_done = [s for s in last_xn + last_tr if s is not None]

    def ph_routing(self):
        sc, cg = self.sc, self.cfg
        B, SEQ, D, E, CAP = cg.b, cg.seq, cg.d, cg.e, cg.cap
        DCH, CAPCH = cg.dch, cg.capch
        SCH = SEQ // 128
        with ExitStack() as rsc:
            gtw = self.sb("gtw", [128, DCH * E], BF16, rsc)
            fmb = self.sb("fmb", [128, 2 * 512], BF16, rsc)
            sco_b = [self.sb(f"sco{b}", [16, SEQ], F32, rsc) for b in range(B)]
            wrk = self.sb("wrk", [16, SEQ], F32, rsc)
            zro16 = self.sb("zro16", [16, SEQ], F32, rsc)
            m8 = self.sb("m8", [16, 8], F32, rsc)
            msk = self.sb("msk", [16, SEQ], F32, rsc)
            cum16 = self.sb("cum16", [16, SEQ], F32, rsc)
            qT = self.sb("qT", [128, B * SCH * 16], F32, rsc)
            scTb = self.sb("scTb", [128, B * SCH * 16], BF16, rsc)
            stb = self.sb("stb", [128, SCH * CAP], BF16, rsc)
            wrow = self.sb("wrow", [1, CAP], BF16, rsc)
            elb = self.sb("elb", [16, SEQ], F32, rsc)
            rcp = self.sb("rcp", [1, 512], F32, rsc)
            identf = self.sb("identf", [16, 16], F32, rsc)
            iotacap = self.sb("iotacap", [128, CAP], BF16, rsc)
            l_idf = sc.dma("sync", lambda sy: sy.dma_start(
                out=identf[:], in_=self.d_identf[0:16, 0:16]))
            l_iot = sc.dma("sync", lambda sy: sy.dma_start(
                out=iotacap[:], in_=self.d_iotacap))
            lps = self.ps("lps", [16, 512], F32, rsc)
            l2ps = self.ps("l2ps", [1, 512], F32, rsc)
            l3ps = self.ps("l3ps", [16, 512], F32, rsc)
            trps = self.ps("trps", [128, 32], F32, rsc)
            wrps = self.ps("wrps", [1, CAP], F32, rsc)
            wcps = self.ps("wcps", [128, CAPCH], F32, rsc)

            s_gtw = sc.dma("gpsimd", lambda g: g.dma_start(
                out=gtw[:].rearrange("p (c e) -> p c e", e=E),
                in_=self.d_gatewT.rearrange("(c p) e -> p c e", p=128)))
            z16 = sc.step("vector", lambda v: v.memset(zro16[:], 0.0))
            W5 = min(512, SEQ)
            NW = SEQ // W5
            rt_q = {}
            prev_weldc = None
            for b in range(B):
                sc_done = []
                prev_s = None
                for w5 in range(NW):
                    t0 = w5 * W5
                    mms = []
                    for dc in range(DCH):
                        fbi = dc % 2
                        s_l = sc.dma("gpsimd",
                            lambda g, b=b, dc=dc, t0=t0, fbi=fbi: g.dma_start(
                                out=fmb[:, fbi * 512:fbi * 512 + W5],
                                in_=self.d_xnfm[b, dc, :, t0:t0 + W5]),
                            deps=[self.ln_fm[(b, dc, t0 // 128)],
                                  self.ln_fm[(b, dc, min(t0 // 128 + 3,
                                                         cg.tt - 1))],
                                  mms[dc - 2] if dc >= 2 else prev_weldc])
                        mms.append(sc.step("tensor",
                            lambda te, dc=dc, fbi=fbi: te.matmul(
                                lps[:, 0:W5], gtw[:, dc * E:(dc + 1) * E],
                                fmb[:, fbi * 512:fbi * 512 + W5],
                                start=(dc == 0), stop=(dc == DCH - 1)),
                            deps=[s_l, s_gtw, mms[-1] if mms else prev_s]))
                    s_el = sc.step("scalar", lambda s, t0=t0: s.activation(
                        elb[:, t0:t0 + W5], lps[:, 0:W5], AF.Exp,
                        bias=0.0, scale=1.0), deps=[mms[-1]])
                    prev_weldc = s_el
                    s_sm = sc.step("tensor", lambda te, t0=t0: te.matmul(
                        l2ps[:, 0:W5], self.onesff[:, 0:1],
                        elb[:, t0:t0 + W5],
                        start=True, stop=True), deps=[s_el, self.c_ld[5]])
                    s_rc = sc.step("vector", lambda v: v.reciprocal(
                        rcp[:, 0:W5], l2ps[0:1, 0:W5]), deps=[s_sm])
                    s_bc = sc.step("tensor", lambda te: te.matmul(
                        l3ps[:, 0:W5], self.onesff[0:1, 0:16], rcp[:, 0:W5],
                        start=True, stop=True), deps=[s_rc])
                    s_s = sc.step("vector", lambda v, b=b, t0=t0:
                                  v.tensor_tensor(
                                      sco_b[b][:, t0:t0 + W5],
                                      elb[:, t0:t0 + W5], l3ps[:, 0:W5],
                                      op=ALU.mult), deps=[s_bc, s_el])
                    prev_s = s_s
                    sc_done.append(s_s)
                s_cp = sc.step("vector", lambda v, b=b: v.tensor_copy(
                    wrk[:], sco_b[b][:]), deps=sc_done + [rt_q.get(b - 1)])
                prev = s_cp
                for r in range(cg.mr_rounds):
                    s_m8 = sc.step("vector", lambda v: v.max(
                        out=m8[:], in_=wrk[:]), deps=[prev])
                    prev = sc.step("vector", lambda v: v.match_replace(
                        out=wrk[:], in_to_replace=m8[:], in_values=wrk[:],
                        imm_value=0.0), deps=[s_m8])
                s_mk = sc.step("vector", lambda v, b=b: v.tensor_tensor(
                    msk[:], sco_b[b][:], wrk[:], op=ALU.is_gt), deps=[prev])
                s_cu = sc.step("vector", lambda v: v.tensor_tensor_scan(
                    cum16[:], msk[:], zro16[:], 0.0, op0=ALU.add, op1=ALU.add),
                    deps=[s_mk, z16])
                s_q = sc.step("vector", lambda v: v.tensor_tensor(
                    cum16[:], cum16[:], msk[:], op=ALU.mult), deps=[s_cu])
                prev_tr = None
                for ch in range(SCH):
                    s1 = sc.step("tensor", lambda te, ch=ch: te.transpose(
                        trps[:, 0:16], cum16[:, ch * 128:(ch + 1) * 128],
                        identf[:]),
                        deps=[s_q, prev_tr, l_idf])
                    s2 = sc.step("tensor", lambda te, b=b, ch=ch: te.transpose(
                        trps[:, 16:32], sco_b[b][:, ch * 128:(ch + 1) * 128],
                        identf[:]), deps=[s1])
                    o = (b * SCH + ch) * 16
                    e1 = sc.step("scalar", lambda s, o=o: s.copy(
                        qT[:, o:o + 16], trps[:, 0:16]), deps=[s1, s2])
                    e2 = sc.step("scalar", lambda s, o=o: s.copy(
                        scTb[:, o:o + 16], trps[:, 16:32]), deps=[s2, e1])
                    prev_tr = e2
                rt_q[b] = prev_tr

            if "rt" in self.debug:
                d_dbq = self.ddbg("qt", [128, B * SCH * 16])
                d_dbs = self.ddbg("sct", [128, B * SCH * 16], BF16)
                sc.dma("sync", lambda sy: sy.dma_start(out=d_dbq, in_=qT[:]),
                       deps=[rt_q[bb] for bb in range(B)])
                sc.dma("sync", lambda sy: sy.dma_start(out=d_dbs, in_=scTb[:]),
                       deps=[rt_q[bb] for bb in range(B)])

            self.rt_wc = {}
            self.rt_st = {}
            last_st = None
            prev_wr = None
            prev_wc = None
            for b in range(B):
                for e in range(E):
                    sts = []
                    for chk in range(SCH):
                        o = (b * SCH + chk) * 16 + e
                        sts.append(sc.step("vector",
                            lambda v, chk=chk, o=o: v.tensor_single_scalar(
                                stb[:, chk * CAP:(chk + 1) * CAP],
                                iotacap[:], qT[:, o:o + 1],
                                op=ALU.is_equal),
                            deps=[rt_q[b], last_st, l_iot]))
                    wms = []
                    for chk in range(SCH):
                        o = (b * SCH + chk) * 16 + e
                        wms.append(sc.step("tensor",
                            lambda te, chk=chk, o=o: te.matmul(
                                wrps[:, :], scTb[:, o:o + 1],
                                stb[:, chk * CAP:(chk + 1) * CAP],
                                start=(chk == 0), stop=(chk == SCH - 1)),
                            deps=[sts[chk], wms[-1] if wms else prev_wr]))
                    s_wr = sc.step("scalar", lambda s: s.copy(
                        wrow[:], wrps[:]), deps=[wms[-1]])
                    prev_wr = s_wr
                    wcs = []
                    for cc in range(CAPCH):
                        w = min(128, CAP - cc * 128)
                        wcs.append(sc.step("tensor", lambda te, cc=cc, w=w:
                            te.matmul(wcps[:, cc:cc + 1],
                                      wrow[0:1, cc * 128:cc * 128 + w],
                                      self.onesbf[0:1, 0:1],
                                      start=True, stop=True),
                            deps=[s_wr, wcs[-1] if wcs else prev_wc,
                                  self.c_ld[4]]))
                    o = (b * E + e) * CAPCH
                    self.rt_wc[(b, e)] = sc.step("vector", lambda v, o=o:
                        v.tensor_copy(self.wcol[:, o:o + CAPCH], wcps[:]),
                        deps=wcs)
                    prev_wc = self.rt_wc[(b, e)]
                    last_st = sc.dma("sync", lambda sy, b=b, e=e: sy.dma_start(
                        out=self.d_st[b, e].rearrange("c p k -> p c k"),
                        in_=stb[:].rearrange("p (c k) -> p c k", k=CAP)),
                        deps=sts + [wms[-1]])
                    self.rt_st[(b, e)] = last_st
            if "rt" in self.debug:
                d_dbw = self.ddbg("wcol", [128, B * E * CAPCH])
                sc.dma("sync", lambda sy: sy.dma_start(
                    out=d_dbw, in_=self.wcol[:]),
                    deps=list(self.rt_wc.values()))

    def ph_experts(self):
        sc, cg = self.sc, self.cfg
        B, D, E, F, CAP = cg.b, cg.d, cg.e, cg.f, cg.cap
        DCH, CAPCH = cg.dch, cg.capch
        SCH = cg.seq // 128
        FH = F // 2
        FHCH = FH // 128
        BC = B * CAP
        with ExitStack() as esc:
            wgt = self.sb("wgt", [128, 2 * DCH * FH], BF16, esc)
            wut = self.sb("wut", [128, 2 * DCH * FH], BF16, esc)
            wdt = self.sb("wdt", [128, 2 * FHCH * D], BF16, esc)
            stl = self.sb("stl", [128, B * SCH * CAP], BF16, esc)
            xtok = self.sb("xtok", [128, DCH * BC], BF16, esc)
            hbuf = self.sb("hbuf", [128, 2 * FHCH * BC], BF16, esc)
            eo512 = self.sb("eo512", [128, 512], BF16, esc)
            ggb = self.sb("ggb", [128, BC], F32, esc)
            gps = self.ps("gps", [128, BC], F32, esc)
            ups = self.ps("ups", [128, BC], F32, esc)
            xps = self.ps("xps", [128, CAP], F32, esc)
            dps = self.ps("dps", [128, 512], F32, esc)

            stl_use = [None, None]
            w_use = [None, None]
            w_use_d = [None]
            prev_gather = None
            prev_gp = None
            prev_up = None
            prev_hh = None
            ex_last = []
            for e in range(E):
                st_ld = {}
                g_done = {}
                for b in range(B):
                    st_ld[b] = sc.dma("gpsimd",
                        lambda g, b=b, e=e: g.dma_start(
                            out=stl[:, b * SCH * CAP:(b + 1) * SCH * CAP]
                            .rearrange("p (c k) -> p c k", k=CAP),
                            in_=self.d_st[b, e].rearrange("c p k -> p c k")),
                        deps=[self.rt_st[(b, e)], stl_use[b]])
                lw = {}
                for hf in range(2):
                    wbi = (e * 2 + hf) % 2
                    wo_g = wbi * DCH * FH
                    lw[("g", hf)] = sc.dma("gpsimd",
                        lambda g, e=e, hf=hf, wo_g=wo_g: g.dma_start(
                            out=wgt[:, wo_g:wo_g + DCH * FH]
                            .rearrange("p (c f) -> p c f", f=FH),
                            in_=self.d_wgT[e][:, hf * FH:(hf + 1) * FH]
                            .rearrange("(c p) f -> p c f", p=128)),
                        deps=[w_use[wbi]])
                    lw[("u", hf)] = sc.dma("gpsimd",
                        lambda g, e=e, hf=hf, wo_g=wo_g: g.dma_start(
                            out=wut[:, wo_g:wo_g + DCH * FH]
                            .rearrange("p (c f) -> p c f", f=FH),
                            in_=self.d_wuT[e][:, hf * FH:(hf + 1) * FH]
                            .rearrange("(c p) f -> p c f", p=128)),
                        deps=[w_use[wbi]])
                    lw[("d", hf)] = sc.dma("gpsimd",
                        lambda g, e=e, hf=hf: g.dma_start(
                            out=wdt[:, hf * FHCH * D:(hf + 1) * FHCH * D]
                            .rearrange("p (c dd) -> p c dd", dd=D),
                            in_=self.d_wdT[e][hf * FH:(hf + 1) * FH, :]
                            .rearrange("(c p) dd -> p c dd", p=128)),
                        deps=[w_use_d[0]])
                for b in range(B):
                    for dt_ in range(DCH):
                        gmm = []
                        for chk in range(SCH):
                            gmm.append(sc.step("tensor",
                                lambda te, b=b, dt_=dt_, chk=chk:
                                te.matmul(
                                    xps[:, 0:CAP],
                                    self.xnt(b, chk)[:, dt_ * 128:
                                                     (dt_ + 1) * 128],
                                    stl[:, (b * SCH + chk) * CAP:
                                        (b * SCH + chk + 1) * CAP],
                                    start=(chk == 0), stop=(chk == SCH - 1)),
                                deps=[st_ld[b],
                                      self.ln_done[0] if e == 0 else None,
                                      gmm[-1] if gmm else prev_gather]))
                        g_done[(b, dt_)] = sc.step("scalar",
                            lambda s, b=b, dt_=dt_: s.copy(
                                xtok[:, (dt_ * B + b) * CAP:
                                     (dt_ * B + b + 1) * CAP],
                                xps[:, 0:CAP]),
                            deps=[gmm[-1], prev_gather, prev_hh])
                        prev_gather = g_done[(b, dt_)]
                    stl_use[b] = g_done[(b, DCH - 1)]
                h_done = {}
                for hf in range(2):
                    wbi = (e * 2 + hf) % 2
                    wo_g = wbi * DCH * FH
                    for ft in range(FHCH):
                        gm = []
                        for dc in range(DCH):
                            ws = wgt[:, wo_g + dc * FH + ft * 128:
                                     wo_g + dc * FH + ft * 128 + 128]
                            gm.append(sc.step("tensor",
                                lambda te, dc=dc, ws=ws: te.matmul(
                                    gps[:, 0:BC], ws,
                                    xtok[:, dc * BC:(dc + 1) * BC],
                                    start=(dc == 0), stop=(dc == DCH - 1)),
                                deps=[lw[("g", hf)]] +
                                     [g_done[(bb, dc)] for bb in range(B)] +
                                     [gm[-1] if gm else prev_gp]))
                        s_gel = sc.step("scalar", lambda s: s.activation(
                            ggb[:], gps[:, 0:BC], AF.Gelu, bias=0.0,
                            scale=1.0), deps=[gm[-1], prev_hh])
                        prev_gp = s_gel
                        um = []
                        for dc in range(DCH):
                            ws = wut[:, wo_g + dc * FH + ft * 128:
                                     wo_g + dc * FH + ft * 128 + 128]
                            um.append(sc.step("tensor",
                                lambda te, dc=dc, ws=ws: te.matmul(
                                    ups[:, 0:BC], ws,
                                    xtok[:, dc * BC:(dc + 1) * BC],
                                    start=(dc == 0), stop=(dc == DCH - 1)),
                                deps=[lw[("u", hf)]] +
                                     [g_done[(bb, dc)] for bb in range(B)] +
                                     [um[-1] if um else prev_up]))
                        s_h = sc.step("vector",
                            lambda v, hf=hf, ft=ft: v.tensor_tensor(
                                hbuf[:, (hf * FHCH + ft) * BC:
                                     (hf * FHCH + ft + 1) * BC],
                                ggb[:], ups[:, 0:BC], op=ALU.mult),
                            deps=[s_gel, um[-1], prev_hh])
                        prev_up = s_h
                        prev_hh = s_h
                        h_done[(hf, ft)] = s_h
                prev_d = None
                eo_prev = getattr(self, "_eo_prev", [None, None])
                for b in range(B):
                    eo_dmas = []
                    for cc in range(CAPCH):
                        o = (b * E + e) * CAPCH + cc
                        for ns in range(D // 512):
                            ebi = 0
                            dmms = []
                            for hf in range(2):
                                for fc in range(FHCH):
                                    hs = hbuf[:, (hf * FHCH + fc) * BC +
                                              b * CAP + cc * 128:
                                              (hf * FHCH + fc) * BC +
                                              b * CAP + cc * 128 + 128]
                                    wsd = wdt[:, hf * FHCH * D + fc * D +
                                              ns * 512:
                                              hf * FHCH * D + fc * D +
                                              ns * 512 + 512]
                                    dmms.append(sc.step("tensor",
                                        lambda te, hs=hs, wsd=wsd,
                                        fst=(hf == 0 and fc == 0),
                                        lst=(hf == 1 and fc == FHCH - 1):
                                        te.matmul(dps[:, 0:512], hs, wsd,
                                                  start=fst, stop=lst),
                                        deps=[lw[("d", hf)],
                                              h_done[(hf, fc)],
                                              dmms[-1] if dmms else prev_d]))
                            s_ev = sc.step("vector",
                                lambda v, ebi=ebi, o=o:
                                v.tensor_single_scalar(
                                    eo512[:, 0:512],
                                    dps[:, 0:512],
                                    self.wcol[:, o:o + 1], op=ALU.mult),
                                deps=[dmms[-1], self.rt_wc[(b, e)],
                                      eo_prev[ebi]])
                            prev_d = s_ev
                            s_dma = sc.dma("gpsimd",
                                lambda g, b=b, e=e, cc=cc, ns=ns, ebi=ebi:
                                g.dma_start(
                                    out=self.d_eo[b, e].rearrange(
                                        "(c p) dd -> p c dd", p=128)
                                    [:, cc, ns * 512:(ns + 1) * 512],
                                    in_=eo512[:, 0:512]),
                                deps=[s_ev])
                            eo_prev[ebi] = s_dma
                            eo_dmas.append(s_dma)
                    ex_last.extend(eo_dmas)
                    self.ex_eo = getattr(self, "ex_eo", {})
                    self.ex_eo[(b, e)] = eo_dmas[-1]
                    self._eo_prev = eo_prev
                w_use[0 if (e * 2) % 2 == 0 else 1] = prev_d
                w_use[(e * 2 + 1) % 2] = prev_d
                w_use_d[0] = prev_d
            self.ex_done = ex_last

    def ph_scatter(self):
        sc, cg = self.sc, self.cfg
        B, SEQ, D, E, CAP = cg.b, cg.seq, cg.d, cg.e, cg.cap
        CAPCH, TT = cg.capch, cg.tt
        with ExitStack() as ssc:
            eob = self.sb("eob", [128, E * CAPCH * D], BF16, ssc)
            sttb = self.sb("sttb", [128, E * CAP], BF16, ssc)
            s2b = self.sb("s2b", [128, E * CAPCH * 128], BF16, ssc)
            ftile = self.sb("ftile", [128, 2 * D], F32, ssc)
            s2ps = self.ps("s2ps", [128, 128], BF16, ssc)
            acc0 = self.ps("acc0", [128, 512], F32, ssc)
            acc1 = self.ps("acc1", [128, 512], F32, ssc)
            accs = [acc0, acc1]
            prev_ft = [None, None]
            prev_stl = None
            for b in range(B):
                l_eo = []
                for e in range(E):
                    l_eo.append(sc.dma("gpsimd", lambda g, b=b, e=e:
                        g.dma_start(
                            out=eob[:, e * CAPCH * D:(e + 1) * CAPCH * D]
                            .rearrange("p (c dd) -> p c dd", dd=D),
                            in_=self.d_eo[b, e].rearrange(
                                "(c p) dd -> p c dd", p=128)),
                        deps=[self.ex_eo[(b, e)], prev_stl]))
                prev_s2t = None
                for t in range(TT):
                    s_stl = sc.dma("gpsimd", lambda g, b=b, t=t:
                        g.dma_start(
                            out=sttb[:].rearrange("p (e k) -> p e k",
                                                  k=CAP),
                            in_=self.d_st[b, :, t, :, :].rearrange(
                                "e p k -> p e k")),
                        deps=[self.rt_st[(b, E - 1)], prev_stl, prev_s2t])
                    s2_done = []
                    prev_s2 = prev_stl
                    for e in range(E):
                        for cc in range(CAPCH):
                            s_t = sc.step("tensor",
                                lambda te, e=e, cc=cc: te.transpose(
                                    s2ps[:], sttb[:, e * CAP + cc * 128:
                                                  e * CAP + cc * 128 + 128],
                                    self.identb[:]),
                                deps=[s_stl, prev_s2, self.c_ld[0]])
                            s_e = sc.step("scalar",
                                lambda s, e=e, cc=cc: s.copy(
                                    s2b[:, (e * CAPCH + cc) * 128:
                                        (e * CAPCH + cc + 1) * 128],
                                    s2ps[:]), deps=[s_t])
                            prev_s2 = s_e
                            prev_s2t = s_e
                            s2_done.append(s_e)
                    evs = []
                    for ns in range(D // 512):
                        mml = []
                        for e in range(E):
                            for cc in range(CAPCH):
                                first = (e == 0 and cc == 0)
                                mml.append(sc.step("tensor",
                                    lambda te, e=e, cc=cc, ns=ns, first=first:
                                    te.matmul(
                                        accs[ns][:],
                                        s2b[:, (e * CAPCH + cc) * 128:
                                            (e * CAPCH + cc + 1) * 128],
                                        eob[:, e * CAPCH * D + cc * D +
                                            ns * 512:
                                            e * CAPCH * D + cc * D +
                                            ns * 512 + 512],
                                        start=first,
                                        stop=(e == E - 1 and
                                              cc == CAPCH - 1)),
                                    deps=[s2_done[e * CAPCH + cc], l_eo[e],
                                          mml[-1] if mml else
                                          prev_ft[ns]]))
                        tbi = t % 2
                        s_ev = sc.step("vector", lambda v, ns=ns, tbi=tbi:
                            v.tensor_copy(
                                ftile[:, tbi * D + ns * 512:
                                      tbi * D + ns * 512 + 512],
                                accs[ns][:]),
                            deps=[mml[-1], prev_ft[ns]])
                        prev_ft[ns] = s_ev
                        evs.append(s_ev)
                    s_fs = sc.dma("sync", lambda sy, b=b, t=t, tbi=t % 2:
                        sy.dma_start(out=self.d_fsc[b, t],
                                     in_=ftile[:, tbi * D:(tbi + 1) * D]),
                        deps=evs)
                    prev_ft = [s_fs, s_fs]
                    prev_stl = s_fs if t == TT - 1 else prev_stl
                    self.fsc_done = getattr(self, "fsc_done", {})
                    self.fsc_done[(b, t)] = s_fs

    def ph_shared(self):
        sc, cg = self.sc, self.cfg
        B, SEQ, D, FSH = cg.b, cg.seq, cg.d, cg.fsh
        DCH, FSHCH, TT = cg.dch, cg.fshch, cg.tt
        with ExitStack() as ssc:
            xnfmb = self.sb("xnfmb", [128, DCH * SEQ], BF16, ssc)
            wsg = self.sb("wsg", [128, DCH * FSH], BF16, ssc)
            wsu = self.sb("wsu", [128, DCH * FSH], BF16, ssc)
            wsd = self.sb("wsd", [128, FSHCH * D], BF16, ssc)
            hsb = self.sb("hsb", [128, FSHCH * 512], BF16, ssc)
            gsg = self.sb("gsg", [128, 512], F32, ssc)
            gtile = self.sb("gtile", [128, D], BF16, ssc)
            ptl = self.sb("ptl", [128, 2 * D], F32, ssc)
            sps = self.ps("sps", [128, 512], F32, ssc)
            sus = self.ps("sus", [128, 512], F32, ssc)
            acc0 = self.ps("sacc0", [128, 512], F32, ssc)
            acc1 = self.ps("sacc1", [128, 512], F32, ssc)
            accs = [acc0, acc1]
            lws = []
            lws.append(sc.dma("gpsimd", lambda g: g.dma_start(
                out=wsg[:].rearrange("p (c f) -> p c f", f=FSH),
                in_=self.d_wsgT.rearrange(
                    "(c p) f -> p c f", p=128))))
            lws.append(sc.dma("gpsimd", lambda g: g.dma_start(
                out=wsu[:].rearrange("p (c f) -> p c f", f=FSH),
                in_=self.d_wsuT.rearrange(
                    "(c p) f -> p c f", p=128))))
            lws.append(sc.dma("gpsimd", lambda g: g.dma_start(
                out=wsd[:].rearrange("p (c dd) -> p c dd", dd=D),
                in_=self.d_wsdT.rearrange(
                    "(c p) dd -> p c dd", p=128))))
            self.g_done = {}
            prev_g = None
            prev_hs = None
            prev_gel = None
            prev_acc = [None, None]
            prev_ptl = [None, None]
            for b in range(B):
                l_fm = []
                for dc in range(DCH):
                    l_fm.append(sc.dma("sync", lambda sy, b=b, dc=dc:
                        sy.dma_start(
                            out=xnfmb[:, dc * SEQ:(dc + 1) * SEQ],
                            in_=self.d_xnfm[b, dc]),
                        deps=[self.ln_fm[(b, dc, TT - 1)],
                              self.g_done.get((b - 1, TT - 1))]))
                for t4 in range(0, TT, 4):
                    nt = min(4, TT - t4)
                    nw4 = nt * 128
                    h_done = []
                    for fc in range(FSHCH):
                        gmm = []
                        for dc in range(DCH):
                            ws = wsg[:, dc * FSH + fc * 128:
                                     dc * FSH + fc * 128 + 128]
                            gmm.append(sc.step("tensor",
                                lambda te, t4=t4, dc=dc, ws=ws, nw4=nw4:
                                te.matmul(
                                    sps[:, 0:nw4], ws,
                                    xnfmb[:, dc * SEQ + t4 * 128:
                                          dc * SEQ + t4 * 128 + nw4],
                                    start=(dc == 0), stop=(dc == DCH - 1)),
                                deps=[lws[0], l_fm[dc],
                                      gmm[-1] if gmm else prev_gel]))
                        s_gel = sc.step("scalar",
                            lambda s, nw4=nw4: s.activation(
                                gsg[:, 0:nw4], sps[:, 0:nw4], AF.Gelu,
                                bias=0.0, scale=1.0),
                            deps=[gmm[-1], prev_hs])
                        prev_gel = s_gel
                        umm = []
                        for dc in range(DCH):
                            ws = wsu[:, dc * FSH + fc * 128:
                                     dc * FSH + fc * 128 + 128]
                            umm.append(sc.step("tensor",
                                lambda te, t4=t4, dc=dc, ws=ws, nw4=nw4:
                                te.matmul(
                                    sus[:, 0:nw4], ws,
                                    xnfmb[:, dc * SEQ + t4 * 128:
                                          dc * SEQ + t4 * 128 + nw4],
                                    start=(dc == 0), stop=(dc == DCH - 1)),
                                deps=[lws[1], l_fm[dc],
                                      umm[-1] if umm else prev_hs]))
                        s_h = sc.step("vector", lambda v, fc=fc, nw4=nw4:
                                      v.tensor_tensor(
                                          hsb[:, fc * 512:fc * 512 + nw4],
                                          gsg[:, 0:nw4], sus[:, 0:nw4],
                                          op=ALU.mult),
                                      deps=[s_gel, umm[-1], prev_hs])
                        prev_hs = s_h
                        h_done.append(s_h)
                    for tt_ in range(nt):
                        t = t4 + tt_
                        tbi = t % 2
                        s_pl = sc.dma("sync", lambda sy, b=b, t=t, tbi=tbi:
                            sy.dma_start(out=ptl[:, tbi * D:(tbi + 1) * D],
                                         in_=self.d_fsc[b, t]),
                            deps=[self.fsc_done[(b, t)], prev_ptl[tbi]])
                        sil = []
                        for ns in range(D // 512):
                            mml = []
                            for fc in range(FSHCH):
                                ws = wsd[:, fc * D + ns * 512:
                                         fc * D + ns * 512 + 512]
                                mml.append(sc.step("tensor",
                                    lambda te, fc=fc, ns=ns, ws=ws, tt_=tt_:
                                    te.matmul(
                                        accs[ns][:],
                                        hsb[:, fc * 512 + tt_ * 128:
                                            fc * 512 + tt_ * 128 + 128],
                                        ws,
                                        start=(fc == 0),
                                        stop=(fc == FSHCH - 1)),
                                    deps=[lws[2], h_done[fc],
                                          mml[-1] if mml else prev_acc[ns]]))
                            s_ad = sc.step("vector",
                                lambda v, ns=ns, tbi=tbi: v.tensor_tensor(
                                    ptl[:, tbi * D + ns * 512:
                                        tbi * D + ns * 512 + 512],
                                    ptl[:, tbi * D + ns * 512:
                                        tbi * D + ns * 512 + 512],
                                    accs[ns][:], op=ALU.add),
                                deps=[mml[-1], s_pl])
                            prev_acc[ns] = s_ad
                            s_si = sc.step("scalar",
                                lambda s, ns=ns, tbi=tbi: s.activation(
                                    gtile[:, ns * 512:(ns + 1) * 512],
                                    ptl[:, tbi * D + ns * 512:
                                        tbi * D + ns * 512 + 512],
                                    AF.Silu, bias=0.0, scale=1.0),
                                deps=[s_ad, prev_g])
                            sil.append(s_si)
                        s_g = sc.dma("sync", lambda sy, b=b, t=t: sy.dma_start(
                            out=self.d_g[b, t], in_=gtile[:]), deps=sil)
                        self.g_done[(b, t)] = s_g
                        prev_g = s_g
                        prev_ptl[tbi] = s_g

    def ph_attention(self):
        import os
        KATT = int(os.environ.get("KATT", "99"))
        sc, cg = self.sc, self.cfg
        B, SEQ, D, H, DH, MEM = cg.b, cg.seq, cg.d, cg.h, cg.dh, cg.mem
        DCH, JP, JCN, TT = cg.dch, cg.jp, cg.jcn, cg.tt
        MEMC = cg.memc
        TOPK = float(cg.topk)
        NBIS = len(cg.bis_sched)
        with ExitStack() as asc:
            wob = self.sb("wob", [128, DCH * D], BF16, asc)
            kbuf = self.sb("kbuf", [128, DCH * JP], BF16, asc)
            vT = self.sb("vT", [128, JCN * D], BF16, asc)
            qtl = self.sb("qtl", [128, 2 * DCH * 128], BF16, asc)
            scr = self.sb("scr", [128, 2 * 2048], BF16, asc)
            csb = self.sb("csb", [128, 2 * JP], BF16, asc)
            pkb = self.sb("pkb", [128, 256], BF16, asc)
            xrow = self.sb("xrow", [128, 16 * JP], BF16, asc)
            bst = self.sb("bst", [128, 16 * 8], F32, asc)
            wjoff = self.sb("wjoff", [128, TT], F32, asc)
            bstb = self.sb("bstb", [128, 16 * 2], BF16, asc)
            patgs = self.sb("patgs", [128, 16 * 128], BF16, asc)
            fpbig = self.sb("fpbig", [8, 256], BF16, asc)
            fpad = self.sb("fpad", [8, 128], BF16, asc)
            atb = self.sb("atb", [128, 2 * 2048], BF16, asc)
            obuf = self.sb("obuf", [128, DCH * 128], BF16, asc)
            gtl = self.sb("gtl", [128, D], BF16, asc)
            xtl = self.sb("xtl", [128, D], F32, asc)
            otl = self.sb("otl", [128, D], F32, asc)
            wstr = self.sb("wstr", [128, 2 * D], BF16, asc)
            xnstr = scr
            vstg = csb
            mstg = self.sb("mstg", [128, MEM], BF16, asc)
            dtp = self.ps("dtp", [128, 512], F32, asc)
            pkp = self.ps("pkp", [128, 1024], BF16, asc)
            pm1 = self.ps("pm1", [128, 512], F32, asc)
            pob = self.ps("pob", [128, 512], F32, asc)
            ovp = self.ps("ovp", [64, 2048], F32, asc)
            DSLOT = [0, 512]

            def bcol(gl, j):
                return bst[:, gl * 8 + j:gl * 8 + j + 1]

            s_lwo = sc.dma("gpsimd", lambda g: g.dma_start(
                out=wob[:].rearrange("p (c o) -> p c o", o=D),
                in_=self.d_wo.rearrange(
                    "(c p) o -> p c o", p=128)))
            wj_prev = None
            for tw in range(TT):
                wjv = 2.0 * TOPK - len(cg.jlist(tw * 16)) * 128
                wj_prev = sc.step("gpsimd",
                    lambda g, tw=tw, wjv=wjv: g.memset(
                        wjoff[:, tw:tw + 1], wjv),
                    deps=[wj_prev])
            s_fb0 = sc.step("gpsimd", lambda gg: gg.memset(fpbig[:], 1.0))
            s_fb1 = sc.step("gpsimd", lambda gg: gg.affine_select(
                fpbig[:], fpbig[:], pattern=[[-1, 256]],
                compare_op=ALU.is_ge, fill=0.0, base=120,
                channel_multiplier=1), deps=[s_fb0])
            s_fbig = sc.step("gpsimd", lambda gg: gg.tensor_scalar(
                fpbig[:], fpbig[:], 57344.0, -57344.0,
                op0=ALU.mult, op1=ALU.add), deps=[s_fb1])
            s_fp0 = sc.step("gpsimd", lambda gg: gg.affine_select(
                fpad[:], self.ones8b[:], pattern=[[-1, 128]],
                compare_op=ALU.is_ge, fill=0.0, base=MEM - 1,
                channel_multiplier=0), deps=[self.c_ld[6]])
            s_fpad = sc.step("gpsimd", lambda gg: gg.tensor_scalar(
                fpad[:], fpad[:], 57344.0, -57344.0,
                op0=ALU.mult, op1=ALU.add), deps=[s_fp0])

            for b in range(B):
                prev_b = getattr(self, "attn_b_done", None)
                zk = sc.step("vector", lambda v: v.memset(kbuf[:], 0.0),
                             deps=[prev_b])
                zv = sc.step("vector", lambda v: v.memset(vT[:], 0.0),
                             deps=[prev_b])
                kd_extra = [zk]
                prev_ms = prev_b
                for oc in range(DCH):
                    l_mk = sc.dma("gpsimd", lambda g, oc=oc: g.dma_start(
                        out=kbuf[:, oc * JP + SEQ:oc * JP + SEQ + MEM],
                        in_=self.d_memk[2 * oc:2 * oc + 2].rearrange(
                            "h d m -> (h d) m")), deps=[zk])
                    kd_extra.append(l_mk)
                    l_mv = sc.dma("gpsimd", lambda g, oc=oc: g.dma_start(
                        out=mstg[:],
                        in_=self.d_memv[2 * oc:2 * oc + 2].rearrange(
                            "h d m -> (h d) m")), deps=[prev_ms])
                    s_mt = sc.step("tensor", lambda te: te.transpose(
                        pkp[0:MEM, 0:128], mstg[:], self.identb[:]),
                        deps=[l_mv, prev_ms, self.c_ld[0]])
                    s_mc = sc.step("scalar", lambda s, oc=oc: s.copy(
                        vT[0:MEM, MEMC * D + oc * 128:
                           MEMC * D + (oc + 1) * 128],
                        pkp[0:MEM, 0:128]), deps=[s_mt, zv])
                    prev_ms = s_mc
                    kd_extra.append(s_mc)
                qkv_done = {"q": {}, "k": {}, "v": {}}
                vt_done = {}
                prev_ev = None
                prev_w = [prev_b, prev_b]
                prev_xn = None
                prev_vstg = prev_ms
                prev_qst = [prev_b, prev_b]
                TW = min(512, SEQ)
                for t5 in range(SEQ // TW):
                    l_xn = sc.dma("sync", lambda sy, b=b, t5=t5, TW=TW:
                        sy.dma_start(
                        out=xnstr[:, 0:DCH * TW]
                        .rearrange("p (c w) -> p c w", c=DCH),
                        in_=self.d_xnfm[b, :, :, t5 * TW:(t5 + 1) * TW]
                        .rearrange("c p w -> p c w")),
                        deps=[self.ln_fm[(b, DCH - 1, TT - 1)], prev_xn,
                              prev_b])
                    prev_ev2 = {0: prev_ev, 1: prev_b}
                    for ni, (nm, dsrc) in enumerate(
                            [("q", self.d_wq), ("k", self.d_wk),
                             ("v", self.d_wv)]):
                        for oc in range(DCH):
                            wbi = (ni * DCH + oc) % 2
                            pbi = (ni * DCH + oc) % 2
                            pml = pm1 if pbi == 0 else pob
                            prev_ev = prev_ev2[pbi]
                            l_w = sc.dma("gpsimd",
                                lambda g, dsrc=dsrc, oc=oc, wbi=wbi:
                                g.dma_start(
                                    out=wstr[:, wbi * D:wbi * D + D]
                                    .rearrange("p (c o) -> p c o", o=128),
                                    in_=dsrc[:, oc * 128:(oc + 1) * 128]
                                    .rearrange("(c p) o -> p c o", p=128)),
                                deps=[prev_w[wbi]])
                            mm = []
                            for dc in range(DCH):
                                ws = wstr[:, wbi * D + dc * 128:
                                          wbi * D + dc * 128 + 128]
                                mm.append(sc.step("tensor",
                                    lambda te, ws=ws, dc=dc, pml=pml:
                                    te.matmul(
                                        pml[:, 0:TW], ws,
                                        xnstr[:, dc * TW:(dc + 1) * TW],
                                        start=(dc == 0),
                                        stop=(dc == DCH - 1)),
                                    deps=[l_w, l_xn,
                                          mm[-1] if mm else prev_ev]))
                            prev_w[wbi] = mm[-1]
                            if nm == "q":
                                pass
                            if nm == "q":
                                qi = oc % 2
                                tgt = qtl[:, qi * TW:(qi + 1) * TW]
                                s_e0 = sc.step("scalar",
                                    lambda s, tgt=tgt, pml=pml: s.activation(
                                        tgt, pml[:, 0:TW], AF.Copy,
                                        bias=0.0, scale=0.125),
                                    deps=[mm[-1], prev_ev, prev_qst[qi]])
                                s_e = sc.dma("sync",
                                    lambda sy, b=b, oc=oc, t5=t5, tgt=tgt,
                                    TW=TW:
                                    sy.dma_start(
                                        out=self.d_qfm[b, oc, :,
                                            t5 * TW:(t5 + 1) * TW],
                                        in_=tgt), deps=[s_e0])
                                prev_qst[qi] = s_e
                                prev_ev = s_e0
                            elif nm == "k":
                                tb = kbuf[:, oc * JP + t5 * TW:
                                          oc * JP + t5 * TW + TW]
                                s_e = sc.step("scalar",
                                    lambda s, tb=tb, pml=pml:
                                    s.copy(tb, pml[:, 0:TW]),
                                    deps=[mm[-1], prev_ev, zk])
                                prev_ev = s_e
                            else:
                                s_e0 = sc.step("scalar",
                                    lambda s, pml=pml:
                                    s.copy(vstg[:, 0:TW], pml[:, 0:TW]),
                                    deps=[mm[-1], prev_ev, prev_vstg])
                                prev_ev = s_e0
                                tr_prev = s_e0
                                for jb in range(TW // 128):
                                    jc0 = t5 * (TW // 128) + jb
                                    s_vt = sc.step("tensor",
                                        lambda te, jb=jb: te.transpose(
                                            pkp[:, 512:640],
                                            vstg[:, jb * 128:(jb + 1) * 128],
                                            self.identb[:]),
                                        deps=[s_e0, tr_prev, prev_ms,
                                              self.c_ld[0]])
                                    s_vc = sc.step("scalar",
                                        lambda s, jc0=jc0, oc=oc: s.copy(
                                            vT[:, jc0 * D + oc * 128:
                                               jc0 * D + (oc + 1) * 128],
                                            pkp[:, 512:640]),
                                        deps=[s_vt, zv])
                                    tr_prev = s_vc
                                    vt_done[(oc, jc0)] = s_vc
                                prev_vstg = tr_prev
                                s_e = tr_prev
                            qkv_done[nm][(oc, t5)] = s_e
                            prev_ev2[pbi] = prev_ev
                    prev_xn = qkv_done["v"][(DCH - 1, t5)]
                for oc in range(DCH):
                    vt_done[(oc, MEMC)] = kd_extra[-1]
                qd = list(qkv_done["q"].values())
                kd = list(qkv_done["k"].values()) + kd_extra

                prev_dt = {0: prev_b, 1: prev_b}
                prev_pk = prev_ms
                pmch = {0: prev_ev2[0], 1: prev_ev2[1]}
                uctr = [0]
                prev_av = {0: prev_b, 1: prev_b}
                prev_oev = prev_b
                prev_out = prev_b
                prev_packT = prev_b
                prev_xr = {}
                prev_pmx = prev_b
                prev_qt = [None, None]
                if KATT < 2:
                    self.attn_b_done = qd[-1]
                    continue
                BTW = 2
                xrow_free = getattr(self, "_xrow_free", {})
                patg_done = getattr(self, "_patg_done", {})
                prev_bstb = getattr(self, "_prev_bstb", {})
                for t in range(TT):
                    jl_t = cg.jlist(t * 16 + 15)
                    njc = len(jl_t)
                    qbi = t % 2
                    s_qt = sc.dma("gpsimd", lambda g, b=b, t=t, qbi=qbi:
                        g.dma_start(
                            out=qtl[:, qbi * DCH * 128:(qbi + 1) * DCH * 128]
                            .rearrange("p (c w) -> p c w", c=DCH),
                            in_=self.d_qfm[b, :, :, t * 128:(t + 1) * 128]
                            .rearrange("c p w -> p c w")),
                        deps=qd + [prev_qt[qbi]])
                    xr_done = {}
                    last_bt = None
                    nblk = (njc + BTW - 1) // BTW
                    for blk in range(nblk):
                        jcs = jl_t[blk * BTW:(blk + 1) * BTW]
                        nw = len(jcs)
                        bt_ev = []
                        for bi, jc in enumerate(jcs):
                            for hg in range(4):
                                mm4 = []
                                for hx in range(4):
                                    hh = hg * 4 + hx
                                    stat = kbuf[(hh % 2) * 64:
                                                (hh % 2) * 64 + 64,
                                                (hh // 2) * JP + jc * 128:
                                                (hh // 2) * JP + jc * 128
                                                + 128]
                                    mov = qtl[(hh % 2) * 64:(hh % 2) * 64
                                              + 64,
                                              qbi * DCH * 128 +
                                              (hh // 2) * 128:
                                              qbi * DCH * 128 +
                                              (hh // 2) * 128 + 128]
                                    mm4.append(sc.step("tensor",
                                        lambda te, stat=stat, mov=mov,
                                        hx=hx:
                                        te.matmul(dtp[:, hx * 128:
                                                      hx * 128 + 128],
                                                  stat, mov,
                                                  start=(hx == 0),
                                                  stop=(hx == 3)),
                                        deps=kd + [s_qt, prev_dt[0],
                                                   mm4[-1] if mm4 else
                                                   None]))
                                s_bt = sc.step("scalar",
                                    lambda s, bi=bi, hg=hg:
                                    s.copy(
                                        scr[:, bi * 2048:(bi + 1) * 2048]
                                        .rearrange("p (i k) -> p i k", k=16)
                                        [:, :, hg * 4:hg * 4 + 4],
                                        dtp[:, 0:512].rearrange(
                                            "p (h i) -> p i h", h=4)),
                                    deps=[mm4[-1], prev_xn,
                                          prev_packT,
                                          prev_qst[0], prev_qst[1]])
                                bt_ev.append(s_bt)
                                prev_dt[0] = s_bt
                        last_bt = bt_ev[-1]
                        for gl in range(16):
                            g = t * 16 + gl
                            gli0 = blk * BTW
                            pks = []
                            evs = []
                            for si, (bi, jc) in enumerate(zip(
                                    range(nw), jcs)):
                                s_pk = sc.step("tensor",
                                    lambda te, bi=bi, gl=gl: te.transpose(
                                        pkp[:, 0:128],
                                        scr[:, bi * 2048 + gl * 128:
                                            bi * 2048 + gl * 128 + 128],
                                        self.identb[:]),
                                    deps=bt_ev + [prev_pk, self.c_ld[0],
                                                  pks[-1] if pks else None])
                                pks.append(s_pk)
                                s_pe = sc.step("vector",
                                    lambda v, si=si: v.tensor_copy(
                                        pkb[:, si * 128:si * 128 + 128],
                                        pkp[:, 0:128]),
                                    deps=[s_pk, prev_pmx])
                                evs.append(s_pe)
                                prev_pk = s_pe
                            fills = []
                            for si, jc in enumerate(jcs):
                                if jc == t:
                                    fills.append((si,
                                        fpbig[:, 120 - 8 * gl:
                                              248 - 8 * gl], s_fbig))
                                elif jc == MEMC:
                                    fills.append((si, fpad[:], s_fpad))
                            tb = uctr[0] % 2
                            uctr[0] += 1
                            tps = pm1 if tb == 0 else pob
                            s_pm = sc.step("tensor", lambda te, nw=nw,
                                tps=tps, bd=bool(fills):
                                te.matmul(tps[:, 0:nw * 128], self.prepat[:],
                                          pkb[:, 0:nw * 128], start=True,
                                          stop=not bd),
                                deps=evs + [self.s_pre, pmch[tb]])
                            prev_pmx = s_pm
                            for fi, (si, fsrc, fdep) in enumerate(fills):
                                s_pm = sc.step("tensor",
                                    lambda te, si=si, fsrc=fsrc, tps=tps,
                                    lastf=(fi == len(fills) - 1):
                                    te.matmul(tps[:, si * 128:si * 128
                                                  + 128],
                                              self.e8c[:], fsrc,
                                              start=False, stop=lastf),
                                    deps=[s_pm, fdep, self.c_ld[2]])
                            s_xr = sc.step("scalar",
                                lambda s, gl=gl, gli0=gli0, nw=nw, tps=tps:
                                s.copy(xrow[:, gl * JP + gli0 * 128:
                                            gl * JP + (gli0 + nw) * 128],
                                       tps[:, 0:nw * 128]),
                                deps=[s_pm, prev_xr.get(gl),
                                      xrow_free.get(gl)])
                            pmch[tb] = s_xr
                            prev_xr[gl] = s_xr
                            xr_done.setdefault(gl, []).append(s_xr)
                            prev_packT = pks[-1]
                    prev_qt[qbi] = last_bt
                    if KATT < 3:
                        continue
                    def bcols(j, g0=0, g1=16):
                        return bst[:].rearrange("p (g o) -> p g o",
                                                o=8)[:, g0:g1, j:j + 1]
                    s_i0 = sc.step("vector", lambda v: v.memset(
                        bcols(0), BIS_LO),
                        deps=[patg_done.get(gl) for gl in range(16)] +
                             [prev_packT])
                    prev_state = [s_i0]
                    prev_cnt2 = getattr(self, "_prev_cnt2", [None, None])
                    cnt_last = {}
                    wjt = njc * 128
                    for it in range(NBIS):
                        c_it = float(16.0 / (1 << it))
                        s_mid = sc.step("vector", lambda v, c_it=c_it:
                            v.tensor_single_scalar(bcols(2), bcols(0), c_it,
                                                   op=ALU.add),
                            deps=prev_state)
                        s_nm = sc.step("vector", lambda v:
                            v.tensor_single_scalar(
                                bcols(5, 8, 16), bcols(2, 8, 16), -1.0,
                                op=ALU.mult), deps=[s_mid])
                        cv = []
                        cs = []
                        for gl in range(16):
                            if gl < 8:
                                sv = csb[:, 0:wjt]
                                s_c = sc.step("vector",
                                    lambda v, gl=gl, sv=sv, wjt=wjt:
                                    v.tensor_scalar(
                                        sv, xrow[:, gl * JP:gl * JP + wjt],
                                        bcol(gl, 2), None, op0=ALU.is_ge,
                                        op1=ALU.add, accum_out=bcol(gl, 3)),
                                    deps=[s_mid, prev_cnt2[0], last_bt] +
                                         xr_done.get(gl, []))
                                prev_cnt2[0] = s_c
                                cv.append(s_c)
                            else:
                                sv = csb[:, JP:JP + wjt]
                                s_c = sc.step("scalar",
                                    lambda s, gl=gl, sv=sv, wjt=wjt:
                                    s.activation(
                                        sv, xrow[:, gl * JP:gl * JP + wjt],
                                        AF.Sign, bias=bcol(gl, 5), scale=1.0,
                                        accum_out=bcol(gl, 3)),
                                    deps=[s_nm, prev_cnt2[1], last_bt] +
                                         xr_done.get(gl, []))
                                prev_cnt2[1] = s_c
                                cs.append(s_c)
                            cnt_last[gl] = s_c
                        s_pv = sc.step("vector", lambda v:
                            v.tensor_single_scalar(
                                bcols(4, 0, 8), bcols(3, 0, 8), TOPK,
                                op=ALU.is_ge), deps=cv)
                        s_ps = sc.step("vector", lambda v, t=t:
                            v.tensor_single_scalar(
                                bcols(4, 8, 16), bcols(3, 8, 16),
                                wjoff[:, t:t + 1], op=ALU.is_ge),
                            deps=cs)
                        s_lo = sc.step("vector", lambda v, c_it=c_it:
                            v.scalar_tensor_tensor(
                                bcols(0), bcols(4), c_it, bcols(0),
                                op0=ALU.mult, op1=ALU.add),
                            deps=[s_pv, s_ps])
                        prev_state = [s_lo]
                    self._prev_cnt2 = prev_cnt2
                    bis_fin = prev_state[0]
                    if KATT < 4:
                        continue
                    sm_done = {}
                    for gl in range(16):
                        xr = xrow[:, gl * JP:gl * JP + wjt]
                        s_pexp = sc.step("scalar", lambda s, xr=xr:
                            s.activation(xr, xr, AF.Exp,
                                         bias=self.cexp[:, 0:1],
                                         scale=1.0),
                            deps=[bis_fin, cnt_last[gl]])
                        s_lb = sc.step("vector", lambda v, gl=gl:
                            v.tensor_copy(bstb[:, gl * 2:gl * 2 + 1],
                                          bcol(gl, 0)),
                            deps=[bis_fin, prev_bstb.get(gl)])
                        s_pt0 = sc.step("scalar", lambda s, gl=gl:
                            s.activation(
                                bstb[:, gl * 2 + 1:gl * 2 + 2],
                                bstb[:, gl * 2:gl * 2 + 1], AF.Exp,
                                bias=self.cexp[:, 0:1], scale=1.0),
                            deps=[s_lb, s_pexp])
                        s_pt = sc.step("vector", lambda v, gl=gl:
                            v.tensor_copy(bcol(gl, 6),
                                          bstb[:, gl * 2 + 1:gl * 2 + 2]),
                            deps=[s_pt0])
                        prev_bstb[gl] = s_pt
                        s_z = sc.step("vector", lambda v, xr=xr, gl=gl:
                            v.scalar_tensor_tensor(
                                xr, xr, bcol(gl, 6), xr, op0=ALU.is_gt,
                                op1=ALU.mult, accum_out=bcol(gl, 3)),
                            deps=[s_pexp, s_pt])
                        s_ln = sc.step("scalar", lambda s, gl=gl: s.activation(
                            bcol(gl, 7), bcol(gl, 3), AF.Ln, bias=0.0,
                            scale=1.0), deps=[s_z])
                        s_iv = sc.step("scalar", lambda s, gl=gl: s.activation(
                            bcol(gl, 7), bcol(gl, 7), AF.Exp, bias=0.0,
                            scale=-1.0), deps=[s_ln])
                        sm_done[gl] = (s_z, s_iv)
                    self._prev_bstb = prev_bstb
                    if KATT < 5:
                        continue
                    for gl in range(16):
                        patg_done[gl] = sc.step("vector", lambda v, gl=gl:
                            v.tensor_single_scalar(
                                patgs[:, gl * 128:(gl + 1) * 128],
                                self.postpat[:], bcol(gl, 7), op=ALU.mult),
                            deps=[sm_done[gl][1], self.s_post,
                                  patg_done.get(gl), xrow_free.get(gl)])
                    av_mm = {}
                    for li, jc in enumerate(jl_t):
                        abi = li % 2
                        at_done = []
                        for gq in range(4):
                            tb = gq % 2
                            tps = pm1 if tb == 0 else pob
                            mxs = []
                            for si in range(4):
                                gl = gq * 4 + si
                                mxs.append(sc.step("tensor",
                                    lambda te, gl=gl, li=li, si=si, tps=tps:
                                    te.matmul(
                                        tps[:, si * 128:si * 128 + 128],
                                        xrow[:, gl * JP + li * 128:
                                             gl * JP + li * 128 + 128],
                                        patgs[:, gl * 128:(gl + 1) * 128],
                                        start=(si == 0), stop=(si == 3)),
                                    deps=[patg_done[gl], sm_done[gl][0],
                                          pmch[tb],
                                          mxs[-1] if mxs else None]))
                            at_ev = sc.step("vector",
                                lambda v, gq=gq, abi=abi, tps=tps:
                                v.tensor_copy(
                                    atb[:, abi * 2048:(abi + 1) * 2048]
                                    .rearrange("p (k e i) -> p k e i",
                                               e=16, i=8)
                                    [:, :, gq * 4:gq * 4 + 4, :],
                                    tps[:, 0:512].rearrange(
                                        "p (s i k) -> p k s i", s=4, i=8)),
                                deps=[mxs[-1], prev_av[abi]])
                            at_done.append(at_ev)
                            pmch[tb] = at_ev
                            if li == njc - 1:
                                for si in range(4):
                                    xrow_free[gq * 4 + si] = at_ev
                        avl = []
                        for k2 in range(H):
                            st_fl = (li == 0) and (k2 % 4 == 0)
                            sp_fl = (li == njc - 1) and (k2 % 4 == 3)
                            s_av = sc.step("tensor",
                                lambda te, k2=k2, jc=jc, abi=abi,
                                st_fl=st_fl, sp_fl=sp_fl:
                                te.matmul(
                                    ovp[0:64, k2 * 128:k2 * 128 + 128],
                                    vT[:, jc * D + k2 * 64:
                                       jc * D + k2 * 64 + 64],
                                    atb[:, abi * 2048 + k2 * 128:
                                        abi * 2048 + k2 * 128 + 128],
                                    start=st_fl, stop=sp_fl),
                                deps=at_done + [vt_done[(k2 // 2, jc)],
                                                av_mm.get((k2, li - 1)),
                                                av_mm.get((k2 - 1, li)),
                                                prev_oev if li == 0 else
                                                None])
                            av_mm[(k2, li)] = s_av
                            avl.append(s_av)
                        prev_av[abi] = avl[-1]
                    self._xrow_free = xrow_free
                    self._patg_done = patg_done
                    self._prev_bstb = prev_bstb
                    if KATT < 6:
                        continue
                    oev = []
                    av_last = [av_mm[(kk, njc - 1)] for kk in range(H)]
                    for k2 in range(H):
                        s_oe = sc.step("scalar", lambda s, k2=k2:
                            s.copy(obuf[(k2 % 2) * 64:(k2 % 2) * 64 + 64,
                                        (k2 // 2) * 128:(k2 // 2) * 128 + 128],
                                   ovp[0:64, k2 * 128:k2 * 128 + 128]),
                            deps=av_last + [prev_out,
                                            oev[-1] if oev else None])
                        oev.append(s_oe)
                    prev_oev = oev[-1]
                    s_lx = sc.dma("sync", lambda sy, b=b, t=t: sy.dma_start(
                        out=xtl[:], in_=self.d_x[b, t * 128:(t + 1) * 128, :]),
                        deps=[prev_out])
                    s_lg = sc.dma("gpsimd", lambda g, b=b, t=t: g.dma_start(
                        out=gtl[:], in_=self.d_g[b, t]),
                        deps=[self.g_done[(b, t)], prev_out])
                    cmb = []
                    for ns in range(D // 512):
                        mml = []
                        for c in range(DCH):
                            ws = wob[:, c * D + ns * 512:c * D + ns * 512 + 512]
                            mml.append(sc.step("tensor",
                                lambda te, c=c, ws=ws: te.matmul(
                                    pob[:, 0:512],
                                    obuf[:, c * 128:(c + 1) * 128], ws,
                                    start=(c == 0), stop=False),
                                deps=[s_lwo] + oev +
                                     [mml[-1] if mml else
                                      (cmb[-1] if cmb else pmch[1])]))
                        s_bomm = sc.step("tensor", lambda te, ns=ns:
                            te.matmul(pob[:, 0:512], self.ones8b[0:1, :],
                                      self.bo_rowb[0:1, ns * 512:
                                                    (ns + 1) * 512],
                                      start=False, stop=True),
                            deps=[mml[-1], self.s_bo_ld, self.c_ld[6]])
                        s_c1 = sc.step("vector", lambda v, ns=ns:
                            v.tensor_copy(otl[:, ns * 512:(ns + 1) * 512],
                                          pob[:, 0:512]),
                            deps=[s_bomm, prev_out])
                        s_c2 = sc.step("vector", lambda v, ns=ns:
                            v.tensor_tensor(otl[:, ns * 512:(ns + 1) * 512],
                                            otl[:, ns * 512:(ns + 1) * 512],
                                            gtl[:, ns * 512:(ns + 1) * 512],
                                            op=ALU.mult),
                            deps=[s_c1, s_lg])
                        s_c3 = sc.step("vector", lambda v, ns=ns:
                            v.tensor_tensor(otl[:, ns * 512:(ns + 1) * 512],
                                            otl[:, ns * 512:(ns + 1) * 512],
                                            xtl[:, ns * 512:(ns + 1) * 512],
                                            op=ALU.add),
                            deps=[s_c2, s_lx])
                        cmb.append(s_c1)
                        cmb.append(s_c3)
                    pmch[1] = cmb[-2]
                    prev_out = sc.dma("sync", lambda sy, b=b, t=t:
                        sy.dma_start(
                            out=self.d_out[b, t * 128:(t + 1) * 128, :],
                            in_=otl[:]), deps=[cmb[-1], cmb[-2]])
                self.attn_b_done = prev_out if KATT >= 6 else \
                    (self.sc.last.get("vector"))


def relayout_inputs(cfg, inputs):
    cg = cfg
    x = np.ascontiguousarray(inputs["L0_attn_x"])
    ins = {
        "wq": np.ascontiguousarray(inputs["L0_attn_Wq"]),
        "wk": np.ascontiguousarray(inputs["L0_attn_Wk"]),
        "wv": np.ascontiguousarray(inputs["L0_attn_Wv"]),
        "wo": np.ascontiguousarray(inputs["L0_attn_Wo"]),
        "pre": np.ascontiguousarray(inputs["L0_attn_pre_proj"]),
        "post": np.ascontiguousarray(inputs["L0_attn_post_proj"]),
        "memk": np.ascontiguousarray(
            inputs["L0_attn_mem_k"].transpose(0, 2, 1)),
        "memv": np.ascontiguousarray(
            inputs["L0_attn_mem_v"].transpose(0, 2, 1)),
        "bo": np.ascontiguousarray(inputs["L0_attn_bo"]).reshape(1, -1),
        "gatewT": np.ascontiguousarray(inputs["L0_ffn_gate_w"].T),
        "wgT": np.ascontiguousarray(inputs["L0_ffn_Wg"].transpose(0, 2, 1)),
        "wuT": np.ascontiguousarray(inputs["L0_ffn_Wu"].transpose(0, 2, 1)),
        "wdT": np.ascontiguousarray(inputs["L0_ffn_Wd"].transpose(0, 2, 1)),
        "wsgT": np.ascontiguousarray(inputs["L0_ffn_Ws_g"].T),
        "wsuT": np.ascontiguousarray(inputs["L0_ffn_Ws_u"].T),
        "wsdT": np.ascontiguousarray(inputs["L0_ffn_Ws_d"].T),
    }
    ins.update(host_constants(cfg))
    nb = x.shape[0]
    per_core = nb // cfg.b
    maps = []
    for c in range(per_core):
        m = dict(ins)
        m["x"] = np.ascontiguousarray(x[c * cfg.b:(c + 1) * cfg.b])
        maps.append(m)
    return maps


def kernel(**inputs):
    cfg = Cfg()
    maps = relayout_inputs(cfg, {k: np.asarray(v) for k, v in inputs.items()})
    b = Build(cfg)
    nc = b.build()
    res = run_bass_kernel_spmd(nc, maps, core_ids=list(range(len(maps))))
    outs = [r["out"] for r in res.results]
    return np.concatenate(outs, axis=0)


# version 17
# speedup vs baseline: 1.1071x; 1.1071x over previous
from contextlib import ExitStack
from dataclasses import dataclass, field

import numpy as np
import ml_dtypes

import concourse.bass as bass
import concourse.mybir as mybir
from concourse.bass_utils import run_bass_kernel_spmd

F32 = mybir.dt.float32
BF16 = mybir.dt.bfloat16
AF = mybir.ActivationFunctionType
ALU = mybir.AluOpType
AX = mybir.AxisListType
BFNP = ml_dtypes.bfloat16

EXP_BIAS = -48.0
BIS_LO, BIS_HI = -16.0, 16.0
N_CORES = 8
ENGINES = ("vector", "scalar", "tensor", "gpsimd", "sync")


@dataclass
class Cfg:
    b: int = 2
    seq: int = 2048
    d: int = 1024
    h: int = 16
    dh: int = 64
    mem: int = 16
    topk: int = 64
    e: int = 16
    f: int = 2048
    cap: int = 256
    fsh: int = 2048
    bis_sched: tuple = ("vector",) * 12

    @property
    def tt(self): return self.seq // 128
    @property
    def dch(self): return self.d // 128
    @property
    def jp(self): return ((self.mem + self.seq + 127) // 128) * 128
    @property
    def jcn(self): return self.jp // 128
    @property
    def fch(self): return self.f // 128
    @property
    def fshch(self): return self.fsh // 128
    @property
    def capch(self): return max(1, self.cap // 128)
    @property
    def mr_rounds(self): return self.cap // 8

    def cw(self, g):
        return min((8 * g + 7) // 128 + 1, self.seq // 128)

    def cw_t(self, t):
        return self.cw(t * 16 + 15)

    @property
    def memc(self):
        return self.seq // 128

    def jlist(self, g):
        return list(range(self.cw(g))) + [self.memc]


class Sched:
    NRING = 8
    ERING = 4

    def __init__(self, nc, ctx):
        self.nc = nc
        self.steps = []
        self.last = {}
        self.bar = {}
        self.sems = {}
        for e in ENGINES:
            for i in range(self.ERING):
                self.sems[f"{e}{i}"] = ctx.enter_context(
                    nc.semaphore(f"sem_{e}{i}"))
        for q in ("hdma", "gdma"):
            for i in range(self.NRING):
                self.sems[f"{q}{i}"] = ctx.enter_context(
                    nc.semaphore(f"sem_{q}{i}"))
        self.dma_idx = {"hdma": 0, "gdma": 0}
        self.eng_idx = dict.fromkeys(ENGINES, 0)
        self.ring_prev = {}

    def step(self, eng, fn, deps=()):
        sid = len(self.steps)
        slot = self.eng_idx[eng] % self.ERING
        self.eng_idx[eng] += 1
        dl = [d for d in deps if d is not None]
        if self.bar.get(eng) is not None:
            dl.append(self.bar[eng])
        self.steps.append((eng, fn, dl, False, None, slot))
        self.last[eng] = sid
        return sid

    def dma(self, eng, fn, deps=()):
        sid = len(self.steps)
        q = "gdma" if eng == "gpsimd" else "hdma"
        slot = self.dma_idx[q] % self.NRING
        self.dma_idx[q] += 1
        dl = [d for d in deps if d is not None]
        prev = self.ring_prev.get((q, slot))
        if prev is not None:
            dl.append(prev)
        if self.bar.get(eng) is not None:
            dl.append(self.bar[eng])
        self.steps.append((eng, fn, dl, True, q, slot))
        self.last[eng] = sid
        self.last[f"{q}{slot}"] = sid
        self.ring_prev[(q, slot)] = sid
        return sid

    def barrier(self):
        snap = dict(self.last)
        newbar = {}
        for e in ENGINES:
            deps = [v for k, v in snap.items() if v is not None]
            slot = self.eng_idx[e] % self.ERING
            self.eng_idx[e] += 1
            sid = len(self.steps)
            self.steps.append((e, lambda x: x.nop(), deps, False, None, slot))
            self.last[e] = sid
            newbar[e] = sid
        self.bar = newbar

    def emit(self):
        done = {}
        cnt = {}
        for i, (eng, fn, deps, dma, q, slot) in enumerate(self.steps):
            if dma:
                key = f"{q}{slot}"
                cnt[key] = cnt.get(key, 0) + 16
                done[i] = (key, cnt[key])
            else:
                key = f"{eng}{slot}"
                cnt[key] = cnt.get(key, 0) + 1
                done[i] = (key, cnt[key])
        per_engine = {e: [] for e in ENGINES}
        for i, st in enumerate(self.steps):
            per_engine[st[0]].append((i, st))

        def section(ename):
            def body(e):
                waited = {}
                for i, (eng, fn, deps, dma, q, slot) in per_engine[ename]:
                    for d in deps:
                        s, v = done[d]
                        if waited.get(s, -1) >= v:
                            continue
                        e.wait_ge(self.sems[s], v)
                        waited[s] = v
                    instr = fn(e)
                    if instr is None:
                        instr = e.nop()
                    s, v = done[i]
                    instr.then_inc(self.sems[s], 16 if dma else 1)
            return body

        finals = dict(cnt)

        def sync_section(e):
            section("sync")(e)
            for s, v in finals.items():
                e.wait_ge(self.sems[s], v)

        with self.nc.Block() as blk:
            blk.vector(section("vector"))
            blk.scalar(section("scalar"))
            blk.tensor(section("tensor"))
            blk.gpsimd(section("gpsimd"))
            blk.sync(sync_section)


def host_constants(cfg):
    c = {}
    c["identb"] = np.eye(128, dtype=BFNP)
    c["identf"] = np.eye(128, dtype=np.float32)
    e8 = np.zeros((8, 128), np.float32)
    for r in range(8):
        e8[r, r * 16:(r + 1) * 16] = 1.0
    c["e8c"] = e8.astype(BFNP)
    c["iota_cap"] = np.tile(np.arange(1, cfg.cap + 1, dtype=np.float32),
                            (128, 1)).astype(BFNP)
    c["onesbf"] = np.ones((1, 128), BFNP)
    c["onesff"] = np.ones((16, 128), np.float32)
    c["ones8b"] = np.ones((8, 128), BFNP)
    return c


class Build:

    def __init__(self, cfg, debug=()):
        self.cfg = cfg
        self.debug = set(debug)
        self.nc = bass.Bass("TRN2", target_bir_lowering=False)
        self.dbg = {}

    def din(self, name, shape, dtype=F32):
        return self.nc.dram_tensor(name, list(shape), dtype,
                                   kind="ExternalInput").ap()

    def dout(self, name, shape, dtype=F32):
        return self.nc.dram_tensor(name, list(shape), dtype,
                                   kind="ExternalOutput").ap()

    def dtmp(self, name, shape, dtype=F32):
        return self.nc.dram_tensor(name, list(shape), dtype,
                                   kind="Internal").ap()

    def ddbg(self, name, shape, dtype=F32):
        t = self.dout("dbg_" + name, shape, dtype)
        self.dbg[name] = t
        return t

    def build(self):
        cg = self.cfg
        nc = self.nc
        B, SEQ, D, H, DH, MEM = cg.b, cg.seq, cg.d, cg.h, cg.dh, cg.mem
        E, F, CAP, FSH = cg.e, cg.f, cg.cap, cg.fsh
        TT, DCH, JP, JCN = cg.tt, cg.dch, cg.jp, cg.jcn
        FCH, FSHCH, CAPCH = cg.fch, cg.fshch, cg.capch
        SCH = SEQ // 128
        self.d_x = self.din("x", [B, SEQ, D])
        self.d_wq = self.din("wq", [D, D])
        self.d_wk = self.din("wk", [D, D])
        self.d_wv = self.din("wv", [D, D])
        self.d_wo = self.din("wo", [D, D])
        self.d_pre = self.din("pre", [H, H])
        self.d_post = self.din("post", [H, H])
        self.d_memk = self.din("memk", [H, DH, MEM])
        self.d_memv = self.din("memv", [H, DH, MEM])
        self.d_bo = self.din("bo", [1, D])
        self.d_gatewT = self.din("gatewT", [D, E])
        self.d_wgT = self.din("wgT", [E, D, F])
        self.d_wuT = self.din("wuT", [E, D, F])
        self.d_wdT = self.din("wdT", [E, F, D])
        self.d_wsgT = self.din("wsgT", [D, FSH])
        self.d_wsuT = self.din("wsuT", [D, FSH])
        self.d_wsdT = self.din("wsdT", [FSH, D])
        self.d_identb = self.din("identb", [128, 128], BF16)
        self.d_identf = self.din("identf", [128, 128], F32)
        self.d_e8c = self.din("e8c", [8, 128], BF16)
        self.d_iotacap = self.din("iota_cap", [128, CAP], BF16)
        self.d_onesbf = self.din("onesbf", [1, 128], BF16)
        self.d_onesff = self.din("onesff", [16, 128], F32)
        self.d_ones8b = self.din("ones8b", [8, 128], BF16)

        self.d_xnfm = self.dtmp("xnfm", [B, DCH, 128, SEQ], BF16)
        self.d_st = self.dtmp("stm", [B, E, SCH, 128, CAP], BF16)
        self.d_eo = self.dtmp("eo", [B, E, CAPCH * 128, D], BF16)
        self.d_g = self.dtmp("gbuf", [B, TT, 128, D], BF16)
        self.d_fsc = self.dtmp("fsc", [B, TT, 128, D], F32)
        self.d_qfm = self.dtmp("qfm", [B, DCH, 128, SEQ], BF16)
        self.d_out = self.dout("out", [B, SEQ, D])

        with ExitStack() as ctx:
            self.ctx = ctx
            self.sc = Sched(nc, ctx)
            import os
            stop = int(os.environ.get("KPHASES", "99"))
            self.alloc_persistent()
            self.ph_consts()
            self.sc.barrier()
            self._csc.close()
            with ExitStack() as xnsc:
                self.xntm = self.sb("xn_tm",
                                    [128, self.cfg.b * self.cfg.tt *
                                     self.cfg.d], BF16, xnsc)
                if stop >= 1:
                    self.ph_layernorm()
                    self.sc.barrier()
                if stop >= 2:
                    self.ph_routing()
                    self.sc.barrier()
                if stop >= 3:
                    self.ph_experts()
                    self.sc.barrier()
            if stop >= 4:
                self.ph_scatter()
                self.sc.barrier()
            if stop >= 5:
                self.ph_shared()
                self.sc.barrier()
            if stop >= 6:
                self.ph_attention()
            self.sc.emit()
        return nc

    def sb(self, name, shape, dtype=F32, scope=None):
        return (scope or self.ctx).enter_context(
            self.nc.sbuf_tensor("sb_" + name, list(shape), dtype))

    def ps(self, name, shape, dtype=F32, scope=None):
        return (scope or self.ctx).enter_context(
            self.nc.psum_tensor("ps_" + name, list(shape), dtype))

    def alloc_persistent(self):
        cg = self.cfg
        self.identb = self.sb("identb", [128, 128], BF16)
        self.e8c = self.sb("e8c", [8, 128], BF16)
        self.onesbf = self.sb("onesbf", [1, 128], BF16)
        self.onesff = self.sb("onesff", [16, 128], F32)
        self.ones8b = self.sb("ones8b", [8, 128], BF16)
        self.prepat = self.sb("prepat", [128, 128], BF16)
        self.cexp = self.sb("cexp", [128, 1], F32)
        self.postpat = self.sb("postpat", [128, 128], BF16)
        self.bo_rowb = self.sb("bo_rowb", [1, cg.d], BF16)
        self.wcol = self.sb("wcol", [128, cg.b * cg.e * cg.capch], F32)

    def xnt(self, b, t):
        o = (b * self.cfg.tt + t) * self.cfg.d
        return self.xntm[:, o:o + self.cfg.d]

    def ph_consts(self):
        sc, cg = self.sc, self.cfg
        csc = ExitStack()
        self._csc = csc
        ppf = self.sb("ppf", [128, 256], F32, csc)
        ld = []
        for dst, src in [(self.identb, self.d_identb),
                         (self.e8c, self.d_e8c),
                         (self.onesbf, self.d_onesbf),
                         (self.onesff, self.d_onesff),
                         (self.ones8b, self.d_ones8b)]:
            ld.append(sc.dma("sync", lambda sy, dst=dst, src=src:
                             sy.dma_start(out=dst[:], in_=src)))
        self.c_ld = [ld[0], None, ld[1], None, ld[2], ld[3], ld[4]]
        ld = self.c_ld
        self.s_bo_ld = sc.dma("gpsimd", lambda g: g.dma_start(
            out=self.bo_rowb[:], in_=self.d_bo))
        z0 = sc.step("vector", lambda v: v.memset(ppf[:], 0.0))
        self.s_cexp = sc.step("vector", lambda v: v.memset(
            self.cexp[:], EXP_BIAS))
        pp = []
        for i8 in range(8):
            pp.append(sc.dma("sync", lambda sy, i8=i8: sy.dma_start(
                out=ppf[i8 * 16:(i8 + 1) * 16, i8 * 16:(i8 + 1) * 16],
                in_=self.d_pre), deps=[z0]))
            pp.append(sc.dma("sync", lambda sy, i8=i8: sy.dma_start(
                out=ppf[i8 * 16:(i8 + 1) * 16,
                        128 + i8 * 16:128 + (i8 + 1) * 16],
                in_=self.d_post), deps=[z0]))
        self.s_pre = sc.step("vector", lambda v: v.tensor_copy(
            self.prepat[:], ppf[:, 0:128]), deps=pp)
        self.s_post = sc.step("vector", lambda v: v.tensor_copy(
            self.postpat[:], ppf[:, 128:256]), deps=pp)

    def ph_layernorm(self):
        sc, cg = self.sc, self.cfg
        B, TT, D, DCH = cg.b, cg.tt, cg.d, cg.dch
        with ExitStack() as lsc:
            ln_x = self.sb("ln_x", [128, 2 * D], F32, lsc)
            ln_sq = self.sb("ln_sq", [128, D], F32, lsc)
            ln_st = self.sb("ln_st", [128, 16], F32, lsc)
            ln_stg = self.sb("ln_stg", [128, 2 * 128], BF16, lsc)
            ln_ps = self.ps("ln_ps", [128, 2048], BF16, lsc)
            last_xn = [None, None]
            last_tr = [None, None]
            self.ln_fm = {}
            if "ln" in self.debug:
                d_dbgxn = self.ddbg("xn", [B, cg.seq, D])
            prev_sq = None
            for i in range(B * TT):
                b, t = divmod(i, TT)
                bi = i % 2
                xt = ln_x[:, bi * D:(bi + 1) * D]
                st = ln_st[:, bi * 8:bi * 8 + 8]
                s_ld = sc.dma("sync", lambda sy, b=b, t=t, xt=xt: sy.dma_start(
                    out=xt, in_=self.d_x[b, t * 128:(t + 1) * 128, :]),
                    deps=[last_xn[bi]])
                s_sum = sc.step("vector", lambda v, xt=xt, st=st: v.reduce_sum(
                    st[:, 0:1], xt, axis=AX.X), deps=[s_ld, last_xn[bi]])
                s_sq = sc.step("scalar", lambda s, xt=xt, st=st: s.activation(
                    ln_sq[:], xt, AF.Square, bias=0.0, scale=1.0,
                    accum_out=st[:, 1:2]), deps=[s_ld, prev_sq, last_xn[bi]])
                prev_sq = s_sq
                s_m = sc.step("vector", lambda v, st=st: v.tensor_single_scalar(
                    st[:, 2:3], st[:, 0:1], 1.0 / D, op=ALU.mult), deps=[s_sum])
                s_m2 = sc.step("vector", lambda v, st=st: v.tensor_single_scalar(
                    st[:, 3:4], st[:, 1:2], 1.0 / D, op=ALU.mult), deps=[s_sq])
                s_msq = sc.step("vector", lambda v, st=st: v.tensor_tensor(
                    st[:, 4:5], st[:, 2:3], st[:, 2:3], op=ALU.mult),
                    deps=[s_m])
                s_var = sc.step("vector", lambda v, st=st: v.scalar_tensor_tensor(
                    st[:, 5:6], st[:, 3:4], 1e-5, st[:, 4:5],
                    op0=ALU.add, op1=ALU.subtract), deps=[s_m2, s_msq])
                s_sd = sc.step("scalar", lambda s, st=st: s.activation(
                    st[:, 6:7], st[:, 5:6], AF.Sqrt, bias=0.0, scale=1.0),
                    deps=[s_var])
                s_r = sc.step("vector", lambda v, st=st: v.reciprocal(
                    st[:, 7:8], st[:, 6:7]), deps=[s_sd])
                s_xn = sc.step("vector", lambda v, xt=xt, st=st, b=b, t=t:
                               v.tensor_scalar(
                                   self.xnt(b, t), xt, st[:, 2:3], st[:, 7:8],
                                   op0=ALU.subtract, op1=ALU.mult),
                               deps=[s_r, s_ld])
                last_xn[bi] = s_xn
                if "ln" in self.debug:
                    sc.dma("sync", lambda sy, b=b, t=t: sy.dma_start(
                        out=d_dbgxn[b, t * 128:(t + 1) * 128, :],
                        in_=self.xnt(b, t)), deps=[s_xn])
                for dc in range(DCH):
                    pbi = (i * DCH + dc) % 2
                    s_tr = sc.step("tensor", lambda te, b=b, t=t, dc=dc, pbi=pbi:
                                   te.transpose(
                                       ln_ps[:, pbi * 1024:pbi * 1024 + 128],
                                       self.xnt(b, t)[:, dc * 128:(dc + 1) * 128],
                                       self.identb[:]),
                                   deps=[s_xn, self.c_ld[0], last_tr[pbi]])
                    s_ev = sc.step("scalar", lambda s, pbi=pbi: s.copy(
                        ln_stg[:, pbi * 128:(pbi + 1) * 128],
                        ln_ps[:, pbi * 1024:pbi * 1024 + 128]), deps=[s_tr])
                    s_stm = sc.dma("sync", lambda sy, b=b, t=t, dc=dc, pbi=pbi:
                                   sy.dma_start(
                                       out=self.d_xnfm[b, dc, :,
                                                       t * 128:(t + 1) * 128],
                                       in_=ln_stg[:, pbi * 128:(pbi + 1) * 128]),
                                   deps=[s_ev])
                    last_tr[pbi] = s_stm
                    self.ln_fm[(b, dc, t)] = s_stm
            self.ln_done = [s for s in last_xn + last_tr if s is not None]

    def ph_routing(self):
        sc, cg = self.sc, self.cfg
        B, SEQ, D, E, CAP = cg.b, cg.seq, cg.d, cg.e, cg.cap
        DCH, CAPCH = cg.dch, cg.capch
        SCH = SEQ // 128
        with ExitStack() as rsc:
            gtw = self.sb("gtw", [128, DCH * E], BF16, rsc)
            fmb = self.sb("fmb", [128, 2 * 512], BF16, rsc)
            sco_b = [self.sb(f"sco{b}", [16, SEQ], F32, rsc) for b in range(B)]
            wrk = self.sb("wrk", [16, SEQ], F32, rsc)
            zro16 = self.sb("zro16", [16, SEQ], F32, rsc)
            m8 = self.sb("m8", [16, 8], F32, rsc)
            msk = self.sb("msk", [16, SEQ], F32, rsc)
            cum16 = self.sb("cum16", [16, SEQ], F32, rsc)
            qT = self.sb("qT", [128, B * SCH * 16], F32, rsc)
            scTb = self.sb("scTb", [128, B * SCH * 16], BF16, rsc)
            stb = self.sb("stb", [128, SCH * CAP], BF16, rsc)
            wrow = self.sb("wrow", [1, CAP], BF16, rsc)
            elb = self.sb("elb", [16, SEQ], F32, rsc)
            rcp = self.sb("rcp", [1, 512], F32, rsc)
            identf = self.sb("identf", [16, 16], F32, rsc)
            iotacap = self.sb("iotacap", [128, CAP], BF16, rsc)
            l_idf = sc.dma("sync", lambda sy: sy.dma_start(
                out=identf[:], in_=self.d_identf[0:16, 0:16]))
            l_iot = sc.dma("sync", lambda sy: sy.dma_start(
                out=iotacap[:], in_=self.d_iotacap))
            lps = self.ps("lps", [16, 512], F32, rsc)
            l2ps = self.ps("l2ps", [1, 512], F32, rsc)
            l3ps = self.ps("l3ps", [16, 512], F32, rsc)
            trps = self.ps("trps", [128, 32], F32, rsc)
            wrps = self.ps("wrps", [1, CAP], F32, rsc)
            wcps = self.ps("wcps", [128, CAPCH], F32, rsc)

            s_gtw = sc.dma("gpsimd", lambda g: g.dma_start(
                out=gtw[:].rearrange("p (c e) -> p c e", e=E),
                in_=self.d_gatewT.rearrange("(c p) e -> p c e", p=128)))
            z16 = sc.step("vector", lambda v: v.memset(zro16[:], 0.0))
            W5 = min(512, SEQ)
            NW = SEQ // W5
            rt_q = {}
            prev_weldc = None
            for b in range(B):
                sc_done = []
                prev_s = None
                for w5 in range(NW):
                    t0 = w5 * W5
                    mms = []
                    for dc in range(DCH):
                        fbi = dc % 2
                        s_l = sc.dma("gpsimd",
                            lambda g, b=b, dc=dc, t0=t0, fbi=fbi: g.dma_start(
                                out=fmb[:, fbi * 512:fbi * 512 + W5],
                                in_=self.d_xnfm[b, dc, :, t0:t0 + W5]),
                            deps=[self.ln_fm[(b, dc, t0 // 128)],
                                  self.ln_fm[(b, dc, min(t0 // 128 + 3,
                                                         cg.tt - 1))],
                                  mms[dc - 2] if dc >= 2 else prev_weldc])
                        mms.append(sc.step("tensor",
                            lambda te, dc=dc, fbi=fbi: te.matmul(
                                lps[:, 0:W5], gtw[:, dc * E:(dc + 1) * E],
                                fmb[:, fbi * 512:fbi * 512 + W5],
                                start=(dc == 0), stop=(dc == DCH - 1)),
                            deps=[s_l, s_gtw, mms[-1] if mms else prev_s]))
                    s_el = sc.step("scalar", lambda s, t0=t0: s.activation(
                        elb[:, t0:t0 + W5], lps[:, 0:W5], AF.Exp,
                        bias=0.0, scale=1.0), deps=[mms[-1]])
                    prev_weldc = s_el
                    s_sm = sc.step("tensor", lambda te, t0=t0: te.matmul(
                        l2ps[:, 0:W5], self.onesff[:, 0:1],
                        elb[:, t0:t0 + W5],
                        start=True, stop=True), deps=[s_el, self.c_ld[5]])
                    s_rc = sc.step("vector", lambda v: v.reciprocal(
                        rcp[:, 0:W5], l2ps[0:1, 0:W5]), deps=[s_sm])
                    s_bc = sc.step("tensor", lambda te: te.matmul(
                        l3ps[:, 0:W5], self.onesff[0:1, 0:16], rcp[:, 0:W5],
                        start=True, stop=True), deps=[s_rc])
                    s_s = sc.step("vector", lambda v, b=b, t0=t0:
                                  v.tensor_tensor(
                                      sco_b[b][:, t0:t0 + W5],
                                      elb[:, t0:t0 + W5], l3ps[:, 0:W5],
                                      op=ALU.mult), deps=[s_bc, s_el])
                    prev_s = s_s
                    sc_done.append(s_s)
                s_cp = sc.step("vector", lambda v, b=b: v.tensor_copy(
                    wrk[:], sco_b[b][:]), deps=sc_done + [rt_q.get(b - 1)])
                prev = s_cp
                for r in range(cg.mr_rounds):
                    s_m8 = sc.step("vector", lambda v: v.max(
                        out=m8[:], in_=wrk[:]), deps=[prev])
                    prev = sc.step("vector", lambda v: v.match_replace(
                        out=wrk[:], in_to_replace=m8[:], in_values=wrk[:],
                        imm_value=0.0), deps=[s_m8])
                s_mk = sc.step("vector", lambda v, b=b: v.tensor_tensor(
                    msk[:], sco_b[b][:], wrk[:], op=ALU.is_gt), deps=[prev])
                s_cu = sc.step("vector", lambda v: v.tensor_tensor_scan(
                    cum16[:], msk[:], zro16[:], 0.0, op0=ALU.add, op1=ALU.add),
                    deps=[s_mk, z16])
                s_q = sc.step("vector", lambda v: v.tensor_tensor(
                    cum16[:], cum16[:], msk[:], op=ALU.mult), deps=[s_cu])
                prev_tr = None
                for ch in range(SCH):
                    s1 = sc.step("tensor", lambda te, ch=ch: te.transpose(
                        trps[:, 0:16], cum16[:, ch * 128:(ch + 1) * 128],
                        identf[:]),
                        deps=[s_q, prev_tr, l_idf])
                    s2 = sc.step("tensor", lambda te, b=b, ch=ch: te.transpose(
                        trps[:, 16:32], sco_b[b][:, ch * 128:(ch + 1) * 128],
                        identf[:]), deps=[s1])
                    o = (b * SCH + ch) * 16
                    e1 = sc.step("scalar", lambda s, o=o: s.copy(
                        qT[:, o:o + 16], trps[:, 0:16]), deps=[s1, s2])
                    e2 = sc.step("scalar", lambda s, o=o: s.copy(
                        scTb[:, o:o + 16], trps[:, 16:32]), deps=[s2, e1])
                    prev_tr = e2
                rt_q[b] = prev_tr

            if "rt" in self.debug:
                d_dbq = self.ddbg("qt", [128, B * SCH * 16])
                d_dbs = self.ddbg("sct", [128, B * SCH * 16], BF16)
                sc.dma("sync", lambda sy: sy.dma_start(out=d_dbq, in_=qT[:]),
                       deps=[rt_q[bb] for bb in range(B)])
                sc.dma("sync", lambda sy: sy.dma_start(out=d_dbs, in_=scTb[:]),
                       deps=[rt_q[bb] for bb in range(B)])

            self.rt_wc = {}
            self.rt_st = {}
            last_st = None
            prev_wr = None
            prev_wc = None
            for b in range(B):
                for e in range(E):
                    sts = []
                    for chk in range(SCH):
                        o = (b * SCH + chk) * 16 + e
                        sts.append(sc.step("vector",
                            lambda v, chk=chk, o=o: v.tensor_single_scalar(
                                stb[:, chk * CAP:(chk + 1) * CAP],
                                iotacap[:], qT[:, o:o + 1],
                                op=ALU.is_equal),
                            deps=[rt_q[b], last_st, l_iot]))
                    wms = []
                    for chk in range(SCH):
                        o = (b * SCH + chk) * 16 + e
                        wms.append(sc.step("tensor",
                            lambda te, chk=chk, o=o: te.matmul(
                                wrps[:, :], scTb[:, o:o + 1],
                                stb[:, chk * CAP:(chk + 1) * CAP],
                                start=(chk == 0), stop=(chk == SCH - 1)),
                            deps=[sts[chk], wms[-1] if wms else prev_wr]))
                    s_wr = sc.step("scalar", lambda s: s.copy(
                        wrow[:], wrps[:]), deps=[wms[-1]])
                    prev_wr = s_wr
                    wcs = []
                    for cc in range(CAPCH):
                        w = min(128, CAP - cc * 128)
                        wcs.append(sc.step("tensor", lambda te, cc=cc, w=w:
                            te.matmul(wcps[:, cc:cc + 1],
                                      wrow[0:1, cc * 128:cc * 128 + w],
                                      self.onesbf[0:1, 0:1],
                                      start=True, stop=True),
                            deps=[s_wr, wcs[-1] if wcs else prev_wc,
                                  self.c_ld[4]]))
                    o = (b * E + e) * CAPCH
                    self.rt_wc[(b, e)] = sc.step("vector", lambda v, o=o:
                        v.tensor_copy(self.wcol[:, o:o + CAPCH], wcps[:]),
                        deps=wcs)
                    prev_wc = self.rt_wc[(b, e)]
                    last_st = sc.dma("sync", lambda sy, b=b, e=e: sy.dma_start(
                        out=self.d_st[b, e].rearrange("c p k -> p c k"),
                        in_=stb[:].rearrange("p (c k) -> p c k", k=CAP)),
                        deps=sts + [wms[-1]])
                    self.rt_st[(b, e)] = last_st
            if "rt" in self.debug:
                d_dbw = self.ddbg("wcol", [128, B * E * CAPCH])
                sc.dma("sync", lambda sy: sy.dma_start(
                    out=d_dbw, in_=self.wcol[:]),
                    deps=list(self.rt_wc.values()))

    def ph_experts(self):
        sc, cg = self.sc, self.cfg
        B, D, E, F, CAP = cg.b, cg.d, cg.e, cg.f, cg.cap
        DCH, CAPCH = cg.dch, cg.capch
        SCH = cg.seq // 128
        FH = F // 2
        FHCH = FH // 128
        BC = B * CAP
        with ExitStack() as esc:
            wgt = self.sb("wgt", [128, 2 * DCH * FH], BF16, esc)
            wut = self.sb("wut", [128, 2 * DCH * FH], BF16, esc)
            wdt = self.sb("wdt", [128, 2 * FHCH * D], BF16, esc)
            stl = self.sb("stl", [128, B * SCH * CAP], BF16, esc)
            xtok = self.sb("xtok", [128, DCH * BC], BF16, esc)
            hbuf = self.sb("hbuf", [128, 2 * FHCH * BC], BF16, esc)
            eo512 = self.sb("eo512", [128, 512], BF16, esc)
            ggb = self.sb("ggb", [128, BC], F32, esc)
            gps = self.ps("gps", [128, BC], F32, esc)
            ups = self.ps("ups", [128, BC], F32, esc)
            xps = self.ps("xps", [128, CAP], F32, esc)
            dps = self.ps("dps", [128, 512], F32, esc)

            stl_use = [None, None]
            w_use = [None, None]
            w_use_d = [None]
            prev_gather = None
            prev_gp = None
            prev_up = None
            prev_hh = None
            ex_last = []
            for e in range(E):
                st_ld = {}
                g_done = {}
                for b in range(B):
                    st_ld[b] = sc.dma("gpsimd",
                        lambda g, b=b, e=e: g.dma_start(
                            out=stl[:, b * SCH * CAP:(b + 1) * SCH * CAP]
                            .rearrange("p (c k) -> p c k", k=CAP),
                            in_=self.d_st[b, e].rearrange("c p k -> p c k")),
                        deps=[self.rt_st[(b, e)], stl_use[b]])
                lw = {}
                for hf in range(2):
                    wbi = (e * 2 + hf) % 2
                    wo_g = wbi * DCH * FH
                    lw[("g", hf)] = sc.dma("gpsimd",
                        lambda g, e=e, hf=hf, wo_g=wo_g: g.dma_start(
                            out=wgt[:, wo_g:wo_g + DCH * FH]
                            .rearrange("p (c f) -> p c f", f=FH),
                            in_=self.d_wgT[e][:, hf * FH:(hf + 1) * FH]
                            .rearrange("(c p) f -> p c f", p=128)),
                        deps=[w_use[wbi]])
                    lw[("u", hf)] = sc.dma("gpsimd",
                        lambda g, e=e, hf=hf, wo_g=wo_g: g.dma_start(
                            out=wut[:, wo_g:wo_g + DCH * FH]
                            .rearrange("p (c f) -> p c f", f=FH),
                            in_=self.d_wuT[e][:, hf * FH:(hf + 1) * FH]
                            .rearrange("(c p) f -> p c f", p=128)),
                        deps=[w_use[wbi]])
                    lw[("d", hf)] = sc.dma("gpsimd",
                        lambda g, e=e, hf=hf: g.dma_start(
                            out=wdt[:, hf * FHCH * D:(hf + 1) * FHCH * D]
                            .rearrange("p (c dd) -> p c dd", dd=D),
                            in_=self.d_wdT[e][hf * FH:(hf + 1) * FH, :]
                            .rearrange("(c p) dd -> p c dd", p=128)),
                        deps=[w_use_d[0]])
                for b in range(B):
                    for dt_ in range(DCH):
                        gmm = []
                        for chk in range(SCH):
                            gmm.append(sc.step("tensor",
                                lambda te, b=b, dt_=dt_, chk=chk:
                                te.matmul(
                                    xps[:, 0:CAP],
                                    self.xnt(b, chk)[:, dt_ * 128:
                                                     (dt_ + 1) * 128],
                                    stl[:, (b * SCH + chk) * CAP:
                                        (b * SCH + chk + 1) * CAP],
                                    start=(chk == 0), stop=(chk == SCH - 1)),
                                deps=[st_ld[b],
                                      self.ln_done[0] if e == 0 else None,
                                      gmm[-1] if gmm else prev_gather]))
                        g_done[(b, dt_)] = sc.step("scalar",
                            lambda s, b=b, dt_=dt_: s.copy(
                                xtok[:, (dt_ * B + b) * CAP:
                                     (dt_ * B + b + 1) * CAP],
                                xps[:, 0:CAP]),
                            deps=[gmm[-1], prev_gather, prev_hh])
                        prev_gather = g_done[(b, dt_)]
                    stl_use[b] = g_done[(b, DCH - 1)]
                h_done = {}
                for hf in range(2):
                    wbi = (e * 2 + hf) % 2
                    wo_g = wbi * DCH * FH
                    for ft in range(FHCH):
                        gm = []
                        for dc in range(DCH):
                            ws = wgt[:, wo_g + dc * FH + ft * 128:
                                     wo_g + dc * FH + ft * 128 + 128]
                            gm.append(sc.step("tensor",
                                lambda te, dc=dc, ws=ws: te.matmul(
                                    gps[:, 0:BC], ws,
                                    xtok[:, dc * BC:(dc + 1) * BC],
                                    start=(dc == 0), stop=(dc == DCH - 1)),
                                deps=[lw[("g", hf)]] +
                                     [g_done[(bb, dc)] for bb in range(B)] +
                                     [gm[-1] if gm else prev_gp]))
                        s_gel = sc.step("scalar", lambda s: s.activation(
                            ggb[:], gps[:, 0:BC], AF.Gelu, bias=0.0,
                            scale=1.0), deps=[gm[-1], prev_hh])
                        prev_gp = s_gel
                        um = []
                        for dc in range(DCH):
                            ws = wut[:, wo_g + dc * FH + ft * 128:
                                     wo_g + dc * FH + ft * 128 + 128]
                            um.append(sc.step("tensor",
                                lambda te, dc=dc, ws=ws: te.matmul(
                                    ups[:, 0:BC], ws,
                                    xtok[:, dc * BC:(dc + 1) * BC],
                                    start=(dc == 0), stop=(dc == DCH - 1)),
                                deps=[lw[("u", hf)]] +
                                     [g_done[(bb, dc)] for bb in range(B)] +
                                     [um[-1] if um else prev_up]))
                        s_h = sc.step("vector",
                            lambda v, hf=hf, ft=ft: v.tensor_tensor(
                                hbuf[:, (hf * FHCH + ft) * BC:
                                     (hf * FHCH + ft + 1) * BC],
                                ggb[:], ups[:, 0:BC], op=ALU.mult),
                            deps=[s_gel, um[-1], prev_hh])
                        prev_up = s_h
                        prev_hh = s_h
                        h_done[(hf, ft)] = s_h
                prev_d = None
                eo_prev = getattr(self, "_eo_prev", [None, None])
                for b in range(B):
                    eo_dmas = []
                    for cc in range(CAPCH):
                        o = (b * E + e) * CAPCH + cc
                        for ns in range(D // 512):
                            ebi = 0
                            dmms = []
                            for hf in range(2):
                                for fc in range(FHCH):
                                    hs = hbuf[:, (hf * FHCH + fc) * BC +
                                              b * CAP + cc * 128:
                                              (hf * FHCH + fc) * BC +
                                              b * CAP + cc * 128 + 128]
                                    wsd = wdt[:, hf * FHCH * D + fc * D +
                                              ns * 512:
                                              hf * FHCH * D + fc * D +
                                              ns * 512 + 512]
                                    dmms.append(sc.step("tensor",
                                        lambda te, hs=hs, wsd=wsd,
                                        fst=(hf == 0 and fc == 0),
                                        lst=(hf == 1 and fc == FHCH - 1):
                                        te.matmul(dps[:, 0:512], hs, wsd,
                                                  start=fst, stop=lst),
                                        deps=[lw[("d", hf)],
                                              h_done[(hf, fc)],
                                              dmms[-1] if dmms else prev_d]))
                            s_ev = sc.step("vector",
                                lambda v, ebi=ebi, o=o:
                                v.tensor_single_scalar(
                                    eo512[:, 0:512],
                                    dps[:, 0:512],
                                    self.wcol[:, o:o + 1], op=ALU.mult),
                                deps=[dmms[-1], self.rt_wc[(b, e)],
                                      eo_prev[ebi]])
                            prev_d = s_ev
                            s_dma = sc.dma("gpsimd",
                                lambda g, b=b, e=e, cc=cc, ns=ns, ebi=ebi:
                                g.dma_start(
                                    out=self.d_eo[b, e].rearrange(
                                        "(c p) dd -> p c dd", p=128)
                                    [:, cc, ns * 512:(ns + 1) * 512],
                                    in_=eo512[:, 0:512]),
                                deps=[s_ev])
                            eo_prev[ebi] = s_dma
                            eo_dmas.append(s_dma)
                    ex_last.extend(eo_dmas)
                    self.ex_eo = getattr(self, "ex_eo", {})
                    self.ex_eo[(b, e)] = eo_dmas[-1]
                    self._eo_prev = eo_prev
                w_use[0 if (e * 2) % 2 == 0 else 1] = prev_d
                w_use[(e * 2 + 1) % 2] = prev_d
                w_use_d[0] = prev_d
            self.ex_done = ex_last

    def ph_scatter(self):
        sc, cg = self.sc, self.cfg
        B, SEQ, D, E, CAP = cg.b, cg.seq, cg.d, cg.e, cg.cap
        CAPCH, TT = cg.capch, cg.tt
        with ExitStack() as ssc:
            eob = self.sb("eob", [128, E * CAPCH * D], BF16, ssc)
            sttb = self.sb("sttb", [128, E * CAP], BF16, ssc)
            s2b = self.sb("s2b", [128, E * CAPCH * 128], BF16, ssc)
            ftile = self.sb("ftile", [128, 2 * D], F32, ssc)
            s2ps = self.ps("s2ps", [128, 128], BF16, ssc)
            acc0 = self.ps("acc0", [128, 512], F32, ssc)
            acc1 = self.ps("acc1", [128, 512], F32, ssc)
            accs = [acc0, acc1]
            prev_ft = [None, None]
            prev_stl = None
            for b in range(B):
                l_eo = []
                for e in range(E):
                    l_eo.append(sc.dma("gpsimd", lambda g, b=b, e=e:
                        g.dma_start(
                            out=eob[:, e * CAPCH * D:(e + 1) * CAPCH * D]
                            .rearrange("p (c dd) -> p c dd", dd=D),
                            in_=self.d_eo[b, e].rearrange(
                                "(c p) dd -> p c dd", p=128)),
                        deps=[self.ex_eo[(b, e)], prev_stl]))
                prev_s2t = None
                for t in range(TT):
                    s_stl = sc.dma("gpsimd", lambda g, b=b, t=t:
                        g.dma_start(
                            out=sttb[:].rearrange("p (e k) -> p e k",
                                                  k=CAP),
                            in_=self.d_st[b, :, t, :, :].rearrange(
                                "e p k -> p e k")),
                        deps=[self.rt_st[(b, E - 1)], prev_stl, prev_s2t])
                    s2_done = []
                    prev_s2 = prev_stl
                    for e in range(E):
                        for cc in range(CAPCH):
                            s_t = sc.step("tensor",
                                lambda te, e=e, cc=cc: te.transpose(
                                    s2ps[:], sttb[:, e * CAP + cc * 128:
                                                  e * CAP + cc * 128 + 128],
                                    self.identb[:]),
                                deps=[s_stl, prev_s2, self.c_ld[0]])
                            s_e = sc.step("scalar",
                                lambda s, e=e, cc=cc: s.copy(
                                    s2b[:, (e * CAPCH + cc) * 128:
                                        (e * CAPCH + cc + 1) * 128],
                                    s2ps[:]), deps=[s_t])
                            prev_s2 = s_e
                            prev_s2t = s_e
                            s2_done.append(s_e)
                    evs = []
                    for ns in range(D // 512):
                        mml = []
                        for e in range(E):
                            for cc in range(CAPCH):
                                first = (e == 0 and cc == 0)
                                mml.append(sc.step("tensor",
                                    lambda te, e=e, cc=cc, ns=ns, first=first:
                                    te.matmul(
                                        accs[ns][:],
                                        s2b[:, (e * CAPCH + cc) * 128:
                                            (e * CAPCH + cc + 1) * 128],
                                        eob[:, e * CAPCH * D + cc * D +
                                            ns * 512:
                                            e * CAPCH * D + cc * D +
                                            ns * 512 + 512],
                                        start=first,
                                        stop=(e == E - 1 and
                                              cc == CAPCH - 1)),
                                    deps=[s2_done[e * CAPCH + cc], l_eo[e],
                                          mml[-1] if mml else
                                          prev_ft[ns]]))
                        tbi = t % 2
                        s_ev = sc.step("vector", lambda v, ns=ns, tbi=tbi:
                            v.tensor_copy(
                                ftile[:, tbi * D + ns * 512:
                                      tbi * D + ns * 512 + 512],
                                accs[ns][:]),
                            deps=[mml[-1], prev_ft[ns]])
                        prev_ft[ns] = s_ev
                        evs.append(s_ev)
                    s_fs = sc.dma("sync", lambda sy, b=b, t=t, tbi=t % 2:
                        sy.dma_start(out=self.d_fsc[b, t],
                                     in_=ftile[:, tbi * D:(tbi + 1) * D]),
                        deps=evs)
                    prev_ft = [s_fs, s_fs]
                    prev_stl = s_fs if t == TT - 1 else prev_stl
                    self.fsc_done = getattr(self, "fsc_done", {})
                    self.fsc_done[(b, t)] = s_fs

    def ph_shared(self):
        sc, cg = self.sc, self.cfg
        B, SEQ, D, FSH = cg.b, cg.seq, cg.d, cg.fsh
        DCH, FSHCH, TT = cg.dch, cg.fshch, cg.tt
        with ExitStack() as ssc:
            xnfmb = self.sb("xnfmb", [128, DCH * SEQ], BF16, ssc)
            wsg = self.sb("wsg", [128, DCH * FSH], BF16, ssc)
            wsu = self.sb("wsu", [128, DCH * FSH], BF16, ssc)
            wsd = self.sb("wsd", [128, FSHCH * D], BF16, ssc)
            hsb = self.sb("hsb", [128, FSHCH * 512], BF16, ssc)
            gsg = self.sb("gsg", [128, 512], F32, ssc)
            gtile = self.sb("gtile", [128, D], BF16, ssc)
            ptl = self.sb("ptl", [128, 2 * D], F32, ssc)
            sps = self.ps("sps", [128, 512], F32, ssc)
            sus = self.ps("sus", [128, 512], F32, ssc)
            acc0 = self.ps("sacc0", [128, 512], F32, ssc)
            acc1 = self.ps("sacc1", [128, 512], F32, ssc)
            accs = [acc0, acc1]
            lws = []
            lws.append(sc.dma("gpsimd", lambda g: g.dma_start(
                out=wsg[:].rearrange("p (c f) -> p c f", f=FSH),
                in_=self.d_wsgT.rearrange(
                    "(c p) f -> p c f", p=128))))
            lws.append(sc.dma("gpsimd", lambda g: g.dma_start(
                out=wsu[:].rearrange("p (c f) -> p c f", f=FSH),
                in_=self.d_wsuT.rearrange(
                    "(c p) f -> p c f", p=128))))
            lws.append(sc.dma("gpsimd", lambda g: g.dma_start(
                out=wsd[:].rearrange("p (c dd) -> p c dd", dd=D),
                in_=self.d_wsdT.rearrange(
                    "(c p) dd -> p c dd", p=128))))
            self.g_done = {}
            prev_g = None
            prev_hs = None
            prev_gel = None
            prev_acc = [None, None]
            prev_ptl = [None, None]
            for b in range(B):
                l_fm = []
                for dc in range(DCH):
                    l_fm.append(sc.dma("sync", lambda sy, b=b, dc=dc:
                        sy.dma_start(
                            out=xnfmb[:, dc * SEQ:(dc + 1) * SEQ],
                            in_=self.d_xnfm[b, dc]),
                        deps=[self.ln_fm[(b, dc, TT - 1)],
                              self.g_done.get((b - 1, TT - 1))]))
                for t4 in range(0, TT, 4):
                    nt = min(4, TT - t4)
                    nw4 = nt * 128
                    h_done = []
                    for fc in range(FSHCH):
                        gmm = []
                        for dc in range(DCH):
                            ws = wsg[:, dc * FSH + fc * 128:
                                     dc * FSH + fc * 128 + 128]
                            gmm.append(sc.step("tensor",
                                lambda te, t4=t4, dc=dc, ws=ws, nw4=nw4:
                                te.matmul(
                                    sps[:, 0:nw4], ws,
                                    xnfmb[:, dc * SEQ + t4 * 128:
                                          dc * SEQ + t4 * 128 + nw4],
                                    start=(dc == 0), stop=(dc == DCH - 1)),
                                deps=[lws[0], l_fm[dc],
                                      gmm[-1] if gmm else prev_gel]))
                        s_gel = sc.step("scalar",
                            lambda s, nw4=nw4: s.activation(
                                gsg[:, 0:nw4], sps[:, 0:nw4], AF.Gelu,
                                bias=0.0, scale=1.0),
                            deps=[gmm[-1], prev_hs])
                        prev_gel = s_gel
                        umm = []
                        for dc in range(DCH):
                            ws = wsu[:, dc * FSH + fc * 128:
                                     dc * FSH + fc * 128 + 128]
                            umm.append(sc.step("tensor",
                                lambda te, t4=t4, dc=dc, ws=ws, nw4=nw4:
                                te.matmul(
                                    sus[:, 0:nw4], ws,
                                    xnfmb[:, dc * SEQ + t4 * 128:
                                          dc * SEQ + t4 * 128 + nw4],
                                    start=(dc == 0), stop=(dc == DCH - 1)),
                                deps=[lws[1], l_fm[dc],
                                      umm[-1] if umm else prev_hs]))
                        s_h = sc.step("vector", lambda v, fc=fc, nw4=nw4:
                                      v.tensor_tensor(
                                          hsb[:, fc * 512:fc * 512 + nw4],
                                          gsg[:, 0:nw4], sus[:, 0:nw4],
                                          op=ALU.mult),
                                      deps=[s_gel, umm[-1], prev_hs])
                        prev_hs = s_h
                        h_done.append(s_h)
                    for tt_ in range(nt):
                        t = t4 + tt_
                        tbi = t % 2
                        s_pl = sc.dma("sync", lambda sy, b=b, t=t, tbi=tbi:
                            sy.dma_start(out=ptl[:, tbi * D:(tbi + 1) * D],
                                         in_=self.d_fsc[b, t]),
                            deps=[self.fsc_done[(b, t)], prev_ptl[tbi]])
                        sil = []
                        for ns in range(D // 512):
                            mml = []
                            for fc in range(FSHCH):
                                ws = wsd[:, fc * D + ns * 512:
                                         fc * D + ns * 512 + 512]
                                mml.append(sc.step("tensor",
                                    lambda te, fc=fc, ns=ns, ws=ws, tt_=tt_:
                                    te.matmul(
                                        accs[ns][:],
                                        hsb[:, fc * 512 + tt_ * 128:
                                            fc * 512 + tt_ * 128 + 128],
                                        ws,
                                        start=(fc == 0),
                                        stop=(fc == FSHCH - 1)),
                                    deps=[lws[2], h_done[fc],
                                          mml[-1] if mml else prev_acc[ns]]))
                            s_ad = sc.step("vector",
                                lambda v, ns=ns, tbi=tbi: v.tensor_tensor(
                                    ptl[:, tbi * D + ns * 512:
                                        tbi * D + ns * 512 + 512],
                                    ptl[:, tbi * D + ns * 512:
                                        tbi * D + ns * 512 + 512],
                                    accs[ns][:], op=ALU.add),
                                deps=[mml[-1], s_pl])
                            prev_acc[ns] = s_ad
                            s_si = sc.step("scalar",
                                lambda s, ns=ns, tbi=tbi: s.activation(
                                    gtile[:, ns * 512:(ns + 1) * 512],
                                    ptl[:, tbi * D + ns * 512:
                                        tbi * D + ns * 512 + 512],
                                    AF.Silu, bias=0.0, scale=1.0),
                                deps=[s_ad, prev_g])
                            sil.append(s_si)
                        s_g = sc.dma("sync", lambda sy, b=b, t=t: sy.dma_start(
                            out=self.d_g[b, t], in_=gtile[:]), deps=sil)
                        self.g_done[(b, t)] = s_g
                        prev_g = s_g
                        prev_ptl[tbi] = s_g

    def ph_attention(self):
        import os
        KATT = int(os.environ.get("KATT", "99"))
        sc, cg = self.sc, self.cfg
        B, SEQ, D, H, DH, MEM = cg.b, cg.seq, cg.d, cg.h, cg.dh, cg.mem
        DCH, JP, JCN, TT = cg.dch, cg.jp, cg.jcn, cg.tt
        MEMC = cg.memc
        TOPK = float(cg.topk)
        NBIS = len(cg.bis_sched)
        with ExitStack() as asc:
            wob = self.sb("wob", [128, DCH * D], BF16, asc)
            kbuf = self.sb("kbuf", [128, DCH * JP], BF16, asc)
            vT = self.sb("vT", [128, JCN * D], BF16, asc)
            qtl = self.sb("qtl", [128, 2 * DCH * 128], BF16, asc)
            scr = self.sb("scr", [128, 2 * 2048], BF16, asc)
            csb = self.sb("csb", [128, 2 * JP], BF16, asc)
            pkb = self.sb("pkb", [128, 256], BF16, asc)
            xrow = self.sb("xrow", [128, 16 * JP], BF16, asc)
            bst = self.sb("bst", [128, 16 * 8], F32, asc)
            wjoff = self.sb("wjoff", [128, TT], F32, asc)
            bstb = self.sb("bstb", [128, 16 * 2], BF16, asc)
            patgs = self.sb("patgs", [128, 16 * 128], BF16, asc)
            fpbig = self.sb("fpbig", [8, 256], BF16, asc)
            fpad = self.sb("fpad", [8, 128], BF16, asc)
            atb = self.sb("atb", [128, 2 * 2048], BF16, asc)
            obuf = self.sb("obuf", [128, DCH * 128], BF16, asc)
            gtl = self.sb("gtl", [128, D], BF16, asc)
            xtl = self.sb("xtl", [128, D], F32, asc)
            otl = self.sb("otl", [128, D], F32, asc)
            wstr = self.sb("wstr", [128, 2 * D], BF16, asc)
            xnstr = scr
            vstg = csb
            mstg = self.sb("mstg", [128, MEM], BF16, asc)
            dtp = self.ps("dtp", [128, 512], F32, asc)
            pkp = self.ps("pkp", [128, 1024], BF16, asc)
            pm1 = self.ps("pm1", [128, 512], F32, asc)
            pob = self.ps("pob", [128, 512], F32, asc)
            ovp = self.ps("ovp", [64, 2048], F32, asc)
            DSLOT = [0, 512]

            def bcol(gl, j):
                return bst[:, gl * 8 + j:gl * 8 + j + 1]

            s_lwo = sc.dma("gpsimd", lambda g: g.dma_start(
                out=wob[:].rearrange("p (c o) -> p c o", o=D),
                in_=self.d_wo.rearrange(
                    "(c p) o -> p c o", p=128)))
            wj_prev = None
            for tw in range(TT):
                wjv = 2.0 * TOPK - len(cg.jlist(tw * 16)) * 128
                wj_prev = sc.step("gpsimd",
                    lambda g, tw=tw, wjv=wjv: g.memset(
                        wjoff[:, tw:tw + 1], wjv),
                    deps=[wj_prev])
            s_fb0 = sc.step("gpsimd", lambda gg: gg.memset(fpbig[:], 1.0))
            s_fb1 = sc.step("gpsimd", lambda gg: gg.affine_select(
                fpbig[:], fpbig[:], pattern=[[-1, 256]],
                compare_op=ALU.is_ge, fill=0.0, base=120,
                channel_multiplier=1), deps=[s_fb0])
            s_fbig = sc.step("gpsimd", lambda gg: gg.tensor_scalar(
                fpbig[:], fpbig[:], 57344.0, -57344.0,
                op0=ALU.mult, op1=ALU.add), deps=[s_fb1])
            s_fp0 = sc.step("gpsimd", lambda gg: gg.affine_select(
                fpad[:], self.ones8b[:], pattern=[[-1, 128]],
                compare_op=ALU.is_ge, fill=0.0, base=MEM - 1,
                channel_multiplier=0), deps=[self.c_ld[6]])
            s_fpad = sc.step("gpsimd", lambda gg: gg.tensor_scalar(
                fpad[:], fpad[:], 57344.0, -57344.0,
                op0=ALU.mult, op1=ALU.add), deps=[s_fp0])

            for b in range(B):
                prev_b = getattr(self, "attn_b_done", None)
                zk = sc.step("vector", lambda v: v.memset(kbuf[:], 0.0),
                             deps=[prev_b])
                zv = sc.step("vector", lambda v: v.memset(vT[:], 0.0),
                             deps=[prev_b])
                kd_extra = [zk]
                prev_ms = prev_b
                for oc in range(DCH):
                    l_mk = sc.dma("gpsimd", lambda g, oc=oc: g.dma_start(
                        out=kbuf[:, oc * JP + SEQ:oc * JP + SEQ + MEM],
                        in_=self.d_memk[2 * oc:2 * oc + 2].rearrange(
                            "h d m -> (h d) m")), deps=[zk])
                    kd_extra.append(l_mk)
                    l_mv = sc.dma("gpsimd", lambda g, oc=oc: g.dma_start(
                        out=mstg[:],
                        in_=self.d_memv[2 * oc:2 * oc + 2].rearrange(
                            "h d m -> (h d) m")), deps=[prev_ms])
                    s_mt = sc.step("tensor", lambda te: te.transpose(
                        pkp[0:MEM, 0:128], mstg[:], self.identb[:]),
                        deps=[l_mv, prev_ms, self.c_ld[0]])
                    s_mc = sc.step("scalar", lambda s, oc=oc: s.copy(
                        vT[0:MEM, MEMC * D + oc * 128:
                           MEMC * D + (oc + 1) * 128],
                        pkp[0:MEM, 0:128]), deps=[s_mt, zv])
                    prev_ms = s_mc
                    kd_extra.append(s_mc)
                qkv_done = {"q": {}, "k": {}, "v": {}}
                vt_done = {}
                prev_ev = None
                prev_w = [prev_b, prev_b]
                prev_xn = None
                prev_vstg = prev_ms
                prev_qst = [prev_b, prev_b]
                TW = min(512, SEQ)
                for t5 in range(SEQ // TW):
                    l_xn = sc.dma("sync", lambda sy, b=b, t5=t5, TW=TW:
                        sy.dma_start(
                        out=xnstr[:, 0:DCH * TW]
                        .rearrange("p (c w) -> p c w", c=DCH),
                        in_=self.d_xnfm[b, :, :, t5 * TW:(t5 + 1) * TW]
                        .rearrange("c p w -> p c w")),
                        deps=[self.ln_fm[(b, DCH - 1, TT - 1)], prev_xn,
                              prev_b])
                    prev_ev2 = {0: prev_ev, 1: prev_b}
                    for ni, (nm, dsrc) in enumerate(
                            [("q", self.d_wq), ("k", self.d_wk),
                             ("v", self.d_wv)]):
                        for oc in range(DCH):
                            wbi = (ni * DCH + oc) % 2
                            pbi = (ni * DCH + oc) % 2
                            pml = pm1 if pbi == 0 else pob
                            prev_ev = prev_ev2[pbi]
                            l_w = sc.dma("gpsimd",
                                lambda g, dsrc=dsrc, oc=oc, wbi=wbi:
                                g.dma_start(
                                    out=wstr[:, wbi * D:wbi * D + D]
                                    .rearrange("p (c o) -> p c o", o=128),
                                    in_=dsrc[:, oc * 128:(oc + 1) * 128]
                                    .rearrange("(c p) o -> p c o", p=128)),
                                deps=[prev_w[wbi]])
                            mm = []
                            for dc in range(DCH):
                                ws = wstr[:, wbi * D + dc * 128:
                                          wbi * D + dc * 128 + 128]
                                mm.append(sc.step("tensor",
                                    lambda te, ws=ws, dc=dc, pml=pml:
                                    te.matmul(
                                        pml[:, 0:TW], ws,
                                        xnstr[:, dc * TW:(dc + 1) * TW],
                                        start=(dc == 0),
                                        stop=(dc == DCH - 1)),
                                    deps=[l_w, l_xn,
                                          mm[-1] if mm else prev_ev]))
                            prev_w[wbi] = mm[-1]
                            if nm == "q":
                                pass
                            if nm == "q":
                                qi = oc % 2
                                tgt = qtl[:, qi * TW:(qi + 1) * TW]
                                s_e0 = sc.step("scalar",
                                    lambda s, tgt=tgt, pml=pml: s.activation(
                                        tgt, pml[:, 0:TW], AF.Copy,
                                        bias=0.0, scale=0.125),
                                    deps=[mm[-1], prev_ev, prev_qst[qi]])
                                s_e = sc.dma("sync",
                                    lambda sy, b=b, oc=oc, t5=t5, tgt=tgt,
                                    TW=TW:
                                    sy.dma_start(
                                        out=self.d_qfm[b, oc, :,
                                            t5 * TW:(t5 + 1) * TW],
                                        in_=tgt), deps=[s_e0])
                                prev_qst[qi] = s_e
                                prev_ev = s_e0
                            elif nm == "k":
                                tb = kbuf[:, oc * JP + t5 * TW:
                                          oc * JP + t5 * TW + TW]
                                s_e = sc.step("scalar",
                                    lambda s, tb=tb, pml=pml:
                                    s.copy(tb, pml[:, 0:TW]),
                                    deps=[mm[-1], prev_ev, zk])
                                prev_ev = s_e
                            else:
                                s_e0 = sc.step("scalar",
                                    lambda s, pml=pml:
                                    s.copy(vstg[:, 0:TW], pml[:, 0:TW]),
                                    deps=[mm[-1], prev_ev, prev_vstg])
                                prev_ev = s_e0
                                tr_prev = s_e0
                                for jb in range(TW // 128):
                                    jc0 = t5 * (TW // 128) + jb
                                    s_vt = sc.step("tensor",
                                        lambda te, jb=jb: te.transpose(
                                            pkp[:, 512:640],
                                            vstg[:, jb * 128:(jb + 1) * 128],
                                            self.identb[:]),
                                        deps=[s_e0, tr_prev, prev_ms,
                                              self.c_ld[0]])
                                    s_vc = sc.step("scalar",
                                        lambda s, jc0=jc0, oc=oc: s.copy(
                                            vT[:, jc0 * D + oc * 128:
                                               jc0 * D + (oc + 1) * 128],
                                            pkp[:, 512:640]),
                                        deps=[s_vt, zv])
                                    tr_prev = s_vc
                                    vt_done[(oc, jc0)] = s_vc
                                prev_vstg = tr_prev
                                s_e = tr_prev
                            qkv_done[nm][(oc, t5)] = s_e
                            prev_ev2[pbi] = prev_ev
                    prev_xn = qkv_done["v"][(DCH - 1, t5)]
                for oc in range(DCH):
                    vt_done[(oc, MEMC)] = kd_extra[-1]
                qd = list(qkv_done["q"].values())
                kd = list(qkv_done["k"].values()) + kd_extra

                prev_dt = {0: prev_b, 1: prev_b}
                prev_pk = prev_ms
                pmch = {0: prev_ev2[0], 1: prev_ev2[1]}
                uctr = [0]
                prev_av = {0: prev_b, 1: prev_b}
                prev_oev = prev_b
                prev_out = prev_b
                prev_packT = prev_b
                prev_xr = {}
                prev_pmx = prev_b
                prev_qt = [None, None]
                if KATT < 2:
                    self.attn_b_done = qd[-1]
                    continue
                BTW = 2
                xrow_free = getattr(self, "_xrow_free", {})
                patg_done = getattr(self, "_patg_done", {})
                prev_bstb = getattr(self, "_prev_bstb", {})
                for t in range(TT):
                    jl_t = cg.jlist(t * 16 + 15)
                    njc = len(jl_t)
                    qbi = t % 2
                    s_qt = sc.dma("gpsimd", lambda g, b=b, t=t, qbi=qbi:
                        g.dma_start(
                            out=qtl[:, qbi * DCH * 128:(qbi + 1) * DCH * 128]
                            .rearrange("p (c w) -> p c w", c=DCH),
                            in_=self.d_qfm[b, :, :, t * 128:(t + 1) * 128]
                            .rearrange("c p w -> p c w")),
                        deps=qd + [prev_qt[qbi]])
                    xr_done = {}
                    last_bt = None
                    nblk = (njc + BTW - 1) // BTW
                    for blk in range(nblk):
                        jcs = jl_t[blk * BTW:(blk + 1) * BTW]
                        nw = len(jcs)
                        bt_ev = []
                        for bi, jc in enumerate(jcs):
                            for hg in range(4):
                                mm4 = []
                                for hx in range(4):
                                    hh = hg * 4 + hx
                                    stat = kbuf[(hh % 2) * 64:
                                                (hh % 2) * 64 + 64,
                                                (hh // 2) * JP + jc * 128:
                                                (hh // 2) * JP + jc * 128
                                                + 128]
                                    mov = qtl[(hh % 2) * 64:(hh % 2) * 64
                                              + 64,
                                              qbi * DCH * 128 +
                                              (hh // 2) * 128:
                                              qbi * DCH * 128 +
                                              (hh // 2) * 128 + 128]
                                    mm4.append(sc.step("tensor",
                                        lambda te, stat=stat, mov=mov,
                                        hx=hx:
                                        te.matmul(dtp[:, hx * 128:
                                                      hx * 128 + 128],
                                                  stat, mov,
                                                  start=(hx == 0),
                                                  stop=(hx == 3)),
                                        deps=kd + [s_qt, prev_dt[0],
                                                   mm4[-1] if mm4 else
                                                   None]))
                                s_bt = sc.step("scalar",
                                    lambda s, bi=bi, hg=hg:
                                    s.copy(
                                        scr[:, bi * 2048:(bi + 1) * 2048]
                                        .rearrange("p (i k) -> p i k", k=16)
                                        [:, :, hg * 4:hg * 4 + 4],
                                        dtp[:, 0:512].rearrange(
                                            "p (h i) -> p i h", h=4)),
                                    deps=[mm4[-1], prev_xn,
                                          prev_packT,
                                          prev_qst[0], prev_qst[1]])
                                bt_ev.append(s_bt)
                                prev_dt[0] = s_bt
                        last_bt = bt_ev[-1]
                        for gl in range(16):
                            g = t * 16 + gl
                            gli0 = blk * BTW
                            pks = []
                            evs = []
                            for si, (bi, jc) in enumerate(zip(
                                    range(nw), jcs)):
                                s_pk = sc.step("tensor",
                                    lambda te, bi=bi, gl=gl: te.transpose(
                                        pkp[:, 0:128],
                                        scr[:, bi * 2048 + gl * 128:
                                            bi * 2048 + gl * 128 + 128],
                                        self.identb[:]),
                                    deps=bt_ev + [prev_pk, self.c_ld[0],
                                                  pks[-1] if pks else None])
                                pks.append(s_pk)
                                s_pe = sc.step("vector",
                                    lambda v, si=si: v.tensor_copy(
                                        pkb[:, si * 128:si * 128 + 128],
                                        pkp[:, 0:128]),
                                    deps=[s_pk, prev_pmx])
                                evs.append(s_pe)
                                prev_pk = s_pe
                            fills = []
                            for si, jc in enumerate(jcs):
                                if jc == t:
                                    fills.append((si,
                                        fpbig[:, 120 - 8 * gl:
                                              248 - 8 * gl], s_fbig))
                                elif jc == MEMC:
                                    fills.append((si, fpad[:], s_fpad))
                            tb = uctr[0] % 2
                            uctr[0] += 1
                            tps = pm1 if tb == 0 else pob
                            s_pm = sc.step("tensor", lambda te, nw=nw,
                                tps=tps, bd=bool(fills):
                                te.matmul(tps[:, 0:nw * 128], self.prepat[:],
                                          pkb[:, 0:nw * 128], start=True,
                                          stop=not bd),
                                deps=evs + [self.s_pre, pmch[tb]])
                            prev_pmx = s_pm
                            for fi, (si, fsrc, fdep) in enumerate(fills):
                                s_pm = sc.step("tensor",
                                    lambda te, si=si, fsrc=fsrc, tps=tps,
                                    lastf=(fi == len(fills) - 1):
                                    te.matmul(tps[:, si * 128:si * 128
                                                  + 128],
                                              self.e8c[:], fsrc,
                                              start=False, stop=lastf),
                                    deps=[s_pm, fdep, self.c_ld[2]])
                            s_xr = sc.step("scalar",
                                lambda s, gl=gl, gli0=gli0, nw=nw, tps=tps:
                                s.copy(xrow[:, gl * JP + gli0 * 128:
                                            gl * JP + (gli0 + nw) * 128],
                                       tps[:, 0:nw * 128]),
                                deps=[s_pm, prev_xr.get(gl),
                                      xrow_free.get(gl)])
                            pmch[tb] = s_xr
                            prev_xr[gl] = s_xr
                            xr_done.setdefault(gl, []).append(s_xr)
                            prev_packT = pks[-1]
                    prev_qt[qbi] = last_bt
                    if KATT < 3:
                        continue
                    def bcols(j, g0=0, g1=16):
                        return bst[:].rearrange("p (g o) -> p g o",
                                                o=8)[:, g0:g1, j:j + 1]
                    s_i0 = sc.step("vector", lambda v: v.memset(
                        bcols(0), BIS_LO),
                        deps=[patg_done.get(gl) for gl in range(16)] +
                             [prev_packT])
                    prev_state = [s_i0]
                    prev_cnt2 = getattr(self, "_prev_cnt2", [None, None])
                    cnt_last = {}
                    wjt = njc * 128
                    for it in range(NBIS):
                        c_it = float(16.0 / (1 << it))
                        s_mid = sc.step("vector", lambda v, c_it=c_it:
                            v.tensor_single_scalar(bcols(2), bcols(0), c_it,
                                                   op=ALU.add),
                            deps=prev_state)
                        s_nm = sc.step("vector", lambda v, c_it=c_it:
                            v.tensor_scalar(
                                bcols(5, 8, 16), bcols(0, 8, 16), -1.0,
                                -c_it, op0=ALU.mult, op1=ALU.add),
                            deps=prev_state)
                        cv = []
                        cs = []
                        for gl in range(16):
                            if gl < 8:
                                sv = csb[:, 0:wjt]
                                s_c = sc.step("vector",
                                    lambda v, gl=gl, sv=sv, wjt=wjt:
                                    v.tensor_scalar(
                                        sv, xrow[:, gl * JP:gl * JP + wjt],
                                        bcol(gl, 2), None, op0=ALU.is_ge,
                                        op1=ALU.add, accum_out=bcol(gl, 3)),
                                    deps=[s_mid, prev_cnt2[0], last_bt] +
                                         xr_done.get(gl, []))
                                prev_cnt2[0] = s_c
                                cv.append(s_c)
                            else:
                                sv = csb[:, JP:JP + wjt]
                                s_c = sc.step("scalar",
                                    lambda s, gl=gl, sv=sv, wjt=wjt:
                                    s.activation(
                                        sv, xrow[:, gl * JP:gl * JP + wjt],
                                        AF.Sign, bias=bcol(gl, 5), scale=1.0,
                                        accum_out=bcol(gl, 3)),
                                    deps=[s_nm, prev_cnt2[1], last_bt] +
                                         xr_done.get(gl, []))
                                prev_cnt2[1] = s_c
                                cs.append(s_c)
                            cnt_last[gl] = s_c
                        s_pv = sc.step("vector", lambda v:
                            v.tensor_single_scalar(
                                bcols(4, 0, 8), bcols(3, 0, 8), TOPK,
                                op=ALU.is_ge), deps=cv)
                        s_ps = sc.step("vector", lambda v, t=t:
                            v.tensor_single_scalar(
                                bcols(4, 8, 16), bcols(3, 8, 16),
                                wjoff[:, t:t + 1], op=ALU.is_ge),
                            deps=cs)
                        s_lo = sc.step("vector", lambda v, c_it=c_it:
                            v.scalar_tensor_tensor(
                                bcols(0), bcols(4), c_it, bcols(0),
                                op0=ALU.mult, op1=ALU.add),
                            deps=[s_pv, s_ps])
                        prev_state = [s_lo]
                    self._prev_cnt2 = prev_cnt2
                    bis_fin = prev_state[0]
                    if KATT < 4:
                        continue
                    sm_done = {}
                    for gl in range(16):
                        xr = xrow[:, gl * JP:gl * JP + wjt]
                        s_pexp = sc.step("scalar", lambda s, xr=xr:
                            s.activation(xr, xr, AF.Exp,
                                         bias=self.cexp[:, 0:1],
                                         scale=1.0),
                            deps=[bis_fin, cnt_last[gl]])
                        s_lb = sc.step("vector", lambda v, gl=gl:
                            v.tensor_copy(bstb[:, gl * 2:gl * 2 + 1],
                                          bcol(gl, 0)),
                            deps=[bis_fin, prev_bstb.get(gl)])
                        s_pt0 = sc.step("scalar", lambda s, gl=gl:
                            s.activation(
                                bstb[:, gl * 2 + 1:gl * 2 + 2],
                                bstb[:, gl * 2:gl * 2 + 1], AF.Exp,
                                bias=self.cexp[:, 0:1], scale=1.0),
                            deps=[s_lb, s_pexp])
                        s_pt = sc.step("vector", lambda v, gl=gl:
                            v.tensor_copy(bcol(gl, 6),
                                          bstb[:, gl * 2 + 1:gl * 2 + 2]),
                            deps=[s_pt0])
                        prev_bstb[gl] = s_pt
                        s_z = sc.step("vector", lambda v, xr=xr, gl=gl:
                            v.scalar_tensor_tensor(
                                xr, xr, bcol(gl, 6), xr, op0=ALU.is_gt,
                                op1=ALU.mult, accum_out=bcol(gl, 3)),
                            deps=[s_pexp, s_pt])
                        s_ln = sc.step("scalar", lambda s, gl=gl: s.activation(
                            bcol(gl, 7), bcol(gl, 3), AF.Ln, bias=0.0,
                            scale=1.0), deps=[s_z])
                        s_iv = sc.step("scalar", lambda s, gl=gl: s.activation(
                            bcol(gl, 7), bcol(gl, 7), AF.Exp, bias=0.0,
                            scale=-1.0), deps=[s_ln])
                        sm_done[gl] = (s_z, s_iv)
                    self._prev_bstb = prev_bstb
                    if KATT < 5:
                        continue
                    for gl in range(16):
                        patg_done[gl] = sc.step("vector", lambda v, gl=gl:
                            v.tensor_single_scalar(
                                patgs[:, gl * 128:(gl + 1) * 128],
                                self.postpat[:], bcol(gl, 7), op=ALU.mult),
                            deps=[sm_done[gl][1], self.s_post,
                                  patg_done.get(gl), xrow_free.get(gl)])
                    av_mm = {}
                    for li, jc in enumerate(jl_t):
                        abi = li % 2
                        at_done = []
                        for gq in range(4):
                            tb = gq % 2
                            tps = pm1 if tb == 0 else pob
                            mxs = []
                            for si in range(4):
                                gl = gq * 4 + si
                                mxs.append(sc.step("tensor",
                                    lambda te, gl=gl, li=li, si=si, tps=tps:
                                    te.matmul(
                                        tps[:, si * 128:si * 128 + 128],
                                        xrow[:, gl * JP + li * 128:
                                             gl * JP + li * 128 + 128],
                                        patgs[:, gl * 128:(gl + 1) * 128],
                                        start=(si == 0), stop=(si == 3)),
                                    deps=[patg_done[gl], sm_done[gl][0],
                                          pmch[tb],
                                          mxs[-1] if mxs else None]))
                            at_ev = sc.step("vector",
                                lambda v, gq=gq, abi=abi, tps=tps:
                                v.tensor_copy(
                                    atb[:, abi * 2048:(abi + 1) * 2048]
                                    .rearrange("p (k e i) -> p k e i",
                                               e=16, i=8)
                                    [:, :, gq * 4:gq * 4 + 4, :],
                                    tps[:, 0:512].rearrange(
                                        "p (s i k) -> p k s i", s=4, i=8)),
                                deps=[mxs[-1], prev_av[abi]])
                            at_done.append(at_ev)
                            pmch[tb] = at_ev
                            if li == njc - 1:
                                for si in range(4):
                                    xrow_free[gq * 4 + si] = at_ev
                        avl = []
                        for k2 in range(H):
                            st_fl = (li == 0) and (k2 % 4 == 0)
                            sp_fl = (li == njc - 1) and (k2 % 4 == 3)
                            s_av = sc.step("tensor",
                                lambda te, k2=k2, jc=jc, abi=abi,
                                st_fl=st_fl, sp_fl=sp_fl:
                                te.matmul(
                                    ovp[0:64, k2 * 128:k2 * 128 + 128],
                                    vT[:, jc * D + k2 * 64:
                                       jc * D + k2 * 64 + 64],
                                    atb[:, abi * 2048 + k2 * 128:
                                        abi * 2048 + k2 * 128 + 128],
                                    start=st_fl, stop=sp_fl),
                                deps=at_done + [vt_done[(k2 // 2, jc)],
                                                av_mm.get((k2, li - 1)),
                                                av_mm.get((k2 - 1, li)),
                                                prev_oev if li == 0 else
                                                None])
                            av_mm[(k2, li)] = s_av
                            avl.append(s_av)
                        prev_av[abi] = avl[-1]
                    self._xrow_free = xrow_free
                    self._patg_done = patg_done
                    self._prev_bstb = prev_bstb
                    if KATT < 6:
                        continue
                    oev = []
                    av_last = [av_mm[(kk, njc - 1)] for kk in range(H)]
                    for k2 in range(H):
                        s_oe = sc.step("scalar", lambda s, k2=k2:
                            s.copy(obuf[(k2 % 2) * 64:(k2 % 2) * 64 + 64,
                                        (k2 // 2) * 128:(k2 // 2) * 128 + 128],
                                   ovp[0:64, k2 * 128:k2 * 128 + 128]),
                            deps=av_last + [prev_out,
                                            oev[-1] if oev else None])
                        oev.append(s_oe)
                    prev_oev = oev[-1]
                    s_lx = sc.dma("sync", lambda sy, b=b, t=t: sy.dma_start(
                        out=xtl[:], in_=self.d_x[b, t * 128:(t + 1) * 128, :]),
                        deps=[prev_out])
                    s_lg = sc.dma("gpsimd", lambda g, b=b, t=t: g.dma_start(
                        out=gtl[:], in_=self.d_g[b, t]),
                        deps=[self.g_done[(b, t)], prev_out])
                    cmb = []
                    for ns in range(D // 512):
                        mml = []
                        for c in range(DCH):
                            ws = wob[:, c * D + ns * 512:c * D + ns * 512 + 512]
                            mml.append(sc.step("tensor",
                                lambda te, c=c, ws=ws: te.matmul(
                                    pob[:, 0:512],
                                    obuf[:, c * 128:(c + 1) * 128], ws,
                                    start=(c == 0), stop=False),
                                deps=[s_lwo] + oev +
                                     [mml[-1] if mml else
                                      (cmb[-1] if cmb else pmch[1])]))
                        s_bomm = sc.step("tensor", lambda te, ns=ns:
                            te.matmul(pob[:, 0:512], self.ones8b[0:1, :],
                                      self.bo_rowb[0:1, ns * 512:
                                                    (ns + 1) * 512],
                                      start=False, stop=True),
                            deps=[mml[-1], self.s_bo_ld, self.c_ld[6]])
                        s_c1 = sc.step("vector", lambda v, ns=ns:
                            v.tensor_copy(otl[:, ns * 512:(ns + 1) * 512],
                                          pob[:, 0:512]),
                            deps=[s_bomm, prev_out])
                        s_c2 = sc.step("vector", lambda v, ns=ns:
                            v.tensor_tensor(otl[:, ns * 512:(ns + 1) * 512],
                                            otl[:, ns * 512:(ns + 1) * 512],
                                            gtl[:, ns * 512:(ns + 1) * 512],
                                            op=ALU.mult),
                            deps=[s_c1, s_lg])
                        s_c3 = sc.step("vector", lambda v, ns=ns:
                            v.tensor_tensor(otl[:, ns * 512:(ns + 1) * 512],
                                            otl[:, ns * 512:(ns + 1) * 512],
                                            xtl[:, ns * 512:(ns + 1) * 512],
                                            op=ALU.add),
                            deps=[s_c2, s_lx])
                        cmb.append(s_c1)
                        cmb.append(s_c3)
                    pmch[1] = cmb[-2]
                    prev_out = sc.dma("sync", lambda sy, b=b, t=t:
                        sy.dma_start(
                            out=self.d_out[b, t * 128:(t + 1) * 128, :],
                            in_=otl[:]), deps=[cmb[-1], cmb[-2]])
                self.attn_b_done = prev_out if KATT >= 6 else \
                    (self.sc.last.get("vector"))


def relayout_inputs(cfg, inputs):
    cg = cfg
    x = np.ascontiguousarray(inputs["L0_attn_x"])
    ins = {
        "wq": np.ascontiguousarray(inputs["L0_attn_Wq"]),
        "wk": np.ascontiguousarray(inputs["L0_attn_Wk"]),
        "wv": np.ascontiguousarray(inputs["L0_attn_Wv"]),
        "wo": np.ascontiguousarray(inputs["L0_attn_Wo"]),
        "pre": np.ascontiguousarray(inputs["L0_attn_pre_proj"]),
        "post": np.ascontiguousarray(inputs["L0_attn_post_proj"]),
        "memk": np.ascontiguousarray(
            inputs["L0_attn_mem_k"].transpose(0, 2, 1)),
        "memv": np.ascontiguousarray(
            inputs["L0_attn_mem_v"].transpose(0, 2, 1)),
        "bo": np.ascontiguousarray(inputs["L0_attn_bo"]).reshape(1, -1),
        "gatewT": np.ascontiguousarray(inputs["L0_ffn_gate_w"].T),
        "wgT": np.ascontiguousarray(inputs["L0_ffn_Wg"].transpose(0, 2, 1)),
        "wuT": np.ascontiguousarray(inputs["L0_ffn_Wu"].transpose(0, 2, 1)),
        "wdT": np.ascontiguousarray(inputs["L0_ffn_Wd"].transpose(0, 2, 1)),
        "wsgT": np.ascontiguousarray(inputs["L0_ffn_Ws_g"].T),
        "wsuT": np.ascontiguousarray(inputs["L0_ffn_Ws_u"].T),
        "wsdT": np.ascontiguousarray(inputs["L0_ffn_Ws_d"].T),
    }
    ins.update(host_constants(cfg))
    nb = x.shape[0]
    per_core = nb // cfg.b
    maps = []
    for c in range(per_core):
        m = dict(ins)
        m["x"] = np.ascontiguousarray(x[c * cfg.b:(c + 1) * cfg.b])
        maps.append(m)
    return maps


def kernel(**inputs):
    cfg = Cfg()
    maps = relayout_inputs(cfg, {k: np.asarray(v) for k, v in inputs.items()})
    b = Build(cfg)
    nc = b.build()
    res = run_bass_kernel_spmd(nc, maps, core_ids=list(range(len(maps))))
    outs = [r["out"] for r in res.results]
    return np.concatenate(outs, axis=0)


# version 18 (final)
# speedup vs baseline: 1.1294x; 1.0201x over previous
from contextlib import ExitStack
from dataclasses import dataclass, field

import numpy as np
import ml_dtypes

import concourse.bass as bass
import concourse.mybir as mybir
from concourse.bass_utils import run_bass_kernel_spmd

F32 = mybir.dt.float32
BF16 = mybir.dt.bfloat16
AF = mybir.ActivationFunctionType
ALU = mybir.AluOpType
AX = mybir.AxisListType
BFNP = ml_dtypes.bfloat16

EXP_BIAS = -48.0
BIS_LO, BIS_HI = -16.0, 16.0
N_CORES = 8
ENGINES = ("vector", "scalar", "tensor", "gpsimd", "sync")


@dataclass
class Cfg:
    b: int = 2
    seq: int = 2048
    d: int = 1024
    h: int = 16
    dh: int = 64
    mem: int = 16
    topk: int = 64
    e: int = 16
    f: int = 2048
    cap: int = 256
    fsh: int = 2048
    bis_sched: tuple = ("vector",) * 11

    @property
    def tt(self): return self.seq // 128
    @property
    def dch(self): return self.d // 128
    @property
    def jp(self): return ((self.mem + self.seq + 127) // 128) * 128
    @property
    def jcn(self): return self.jp // 128
    @property
    def fch(self): return self.f // 128
    @property
    def fshch(self): return self.fsh // 128
    @property
    def capch(self): return max(1, self.cap // 128)
    @property
    def mr_rounds(self): return self.cap // 8

    def cw(self, g):
        return min((8 * g + 7) // 128 + 1, self.seq // 128)

    def cw_t(self, t):
        return self.cw(t * 16 + 15)

    @property
    def memc(self):
        return self.seq // 128

    def jlist(self, g):
        return list(range(self.cw(g))) + [self.memc]


class Sched:
    NRING = 8
    ERING = 4

    def __init__(self, nc, ctx):
        self.nc = nc
        self.steps = []
        self.last = {}
        self.bar = {}
        self.sems = {}
        for e in ENGINES:
            for i in range(self.ERING):
                self.sems[f"{e}{i}"] = ctx.enter_context(
                    nc.semaphore(f"sem_{e}{i}"))
        for q in ("hdma", "gdma"):
            for i in range(self.NRING):
                self.sems[f"{q}{i}"] = ctx.enter_context(
                    nc.semaphore(f"sem_{q}{i}"))
        self.dma_idx = {"hdma": 0, "gdma": 0}
        self.eng_idx = dict.fromkeys(ENGINES, 0)
        self.ring_prev = {}

    def step(self, eng, fn, deps=()):
        sid = len(self.steps)
        slot = self.eng_idx[eng] % self.ERING
        self.eng_idx[eng] += 1
        dl = [d for d in deps if d is not None]
        if self.bar.get(eng) is not None:
            dl.append(self.bar[eng])
        self.steps.append((eng, fn, dl, False, None, slot))
        self.last[eng] = sid
        return sid

    def dma(self, eng, fn, deps=()):
        sid = len(self.steps)
        q = "gdma" if eng == "gpsimd" else "hdma"
        slot = self.dma_idx[q] % self.NRING
        self.dma_idx[q] += 1
        dl = [d for d in deps if d is not None]
        prev = self.ring_prev.get((q, slot))
        if prev is not None:
            dl.append(prev)
        if self.bar.get(eng) is not None:
            dl.append(self.bar[eng])
        self.steps.append((eng, fn, dl, True, q, slot))
        self.last[eng] = sid
        self.last[f"{q}{slot}"] = sid
        self.ring_prev[(q, slot)] = sid
        return sid

    def barrier(self):
        snap = dict(self.last)
        newbar = {}
        for e in ENGINES:
            deps = [v for k, v in snap.items() if v is not None]
            slot = self.eng_idx[e] % self.ERING
            self.eng_idx[e] += 1
            sid = len(self.steps)
            self.steps.append((e, lambda x: x.nop(), deps, False, None, slot))
            self.last[e] = sid
            newbar[e] = sid
        self.bar = newbar

    def emit(self):
        done = {}
        cnt = {}
        for i, (eng, fn, deps, dma, q, slot) in enumerate(self.steps):
            if dma:
                key = f"{q}{slot}"
                cnt[key] = cnt.get(key, 0) + 16
                done[i] = (key, cnt[key])
            else:
                key = f"{eng}{slot}"
                cnt[key] = cnt.get(key, 0) + 1
                done[i] = (key, cnt[key])
        per_engine = {e: [] for e in ENGINES}
        for i, st in enumerate(self.steps):
            per_engine[st[0]].append((i, st))

        def section(ename):
            def body(e):
                waited = {}
                for i, (eng, fn, deps, dma, q, slot) in per_engine[ename]:
                    for d in deps:
                        s, v = done[d]
                        if waited.get(s, -1) >= v:
                            continue
                        e.wait_ge(self.sems[s], v)
                        waited[s] = v
                    instr = fn(e)
                    if instr is None:
                        instr = e.nop()
                    s, v = done[i]
                    instr.then_inc(self.sems[s], 16 if dma else 1)
            return body

        finals = dict(cnt)

        def sync_section(e):
            section("sync")(e)
            for s, v in finals.items():
                e.wait_ge(self.sems[s], v)

        with self.nc.Block() as blk:
            blk.vector(section("vector"))
            blk.scalar(section("scalar"))
            blk.tensor(section("tensor"))
            blk.gpsimd(section("gpsimd"))
            blk.sync(sync_section)


def host_constants(cfg):
    c = {}
    c["identb"] = np.eye(128, dtype=BFNP)
    c["identf"] = np.eye(128, dtype=np.float32)
    e8 = np.zeros((8, 128), np.float32)
    for r in range(8):
        e8[r, r * 16:(r + 1) * 16] = 1.0
    c["e8c"] = e8.astype(BFNP)
    c["iota_cap"] = np.tile(np.arange(1, cfg.cap + 1, dtype=np.float32),
                            (128, 1)).astype(BFNP)
    c["onesbf"] = np.ones((1, 128), BFNP)
    c["onesff"] = np.ones((16, 128), np.float32)
    c["ones8b"] = np.ones((8, 128), BFNP)
    return c


class Build:

    def __init__(self, cfg, debug=()):
        self.cfg = cfg
        self.debug = set(debug)
        self.nc = bass.Bass("TRN2", target_bir_lowering=False)
        self.dbg = {}

    def din(self, name, shape, dtype=F32):
        return self.nc.dram_tensor(name, list(shape), dtype,
                                   kind="ExternalInput").ap()

    def dout(self, name, shape, dtype=F32):
        return self.nc.dram_tensor(name, list(shape), dtype,
                                   kind="ExternalOutput").ap()

    def dtmp(self, name, shape, dtype=F32):
        return self.nc.dram_tensor(name, list(shape), dtype,
                                   kind="Internal").ap()

    def ddbg(self, name, shape, dtype=F32):
        t = self.dout("dbg_" + name, shape, dtype)
        self.dbg[name] = t
        return t

    def build(self):
        cg = self.cfg
        nc = self.nc
        B, SEQ, D, H, DH, MEM = cg.b, cg.seq, cg.d, cg.h, cg.dh, cg.mem
        E, F, CAP, FSH = cg.e, cg.f, cg.cap, cg.fsh
        TT, DCH, JP, JCN = cg.tt, cg.dch, cg.jp, cg.jcn
        FCH, FSHCH, CAPCH = cg.fch, cg.fshch, cg.capch
        SCH = SEQ // 128
        self.d_x = self.din("x", [B, SEQ, D])
        self.d_wq = self.din("wq", [D, D])
        self.d_wk = self.din("wk", [D, D])
        self.d_wv = self.din("wv", [D, D])
        self.d_wo = self.din("wo", [D, D])
        self.d_pre = self.din("pre", [H, H])
        self.d_post = self.din("post", [H, H])
        self.d_memk = self.din("memk", [H, DH, MEM])
        self.d_memv = self.din("memv", [H, DH, MEM])
        self.d_bo = self.din("bo", [1, D])
        self.d_gatewT = self.din("gatewT", [D, E])
        self.d_wgT = self.din("wgT", [E, D, F])
        self.d_wuT = self.din("wuT", [E, D, F])
        self.d_wdT = self.din("wdT", [E, F, D])
        self.d_wsgT = self.din("wsgT", [D, FSH])
        self.d_wsuT = self.din("wsuT", [D, FSH])
        self.d_wsdT = self.din("wsdT", [FSH, D])
        self.d_identb = self.din("identb", [128, 128], BF16)
        self.d_identf = self.din("identf", [128, 128], F32)
        self.d_e8c = self.din("e8c", [8, 128], BF16)
        self.d_iotacap = self.din("iota_cap", [128, CAP], BF16)
        self.d_onesbf = self.din("onesbf", [1, 128], BF16)
        self.d_onesff = self.din("onesff", [16, 128], F32)
        self.d_ones8b = self.din("ones8b", [8, 128], BF16)

        self.d_xnfm = self.dtmp("xnfm", [B, DCH, 128, SEQ], BF16)
        self.d_st = self.dtmp("stm", [B, E, SCH, 128, CAP], BF16)
        self.d_eo = self.dtmp("eo", [B, E, CAPCH * 128, D], BF16)
        self.d_g = self.dtmp("gbuf", [B, TT, 128, D], BF16)
        self.d_fsc = self.dtmp("fsc", [B, TT, 128, D], F32)
        self.d_qfm = self.dtmp("qfm", [B, DCH, 128, SEQ], BF16)
        self.d_out = self.dout("out", [B, SEQ, D])

        with ExitStack() as ctx:
            self.ctx = ctx
            self.sc = Sched(nc, ctx)
            import os
            stop = int(os.environ.get("KPHASES", "99"))
            self.alloc_persistent()
            self.ph_consts()
            self.sc.barrier()
            self._csc.close()
            with ExitStack() as xnsc:
                self.xntm = self.sb("xn_tm",
                                    [128, self.cfg.b * self.cfg.tt *
                                     self.cfg.d], BF16, xnsc)
                if stop >= 1:
                    self.ph_layernorm()
                    self.sc.barrier()
                if stop >= 2:
                    self.ph_routing()
                    self.sc.barrier()
                if stop >= 3:
                    self.ph_experts()
                    self.sc.barrier()
            if stop >= 4:
                self.ph_scatter()
                self.sc.barrier()
            if stop >= 5:
                self.ph_shared()
                self.sc.barrier()
            if stop >= 6:
                self.ph_attention()
            self.sc.emit()
        return nc

    def sb(self, name, shape, dtype=F32, scope=None):
        return (scope or self.ctx).enter_context(
            self.nc.sbuf_tensor("sb_" + name, list(shape), dtype))

    def ps(self, name, shape, dtype=F32, scope=None):
        return (scope or self.ctx).enter_context(
            self.nc.psum_tensor("ps_" + name, list(shape), dtype))

    def alloc_persistent(self):
        cg = self.cfg
        self.identb = self.sb("identb", [128, 128], BF16)
        self.e8c = self.sb("e8c", [8, 128], BF16)
        self.onesbf = self.sb("onesbf", [1, 128], BF16)
        self.onesff = self.sb("onesff", [16, 128], F32)
        self.ones8b = self.sb("ones8b", [8, 128], BF16)
        self.prepat = self.sb("prepat", [128, 128], BF16)
        self.cexp = self.sb("cexp", [128, 1], F32)
        self.postpat = self.sb("postpat", [128, 128], BF16)
        self.bo_rowb = self.sb("bo_rowb", [1, cg.d], BF16)
        self.wcol = self.sb("wcol", [128, cg.b * cg.e * cg.capch], F32)

    def xnt(self, b, t):
        o = (b * self.cfg.tt + t) * self.cfg.d
        return self.xntm[:, o:o + self.cfg.d]

    def ph_consts(self):
        sc, cg = self.sc, self.cfg
        csc = ExitStack()
        self._csc = csc
        ppf = self.sb("ppf", [128, 256], F32, csc)
        ld = []
        for dst, src in [(self.identb, self.d_identb),
                         (self.e8c, self.d_e8c),
                         (self.onesbf, self.d_onesbf),
                         (self.onesff, self.d_onesff),
                         (self.ones8b, self.d_ones8b)]:
            ld.append(sc.dma("sync", lambda sy, dst=dst, src=src:
                             sy.dma_start(out=dst[:], in_=src)))
        self.c_ld = [ld[0], None, ld[1], None, ld[2], ld[3], ld[4]]
        ld = self.c_ld
        self.s_bo_ld = sc.dma("gpsimd", lambda g: g.dma_start(
            out=self.bo_rowb[:], in_=self.d_bo))
        z0 = sc.step("vector", lambda v: v.memset(ppf[:], 0.0))
        self.s_cexp = sc.step("vector", lambda v: v.memset(
            self.cexp[:], EXP_BIAS))
        pp = []
        for i8 in range(8):
            pp.append(sc.dma("sync", lambda sy, i8=i8: sy.dma_start(
                out=ppf[i8 * 16:(i8 + 1) * 16, i8 * 16:(i8 + 1) * 16],
                in_=self.d_pre), deps=[z0]))
            pp.append(sc.dma("sync", lambda sy, i8=i8: sy.dma_start(
                out=ppf[i8 * 16:(i8 + 1) * 16,
                        128 + i8 * 16:128 + (i8 + 1) * 16],
                in_=self.d_post), deps=[z0]))
        self.s_pre = sc.step("vector", lambda v: v.tensor_copy(
            self.prepat[:], ppf[:, 0:128]), deps=pp)
        self.s_post = sc.step("vector", lambda v: v.tensor_copy(
            self.postpat[:], ppf[:, 128:256]), deps=pp)

    def ph_layernorm(self):
        sc, cg = self.sc, self.cfg
        B, TT, D, DCH = cg.b, cg.tt, cg.d, cg.dch
        with ExitStack() as lsc:
            ln_x = self.sb("ln_x", [128, 2 * D], F32, lsc)
            ln_sq = self.sb("ln_sq", [128, D], F32, lsc)
            ln_st = self.sb("ln_st", [128, 16], F32, lsc)
            ln_stg = self.sb("ln_stg", [128, 2 * 128], BF16, lsc)
            ln_ps = self.ps("ln_ps", [128, 2048], BF16, lsc)
            last_xn = [None, None]
            last_tr = [None, None]
            self.ln_fm = {}
            if "ln" in self.debug:
                d_dbgxn = self.ddbg("xn", [B, cg.seq, D])
            prev_sq = None
            for i in range(B * TT):
                b, t = divmod(i, TT)
                bi = i % 2
                xt = ln_x[:, bi * D:(bi + 1) * D]
                st = ln_st[:, bi * 8:bi * 8 + 8]
                s_ld = sc.dma("sync", lambda sy, b=b, t=t, xt=xt: sy.dma_start(
                    out=xt, in_=self.d_x[b, t * 128:(t + 1) * 128, :]),
                    deps=[last_xn[bi]])
                s_sum = sc.step("vector", lambda v, xt=xt, st=st: v.reduce_sum(
                    st[:, 0:1], xt, axis=AX.X), deps=[s_ld, last_xn[bi]])
                s_sq = sc.step("scalar", lambda s, xt=xt, st=st: s.activation(
                    ln_sq[:], xt, AF.Square, bias=0.0, scale=1.0,
                    accum_out=st[:, 1:2]), deps=[s_ld, prev_sq, last_xn[bi]])
                prev_sq = s_sq
                s_m = sc.step("vector", lambda v, st=st: v.tensor_single_scalar(
                    st[:, 2:3], st[:, 0:1], 1.0 / D, op=ALU.mult), deps=[s_sum])
                s_m2 = sc.step("vector", lambda v, st=st: v.tensor_single_scalar(
                    st[:, 3:4], st[:, 1:2], 1.0 / D, op=ALU.mult), deps=[s_sq])
                s_msq = sc.step("vector", lambda v, st=st: v.tensor_tensor(
                    st[:, 4:5], st[:, 2:3], st[:, 2:3], op=ALU.mult),
                    deps=[s_m])
                s_var = sc.step("vector", lambda v, st=st: v.scalar_tensor_tensor(
                    st[:, 5:6], st[:, 3:4], 1e-5, st[:, 4:5],
                    op0=ALU.add, op1=ALU.subtract), deps=[s_m2, s_msq])
                s_sd = sc.step("scalar", lambda s, st=st: s.activation(
                    st[:, 6:7], st[:, 5:6], AF.Sqrt, bias=0.0, scale=1.0),
                    deps=[s_var])
                s_r = sc.step("vector", lambda v, st=st: v.reciprocal(
                    st[:, 7:8], st[:, 6:7]), deps=[s_sd])
                s_xn = sc.step("vector", lambda v, xt=xt, st=st, b=b, t=t:
                               v.tensor_scalar(
                                   self.xnt(b, t), xt, st[:, 2:3], st[:, 7:8],
                                   op0=ALU.subtract, op1=ALU.mult),
                               deps=[s_r, s_ld])
                last_xn[bi] = s_xn
                if "ln" in self.debug:
                    sc.dma("sync", lambda sy, b=b, t=t: sy.dma_start(
                        out=d_dbgxn[b, t * 128:(t + 1) * 128, :],
                        in_=self.xnt(b, t)), deps=[s_xn])
                for dc in range(DCH):
                    pbi = (i * DCH + dc) % 2
                    s_tr = sc.step("tensor", lambda te, b=b, t=t, dc=dc, pbi=pbi:
                                   te.transpose(
                                       ln_ps[:, pbi * 1024:pbi * 1024 + 128],
                                       self.xnt(b, t)[:, dc * 128:(dc + 1) * 128],
                                       self.identb[:]),
                                   deps=[s_xn, self.c_ld[0], last_tr[pbi]])
                    s_ev = sc.step("scalar", lambda s, pbi=pbi: s.copy(
                        ln_stg[:, pbi * 128:(pbi + 1) * 128],
                        ln_ps[:, pbi * 1024:pbi * 1024 + 128]), deps=[s_tr])
                    s_stm = sc.dma("sync", lambda sy, b=b, t=t, dc=dc, pbi=pbi:
                                   sy.dma_start(
                                       out=self.d_xnfm[b, dc, :,
                                                       t * 128:(t + 1) * 128],
                                       in_=ln_stg[:, pbi * 128:(pbi + 1) * 128]),
                                   deps=[s_ev])
                    last_tr[pbi] = s_stm
                    self.ln_fm[(b, dc, t)] = s_stm
            self.ln_done = [s for s in last_xn + last_tr if s is not None]

    def ph_routing(self):
        sc, cg = self.sc, self.cfg
        B, SEQ, D, E, CAP = cg.b, cg.seq, cg.d, cg.e, cg.cap
        DCH, CAPCH = cg.dch, cg.capch
        SCH = SEQ // 128
        with ExitStack() as rsc:
            gtw = self.sb("gtw", [128, DCH * E], BF16, rsc)
            fmb = self.sb("fmb", [128, 2 * 512], BF16, rsc)
            sco_b = [self.sb(f"sco{b}", [16, SEQ], F32, rsc) for b in range(B)]
            wrk = self.sb("wrk", [16, SEQ], F32, rsc)
            zro16 = self.sb("zro16", [16, SEQ], F32, rsc)
            m8 = self.sb("m8", [16, 8], F32, rsc)
            msk = self.sb("msk", [16, SEQ], F32, rsc)
            cum16 = self.sb("cum16", [16, SEQ], F32, rsc)
            qT = self.sb("qT", [128, B * SCH * 16], F32, rsc)
            scTb = self.sb("scTb", [128, B * SCH * 16], BF16, rsc)
            stb = self.sb("stb", [128, SCH * CAP], BF16, rsc)
            wrow = self.sb("wrow", [1, CAP], BF16, rsc)
            elb = self.sb("elb", [16, SEQ], F32, rsc)
            rcp = self.sb("rcp", [1, 512], F32, rsc)
            identf = self.sb("identf", [16, 16], F32, rsc)
            iotacap = self.sb("iotacap", [128, CAP], BF16, rsc)
            l_idf = sc.dma("sync", lambda sy: sy.dma_start(
                out=identf[:], in_=self.d_identf[0:16, 0:16]))
            l_iot = sc.dma("sync", lambda sy: sy.dma_start(
                out=iotacap[:], in_=self.d_iotacap))
            lps = self.ps("lps", [16, 512], F32, rsc)
            l2ps = self.ps("l2ps", [1, 512], F32, rsc)
            l3ps = self.ps("l3ps", [16, 512], F32, rsc)
            trps = self.ps("trps", [128, 32], F32, rsc)
            wrps = self.ps("wrps", [1, CAP], F32, rsc)
            wcps = self.ps("wcps", [128, CAPCH], F32, rsc)

            s_gtw = sc.dma("gpsimd", lambda g: g.dma_start(
                out=gtw[:].rearrange("p (c e) -> p c e", e=E),
                in_=self.d_gatewT.rearrange("(c p) e -> p c e", p=128)))
            z16 = sc.step("vector", lambda v: v.memset(zro16[:], 0.0))
            W5 = min(512, SEQ)
            NW = SEQ // W5
            rt_q = {}
            prev_weldc = None
            for b in range(B):
                sc_done = []
                prev_s = None
                for w5 in range(NW):
                    t0 = w5 * W5
                    mms = []
                    for dc in range(DCH):
                        fbi = dc % 2
                        s_l = sc.dma("gpsimd",
                            lambda g, b=b, dc=dc, t0=t0, fbi=fbi: g.dma_start(
                                out=fmb[:, fbi * 512:fbi * 512 + W5],
                                in_=self.d_xnfm[b, dc, :, t0:t0 + W5]),
                            deps=[self.ln_fm[(b, dc, t0 // 128)],
                                  self.ln_fm[(b, dc, min(t0 // 128 + 3,
                                                         cg.tt - 1))],
                                  mms[dc - 2] if dc >= 2 else prev_weldc])
                        mms.append(sc.step("tensor",
                            lambda te, dc=dc, fbi=fbi: te.matmul(
                                lps[:, 0:W5], gtw[:, dc * E:(dc + 1) * E],
                                fmb[:, fbi * 512:fbi * 512 + W5],
                                start=(dc == 0), stop=(dc == DCH - 1)),
                            deps=[s_l, s_gtw, mms[-1] if mms else prev_s]))
                    s_el = sc.step("scalar", lambda s, t0=t0: s.activation(
                        elb[:, t0:t0 + W5], lps[:, 0:W5], AF.Exp,
                        bias=0.0, scale=1.0), deps=[mms[-1]])
                    prev_weldc = s_el
                    s_sm = sc.step("tensor", lambda te, t0=t0: te.matmul(
                        l2ps[:, 0:W5], self.onesff[:, 0:1],
                        elb[:, t0:t0 + W5],
                        start=True, stop=True), deps=[s_el, self.c_ld[5]])
                    s_rc = sc.step("vector", lambda v: v.reciprocal(
                        rcp[:, 0:W5], l2ps[0:1, 0:W5]), deps=[s_sm])
                    s_bc = sc.step("tensor", lambda te: te.matmul(
                        l3ps[:, 0:W5], self.onesff[0:1, 0:16], rcp[:, 0:W5],
                        start=True, stop=True), deps=[s_rc])
                    s_s = sc.step("vector", lambda v, b=b, t0=t0:
                                  v.tensor_tensor(
                                      sco_b[b][:, t0:t0 + W5],
                                      elb[:, t0:t0 + W5], l3ps[:, 0:W5],
                                      op=ALU.mult), deps=[s_bc, s_el])
                    prev_s = s_s
                    sc_done.append(s_s)
                s_cp = sc.step("vector", lambda v, b=b: v.tensor_copy(
                    wrk[:], sco_b[b][:]), deps=sc_done + [rt_q.get(b - 1)])
                prev = s_cp
                for r in range(cg.mr_rounds):
                    s_m8 = sc.step("vector", lambda v: v.max(
                        out=m8[:], in_=wrk[:]), deps=[prev])
                    prev = sc.step("vector", lambda v: v.match_replace(
                        out=wrk[:], in_to_replace=m8[:], in_values=wrk[:],
                        imm_value=0.0), deps=[s_m8])
                s_mk = sc.step("vector", lambda v, b=b: v.tensor_tensor(
                    msk[:], sco_b[b][:], wrk[:], op=ALU.is_gt), deps=[prev])
                s_cu = sc.step("vector", lambda v: v.tensor_tensor_scan(
                    cum16[:], msk[:], zro16[:], 0.0, op0=ALU.add, op1=ALU.add),
                    deps=[s_mk, z16])
                s_q = sc.step("vector", lambda v: v.tensor_tensor(
                    cum16[:], cum16[:], msk[:], op=ALU.mult), deps=[s_cu])
                prev_tr = None
                for ch in range(SCH):
                    s1 = sc.step("tensor", lambda te, ch=ch: te.transpose(
                        trps[:, 0:16], cum16[:, ch * 128:(ch + 1) * 128],
                        identf[:]),
                        deps=[s_q, prev_tr, l_idf])
                    s2 = sc.step("tensor", lambda te, b=b, ch=ch: te.transpose(
                        trps[:, 16:32], sco_b[b][:, ch * 128:(ch + 1) * 128],
                        identf[:]), deps=[s1])
                    o = (b * SCH + ch) * 16
                    e1 = sc.step("scalar", lambda s, o=o: s.copy(
                        qT[:, o:o + 16], trps[:, 0:16]), deps=[s1, s2])
                    e2 = sc.step("scalar", lambda s, o=o: s.copy(
                        scTb[:, o:o + 16], trps[:, 16:32]), deps=[s2, e1])
                    prev_tr = e2
                rt_q[b] = prev_tr

            if "rt" in self.debug:
                d_dbq = self.ddbg("qt", [128, B * SCH * 16])
                d_dbs = self.ddbg("sct", [128, B * SCH * 16], BF16)
                sc.dma("sync", lambda sy: sy.dma_start(out=d_dbq, in_=qT[:]),
                       deps=[rt_q[bb] for bb in range(B)])
                sc.dma("sync", lambda sy: sy.dma_start(out=d_dbs, in_=scTb[:]),
                       deps=[rt_q[bb] for bb in range(B)])

            self.rt_wc = {}
            self.rt_st = {}
            last_st = None
            prev_wr = None
            prev_wc = None
            for b in range(B):
                for e in range(E):
                    sts = []
                    for chk in range(SCH):
                        o = (b * SCH + chk) * 16 + e
                        sts.append(sc.step("vector",
                            lambda v, chk=chk, o=o: v.tensor_single_scalar(
                                stb[:, chk * CAP:(chk + 1) * CAP],
                                iotacap[:], qT[:, o:o + 1],
                                op=ALU.is_equal),
                            deps=[rt_q[b], last_st, l_iot]))
                    wms = []
                    for chk in range(SCH):
                        o = (b * SCH + chk) * 16 + e
                        wms.append(sc.step("tensor",
                            lambda te, chk=chk, o=o: te.matmul(
                                wrps[:, :], scTb[:, o:o + 1],
                                stb[:, chk * CAP:(chk + 1) * CAP],
                                start=(chk == 0), stop=(chk == SCH - 1)),
                            deps=[sts[chk], wms[-1] if wms else prev_wr]))
                    s_wr = sc.step("scalar", lambda s: s.copy(
                        wrow[:], wrps[:]), deps=[wms[-1]])
                    prev_wr = s_wr
                    wcs = []
                    for cc in range(CAPCH):
                        w = min(128, CAP - cc * 128)
                        wcs.append(sc.step("tensor", lambda te, cc=cc, w=w:
                            te.matmul(wcps[:, cc:cc + 1],
                                      wrow[0:1, cc * 128:cc * 128 + w],
                                      self.onesbf[0:1, 0:1],
                                      start=True, stop=True),
                            deps=[s_wr, wcs[-1] if wcs else prev_wc,
                                  self.c_ld[4]]))
                    o = (b * E + e) * CAPCH
                    self.rt_wc[(b, e)] = sc.step("vector", lambda v, o=o:
                        v.tensor_copy(self.wcol[:, o:o + CAPCH], wcps[:]),
                        deps=wcs)
                    prev_wc = self.rt_wc[(b, e)]
                    last_st = sc.dma("sync", lambda sy, b=b, e=e: sy.dma_start(
                        out=self.d_st[b, e].rearrange("c p k -> p c k"),
                        in_=stb[:].rearrange("p (c k) -> p c k", k=CAP)),
                        deps=sts + [wms[-1]])
                    self.rt_st[(b, e)] = last_st
            if "rt" in self.debug:
                d_dbw = self.ddbg("wcol", [128, B * E * CAPCH])
                sc.dma("sync", lambda sy: sy.dma_start(
                    out=d_dbw, in_=self.wcol[:]),
                    deps=list(self.rt_wc.values()))

    def ph_experts(self):
        sc, cg = self.sc, self.cfg
        B, D, E, F, CAP = cg.b, cg.d, cg.e, cg.f, cg.cap
        DCH, CAPCH = cg.dch, cg.capch
        SCH = cg.seq // 128
        FH = F // 2
        FHCH = FH // 128
        BC = B * CAP
        with ExitStack() as esc:
            wgt = self.sb("wgt", [128, 2 * DCH * FH], BF16, esc)
            wut = self.sb("wut", [128, 2 * DCH * FH], BF16, esc)
            wdt = self.sb("wdt", [128, 2 * FHCH * D], BF16, esc)
            stl = self.sb("stl", [128, B * SCH * CAP], BF16, esc)
            xtok = self.sb("xtok", [128, DCH * BC], BF16, esc)
            hbuf = self.sb("hbuf", [128, 2 * FHCH * BC], BF16, esc)
            eo512 = self.sb("eo512", [128, 512], BF16, esc)
            ggb = self.sb("ggb", [128, BC], F32, esc)
            gps = self.ps("gps", [128, BC], F32, esc)
            ups = self.ps("ups", [128, BC], F32, esc)
            xpsA = self.ps("xpsA", [128, 512], F32, esc)
            xpsB = self.ps("xpsB", [128, 512], F32, esc)
            dps = self.ps("dps", [128, 512], F32, esc)

            stl_use = [None, None]
            w_use = [None, None]
            w_use_d = [None]
            prev_gather = None
            prev_gp = None
            prev_up = None
            prev_hh = None
            ex_last = []
            for e in range(E):
                st_ld = {}
                g_done = {}
                for b in range(B):
                    st_ld[b] = sc.dma("gpsimd",
                        lambda g, b=b, e=e: g.dma_start(
                            out=stl[:, b * SCH * CAP:(b + 1) * SCH * CAP]
                            .rearrange("p (c k) -> p c k", k=CAP),
                            in_=self.d_st[b, e].rearrange("c p k -> p c k")),
                        deps=[self.rt_st[(b, e)], stl_use[b]])
                lw = {}
                for hf in range(2):
                    wbi = (e * 2 + hf) % 2
                    wo_g = wbi * DCH * FH
                    lw[("g", hf)] = sc.dma("gpsimd",
                        lambda g, e=e, hf=hf, wo_g=wo_g: g.dma_start(
                            out=wgt[:, wo_g:wo_g + DCH * FH]
                            .rearrange("p (c f) -> p c f", f=FH),
                            in_=self.d_wgT[e][:, hf * FH:(hf + 1) * FH]
                            .rearrange("(c p) f -> p c f", p=128)),
                        deps=[w_use[wbi]])
                    lw[("u", hf)] = sc.dma("gpsimd",
                        lambda g, e=e, hf=hf, wo_g=wo_g: g.dma_start(
                            out=wut[:, wo_g:wo_g + DCH * FH]
                            .rearrange("p (c f) -> p c f", f=FH),
                            in_=self.d_wuT[e][:, hf * FH:(hf + 1) * FH]
                            .rearrange("(c p) f -> p c f", p=128)),
                        deps=[w_use[wbi]])
                    lw[("d", hf)] = sc.dma("gpsimd",
                        lambda g, e=e, hf=hf: g.dma_start(
                            out=wdt[:, hf * FHCH * D:(hf + 1) * FHCH * D]
                            .rearrange("p (c dd) -> p c dd", dd=D),
                            in_=self.d_wdT[e][hf * FH:(hf + 1) * FH, :]
                            .rearrange("(c p) dd -> p c dd", p=128)),
                        deps=[w_use_d[0]])
                gb_prev = getattr(self, "_gb_prev", [None, None])
                for b in range(B):
                    for dt_ in range(DCH):
                        gpar = dt_ % 2
                        xv = xpsA if gpar == 0 else xpsB
                        gmm = []
                        for chk in range(SCH):
                            gmm.append(sc.step("tensor",
                                lambda te, b=b, dt_=dt_, chk=chk, xv=xv:
                                te.matmul(
                                    xv[:, 0:CAP],
                                    self.xnt(b, chk)[:, dt_ * 128:
                                                     (dt_ + 1) * 128],
                                    stl[:, (b * SCH + chk) * CAP:
                                        (b * SCH + chk + 1) * CAP],
                                    start=(chk == 0), stop=(chk == SCH - 1)),
                                deps=[st_ld[b],
                                      self.ln_done[0] if e == 0 else None,
                                      gmm[-1] if gmm else gb_prev[gpar]]))
                        g_done[(b, dt_)] = sc.step("scalar",
                            lambda s, b=b, dt_=dt_, xv=xv: s.copy(
                                xtok[:, (dt_ * B + b) * CAP:
                                     (dt_ * B + b + 1) * CAP],
                                xv[:, 0:CAP]),
                            deps=[gmm[-1], prev_gather, prev_hh])
                        gb_prev[gpar] = g_done[(b, dt_)]
                        prev_gather = g_done[(b, dt_)]
                    stl_use[b] = g_done[(b, DCH - 1)]
                self._gb_prev = gb_prev
                h_done = {}
                for hf in range(2):
                    wbi = (e * 2 + hf) % 2
                    wo_g = wbi * DCH * FH
                    for ft in range(FHCH):
                        gm = []
                        for dc in range(DCH):
                            ws = wgt[:, wo_g + dc * FH + ft * 128:
                                     wo_g + dc * FH + ft * 128 + 128]
                            gm.append(sc.step("tensor",
                                lambda te, dc=dc, ws=ws: te.matmul(
                                    gps[:, 0:BC], ws,
                                    xtok[:, dc * BC:(dc + 1) * BC],
                                    start=(dc == 0), stop=(dc == DCH - 1)),
                                deps=[lw[("g", hf)]] +
                                     [g_done[(bb, dc)] for bb in range(B)] +
                                     [gm[-1] if gm else prev_gp]))
                        s_gel = sc.step("scalar", lambda s: s.activation(
                            ggb[:], gps[:, 0:BC], AF.Gelu, bias=0.0,
                            scale=1.0), deps=[gm[-1], prev_hh])
                        prev_gp = s_gel
                        um = []
                        for dc in range(DCH):
                            ws = wut[:, wo_g + dc * FH + ft * 128:
                                     wo_g + dc * FH + ft * 128 + 128]
                            um.append(sc.step("tensor",
                                lambda te, dc=dc, ws=ws: te.matmul(
                                    ups[:, 0:BC], ws,
                                    xtok[:, dc * BC:(dc + 1) * BC],
                                    start=(dc == 0), stop=(dc == DCH - 1)),
                                deps=[lw[("u", hf)]] +
                                     [g_done[(bb, dc)] for bb in range(B)] +
                                     [um[-1] if um else prev_up]))
                        s_h = sc.step("vector",
                            lambda v, hf=hf, ft=ft: v.tensor_tensor(
                                hbuf[:, (hf * FHCH + ft) * BC:
                                     (hf * FHCH + ft + 1) * BC],
                                ggb[:], ups[:, 0:BC], op=ALU.mult),
                            deps=[s_gel, um[-1], prev_hh])
                        prev_up = s_h
                        prev_hh = s_h
                        h_done[(hf, ft)] = s_h
                prev_d = None
                eo_prev = getattr(self, "_eo_prev", [None, None])
                for b in range(B):
                    eo_dmas = []
                    for cc in range(CAPCH):
                        o = (b * E + e) * CAPCH + cc
                        for ns in range(D // 512):
                            ebi = 0
                            dmms = []
                            for hf in range(2):
                                for fc in range(FHCH):
                                    hs = hbuf[:, (hf * FHCH + fc) * BC +
                                              b * CAP + cc * 128:
                                              (hf * FHCH + fc) * BC +
                                              b * CAP + cc * 128 + 128]
                                    wsd = wdt[:, hf * FHCH * D + fc * D +
                                              ns * 512:
                                              hf * FHCH * D + fc * D +
                                              ns * 512 + 512]
                                    dmms.append(sc.step("tensor",
                                        lambda te, hs=hs, wsd=wsd,
                                        fst=(hf == 0 and fc == 0),
                                        lst=(hf == 1 and fc == FHCH - 1):
                                        te.matmul(dps[:, 0:512], hs, wsd,
                                                  start=fst, stop=lst),
                                        deps=[lw[("d", hf)],
                                              h_done[(hf, fc)],
                                              dmms[-1] if dmms else prev_d]))
                            s_ev = sc.step("vector",
                                lambda v, ebi=ebi, o=o:
                                v.tensor_single_scalar(
                                    eo512[:, 0:512],
                                    dps[:, 0:512],
                                    self.wcol[:, o:o + 1], op=ALU.mult),
                                deps=[dmms[-1], self.rt_wc[(b, e)],
                                      eo_prev[ebi]])
                            prev_d = s_ev
                            s_dma = sc.dma("gpsimd",
                                lambda g, b=b, e=e, cc=cc, ns=ns, ebi=ebi:
                                g.dma_start(
                                    out=self.d_eo[b, e].rearrange(
                                        "(c p) dd -> p c dd", p=128)
                                    [:, cc, ns * 512:(ns + 1) * 512],
                                    in_=eo512[:, 0:512]),
                                deps=[s_ev])
                            eo_prev[ebi] = s_dma
                            eo_dmas.append(s_dma)
                    ex_last.extend(eo_dmas)
                    self.ex_eo = getattr(self, "ex_eo", {})
                    self.ex_eo[(b, e)] = eo_dmas[-1]
                    self._eo_prev = eo_prev
                w_use[0 if (e * 2) % 2 == 0 else 1] = prev_d
                w_use[(e * 2 + 1) % 2] = prev_d
                w_use_d[0] = prev_d
            self.ex_done = ex_last

    def ph_scatter(self):
        sc, cg = self.sc, self.cfg
        B, SEQ, D, E, CAP = cg.b, cg.seq, cg.d, cg.e, cg.cap
        CAPCH, TT = cg.capch, cg.tt
        with ExitStack() as ssc:
            eob = self.sb("eob", [128, E * CAPCH * D], BF16, ssc)
            sttb = self.sb("sttb", [128, E * CAP], BF16, ssc)
            s2b = self.sb("s2b", [128, E * CAPCH * 128], BF16, ssc)
            ftile = self.sb("ftile", [128, 2 * D], F32, ssc)
            s2ps = self.ps("s2ps", [128, 128], BF16, ssc)
            acc0 = self.ps("acc0", [128, 512], F32, ssc)
            acc1 = self.ps("acc1", [128, 512], F32, ssc)
            accs = [acc0, acc1]
            prev_ft = [None, None]
            prev_stl = None
            for b in range(B):
                l_eo = []
                for e in range(E):
                    l_eo.append(sc.dma("gpsimd", lambda g, b=b, e=e:
                        g.dma_start(
                            out=eob[:, e * CAPCH * D:(e + 1) * CAPCH * D]
                            .rearrange("p (c dd) -> p c dd", dd=D),
                            in_=self.d_eo[b, e].rearrange(
                                "(c p) dd -> p c dd", p=128)),
                        deps=[self.ex_eo[(b, e)], prev_stl]))
                prev_s2t = None
                for t in range(TT):
                    s_stl = sc.dma("gpsimd", lambda g, b=b, t=t:
                        g.dma_start(
                            out=sttb[:].rearrange("p (e k) -> p e k",
                                                  k=CAP),
                            in_=self.d_st[b, :, t, :, :].rearrange(
                                "e p k -> p e k")),
                        deps=[self.rt_st[(b, E - 1)], prev_stl, prev_s2t])
                    s2_done = []
                    prev_s2 = prev_stl
                    for e in range(E):
                        for cc in range(CAPCH):
                            s_t = sc.step("tensor",
                                lambda te, e=e, cc=cc: te.transpose(
                                    s2ps[:], sttb[:, e * CAP + cc * 128:
                                                  e * CAP + cc * 128 + 128],
                                    self.identb[:]),
                                deps=[s_stl, prev_s2, self.c_ld[0]])
                            s_e = sc.step("scalar",
                                lambda s, e=e, cc=cc: s.copy(
                                    s2b[:, (e * CAPCH + cc) * 128:
                                        (e * CAPCH + cc + 1) * 128],
                                    s2ps[:]), deps=[s_t])
                            prev_s2 = s_e
                            prev_s2t = s_e
                            s2_done.append(s_e)
                    evs = []
                    for ns in range(D // 512):
                        mml = []
                        for e in range(E):
                            for cc in range(CAPCH):
                                first = (e == 0 and cc == 0)
                                mml.append(sc.step("tensor",
                                    lambda te, e=e, cc=cc, ns=ns, first=first:
                                    te.matmul(
                                        accs[ns][:],
                                        s2b[:, (e * CAPCH + cc) * 128:
                                            (e * CAPCH + cc + 1) * 128],
                                        eob[:, e * CAPCH * D + cc * D +
                                            ns * 512:
                                            e * CAPCH * D + cc * D +
                                            ns * 512 + 512],
                                        start=first,
                                        stop=(e == E - 1 and
                                              cc == CAPCH - 1)),
                                    deps=[s2_done[e * CAPCH + cc], l_eo[e],
                                          mml[-1] if mml else
                                          prev_ft[ns]]))
                        tbi = t % 2
                        s_ev = sc.step("vector", lambda v, ns=ns, tbi=tbi:
                            v.tensor_copy(
                                ftile[:, tbi * D + ns * 512:
                                      tbi * D + ns * 512 + 512],
                                accs[ns][:]),
                            deps=[mml[-1], prev_ft[ns]])
                        prev_ft[ns] = s_ev
                        evs.append(s_ev)
                    s_fs = sc.dma("sync", lambda sy, b=b, t=t, tbi=t % 2:
                        sy.dma_start(out=self.d_fsc[b, t],
                                     in_=ftile[:, tbi * D:(tbi + 1) * D]),
                        deps=evs)
                    prev_ft = [s_fs, s_fs]
                    prev_stl = s_fs if t == TT - 1 else prev_stl
                    self.fsc_done = getattr(self, "fsc_done", {})
                    self.fsc_done[(b, t)] = s_fs

    def ph_shared(self):
        sc, cg = self.sc, self.cfg
        B, SEQ, D, FSH = cg.b, cg.seq, cg.d, cg.fsh
        DCH, FSHCH, TT = cg.dch, cg.fshch, cg.tt
        with ExitStack() as ssc:
            xnfmb = self.sb("xnfmb", [128, DCH * SEQ], BF16, ssc)
            wsg = self.sb("wsg", [128, DCH * FSH], BF16, ssc)
            wsu = self.sb("wsu", [128, DCH * FSH], BF16, ssc)
            wsd = self.sb("wsd", [128, FSHCH * D], BF16, ssc)
            hsb = self.sb("hsb", [128, FSHCH * 512], BF16, ssc)
            gsg = self.sb("gsg", [128, 512], F32, ssc)
            gtile = self.sb("gtile", [128, D], BF16, ssc)
            ptl = self.sb("ptl", [128, 2 * D], F32, ssc)
            sps = self.ps("sps", [128, 512], F32, ssc)
            sus = self.ps("sus", [128, 512], F32, ssc)
            acc0 = self.ps("sacc0", [128, 512], F32, ssc)
            acc1 = self.ps("sacc1", [128, 512], F32, ssc)
            accs = [acc0, acc1]
            lws = []
            lws.append(sc.dma("gpsimd", lambda g: g.dma_start(
                out=wsg[:].rearrange("p (c f) -> p c f", f=FSH),
                in_=self.d_wsgT.rearrange(
                    "(c p) f -> p c f", p=128))))
            lws.append(sc.dma("gpsimd", lambda g: g.dma_start(
                out=wsu[:].rearrange("p (c f) -> p c f", f=FSH),
                in_=self.d_wsuT.rearrange(
                    "(c p) f -> p c f", p=128))))
            lws.append(sc.dma("gpsimd", lambda g: g.dma_start(
                out=wsd[:].rearrange("p (c dd) -> p c dd", dd=D),
                in_=self.d_wsdT.rearrange(
                    "(c p) dd -> p c dd", p=128))))
            self.g_done = {}
            prev_g = None
            prev_hs = None
            prev_gel = None
            prev_acc = [None, None]
            prev_ptl = [None, None]
            for b in range(B):
                l_fm = []
                for dc in range(DCH):
                    l_fm.append(sc.dma("sync", lambda sy, b=b, dc=dc:
                        sy.dma_start(
                            out=xnfmb[:, dc * SEQ:(dc + 1) * SEQ],
                            in_=self.d_xnfm[b, dc]),
                        deps=[self.ln_fm[(b, dc, TT - 1)],
                              self.g_done.get((b - 1, TT - 1))]))
                for t4 in range(0, TT, 4):
                    nt = min(4, TT - t4)
                    nw4 = nt * 128
                    h_done = []
                    for fc in range(FSHCH):
                        gmm = []
                        for dc in range(DCH):
                            ws = wsg[:, dc * FSH + fc * 128:
                                     dc * FSH + fc * 128 + 128]
                            gmm.append(sc.step("tensor",
                                lambda te, t4=t4, dc=dc, ws=ws, nw4=nw4:
                                te.matmul(
                                    sps[:, 0:nw4], ws,
                                    xnfmb[:, dc * SEQ + t4 * 128:
                                          dc * SEQ + t4 * 128 + nw4],
                                    start=(dc == 0), stop=(dc == DCH - 1)),
                                deps=[lws[0], l_fm[dc],
                                      gmm[-1] if gmm else prev_gel]))
                        s_gel = sc.step("scalar",
                            lambda s, nw4=nw4: s.activation(
                                gsg[:, 0:nw4], sps[:, 0:nw4], AF.Gelu,
                                bias=0.0, scale=1.0),
                            deps=[gmm[-1], prev_hs])
                        prev_gel = s_gel
                        umm = []
                        for dc in range(DCH):
                            ws = wsu[:, dc * FSH + fc * 128:
                                     dc * FSH + fc * 128 + 128]
                            umm.append(sc.step("tensor",
                                lambda te, t4=t4, dc=dc, ws=ws, nw4=nw4:
                                te.matmul(
                                    sus[:, 0:nw4], ws,
                                    xnfmb[:, dc * SEQ + t4 * 128:
                                          dc * SEQ + t4 * 128 + nw4],
                                    start=(dc == 0), stop=(dc == DCH - 1)),
                                deps=[lws[1], l_fm[dc],
                                      umm[-1] if umm else prev_hs]))
                        s_h = sc.step("vector", lambda v, fc=fc, nw4=nw4:
                                      v.tensor_tensor(
                                          hsb[:, fc * 512:fc * 512 + nw4],
                                          gsg[:, 0:nw4], sus[:, 0:nw4],
                                          op=ALU.mult),
                                      deps=[s_gel, umm[-1], prev_hs])
                        prev_hs = s_h
                        h_done.append(s_h)
                    for tt_ in range(nt):
                        t = t4 + tt_
                        tbi = t % 2
                        s_pl = sc.dma("sync", lambda sy, b=b, t=t, tbi=tbi:
                            sy.dma_start(out=ptl[:, tbi * D:(tbi + 1) * D],
                                         in_=self.d_fsc[b, t]),
                            deps=[self.fsc_done[(b, t)], prev_ptl[tbi]])
                        sil = []
                        for ns in range(D // 512):
                            mml = []
                            for fc in range(FSHCH):
                                ws = wsd[:, fc * D + ns * 512:
                                         fc * D + ns * 512 + 512]
                                mml.append(sc.step("tensor",
                                    lambda te, fc=fc, ns=ns, ws=ws, tt_=tt_:
                                    te.matmul(
                                        accs[ns][:],
                                        hsb[:, fc * 512 + tt_ * 128:
                                            fc * 512 + tt_ * 128 + 128],
                                        ws,
                                        start=(fc == 0),
                                        stop=(fc == FSHCH - 1)),
                                    deps=[lws[2], h_done[fc],
                                          mml[-1] if mml else prev_acc[ns]]))
                            s_ad = sc.step("vector",
                                lambda v, ns=ns, tbi=tbi: v.tensor_tensor(
                                    ptl[:, tbi * D + ns * 512:
                                        tbi * D + ns * 512 + 512],
                                    ptl[:, tbi * D + ns * 512:
                                        tbi * D + ns * 512 + 512],
                                    accs[ns][:], op=ALU.add),
                                deps=[mml[-1], s_pl])
                            prev_acc[ns] = s_ad
                            s_si = sc.step("scalar",
                                lambda s, ns=ns, tbi=tbi: s.activation(
                                    gtile[:, ns * 512:(ns + 1) * 512],
                                    ptl[:, tbi * D + ns * 512:
                                        tbi * D + ns * 512 + 512],
                                    AF.Silu, bias=0.0, scale=1.0),
                                deps=[s_ad, prev_g])
                            sil.append(s_si)
                        s_g = sc.dma("sync", lambda sy, b=b, t=t: sy.dma_start(
                            out=self.d_g[b, t], in_=gtile[:]), deps=sil)
                        self.g_done[(b, t)] = s_g
                        prev_g = s_g
                        prev_ptl[tbi] = s_g

    def ph_attention(self):
        import os
        KATT = int(os.environ.get("KATT", "99"))
        sc, cg = self.sc, self.cfg
        B, SEQ, D, H, DH, MEM = cg.b, cg.seq, cg.d, cg.h, cg.dh, cg.mem
        DCH, JP, JCN, TT = cg.dch, cg.jp, cg.jcn, cg.tt
        MEMC = cg.memc
        TOPK = float(cg.topk)
        NBIS = len(cg.bis_sched)
        with ExitStack() as asc:
            wob = self.sb("wob", [128, DCH * D], BF16, asc)
            kbuf = self.sb("kbuf", [128, DCH * JP], BF16, asc)
            vT = self.sb("vT", [128, JCN * D], BF16, asc)
            qtl = self.sb("qtl", [128, 2 * DCH * 128], BF16, asc)
            scr = self.sb("scr", [128, 2 * 2048], BF16, asc)
            csb = self.sb("csb", [128, 2 * JP], BF16, asc)
            pkb = self.sb("pkb", [128, 256], BF16, asc)
            xrow = self.sb("xrow", [128, 16 * JP], BF16, asc)
            bst = self.sb("bst", [128, 16 * 8], F32, asc)
            wjoff = self.sb("wjoff", [128, TT], F32, asc)
            bstb = self.sb("bstb", [128, 16 * 2], BF16, asc)
            patgs = self.sb("patgs", [128, 16 * 128], BF16, asc)
            fpbig = self.sb("fpbig", [8, 256], BF16, asc)
            fpad = self.sb("fpad", [8, 128], BF16, asc)
            atb = self.sb("atb", [128, 2 * 2048], BF16, asc)
            obuf = self.sb("obuf", [128, DCH * 128], BF16, asc)
            gtl = self.sb("gtl", [128, D], BF16, asc)
            xtl = self.sb("xtl", [128, D], F32, asc)
            otl = self.sb("otl", [128, D], F32, asc)
            wstr = self.sb("wstr", [128, 2 * D], BF16, asc)
            xnstr = scr
            vstg = csb
            mstg = self.sb("mstg", [128, MEM], BF16, asc)
            dtp = self.ps("dtp", [128, 512], F32, asc)
            pkp = self.ps("pkp", [128, 1024], BF16, asc)
            pm1 = self.ps("pm1", [128, 512], F32, asc)
            pob = self.ps("pob", [128, 512], F32, asc)
            ovp = self.ps("ovp", [64, 2048], F32, asc)
            DSLOT = [0, 512]

            def bcol(gl, j):
                return bst[:, gl * 8 + j:gl * 8 + j + 1]

            s_lwo = sc.dma("gpsimd", lambda g: g.dma_start(
                out=wob[:].rearrange("p (c o) -> p c o", o=D),
                in_=self.d_wo.rearrange(
                    "(c p) o -> p c o", p=128)))
            wj_prev = None
            for tw in range(TT):
                wjv = 2.0 * TOPK - len(cg.jlist(tw * 16)) * 128
                wj_prev = sc.step("gpsimd",
                    lambda g, tw=tw, wjv=wjv: g.memset(
                        wjoff[:, tw:tw + 1], wjv),
                    deps=[wj_prev])
            s_fb0 = sc.step("gpsimd", lambda gg: gg.memset(fpbig[:], 1.0))
            s_fb1 = sc.step("gpsimd", lambda gg: gg.affine_select(
                fpbig[:], fpbig[:], pattern=[[-1, 256]],
                compare_op=ALU.is_ge, fill=0.0, base=120,
                channel_multiplier=1), deps=[s_fb0])
            s_fbig = sc.step("gpsimd", lambda gg: gg.tensor_scalar(
                fpbig[:], fpbig[:], 57344.0, -57344.0,
                op0=ALU.mult, op1=ALU.add), deps=[s_fb1])
            s_fp0 = sc.step("gpsimd", lambda gg: gg.affine_select(
                fpad[:], self.ones8b[:], pattern=[[-1, 128]],
                compare_op=ALU.is_ge, fill=0.0, base=MEM - 1,
                channel_multiplier=0), deps=[self.c_ld[6]])
            s_fpad = sc.step("gpsimd", lambda gg: gg.tensor_scalar(
                fpad[:], fpad[:], 57344.0, -57344.0,
                op0=ALU.mult, op1=ALU.add), deps=[s_fp0])

            for b in range(B):
                prev_b = getattr(self, "attn_b_done", None)
                zk = sc.step("vector", lambda v: v.memset(kbuf[:], 0.0),
                             deps=[prev_b])
                zv = sc.step("vector", lambda v: v.memset(vT[:], 0.0),
                             deps=[prev_b])
                kd_extra = [zk]
                prev_ms = prev_b
                for oc in range(DCH):
                    l_mk = sc.dma("gpsimd", lambda g, oc=oc: g.dma_start(
                        out=kbuf[:, oc * JP + SEQ:oc * JP + SEQ + MEM],
                        in_=self.d_memk[2 * oc:2 * oc + 2].rearrange(
                            "h d m -> (h d) m")), deps=[zk])
                    kd_extra.append(l_mk)
                    l_mv = sc.dma("gpsimd", lambda g, oc=oc: g.dma_start(
                        out=mstg[:],
                        in_=self.d_memv[2 * oc:2 * oc + 2].rearrange(
                            "h d m -> (h d) m")), deps=[prev_ms])
                    s_mt = sc.step("tensor", lambda te: te.transpose(
                        pkp[0:MEM, 0:128], mstg[:], self.identb[:]),
                        deps=[l_mv, prev_ms, self.c_ld[0]])
                    s_mc = sc.step("scalar", lambda s, oc=oc: s.copy(
                        vT[0:MEM, MEMC * D + oc * 128:
                           MEMC * D + (oc + 1) * 128],
                        pkp[0:MEM, 0:128]), deps=[s_mt, zv])
                    prev_ms = s_mc
                    kd_extra.append(s_mc)
                qkv_done = {"q": {}, "k": {}, "v": {}}
                vt_done = {}
                prev_ev = None
                prev_w = [prev_b, prev_b]
                prev_xn = None
                prev_vstg = prev_ms
                prev_qst = [prev_b, prev_b]
                TW = min(512, SEQ)
                for t5 in range(SEQ // TW):
                    l_xn = sc.dma("sync", lambda sy, b=b, t5=t5, TW=TW:
                        sy.dma_start(
                        out=xnstr[:, 0:DCH * TW]
                        .rearrange("p (c w) -> p c w", c=DCH),
                        in_=self.d_xnfm[b, :, :, t5 * TW:(t5 + 1) * TW]
                        .rearrange("c p w -> p c w")),
                        deps=[self.ln_fm[(b, DCH - 1, TT - 1)], prev_xn,
                              prev_b])
                    prev_ev2 = {0: prev_ev, 1: prev_b}
                    for ni, (nm, dsrc) in enumerate(
                            [("q", self.d_wq), ("k", self.d_wk),
                             ("v", self.d_wv)]):
                        for oc in range(DCH):
                            wbi = (ni * DCH + oc) % 2
                            pbi = (ni * DCH + oc) % 2
                            pml = pm1 if pbi == 0 else pob
                            prev_ev = prev_ev2[pbi]
                            l_w = sc.dma("gpsimd",
                                lambda g, dsrc=dsrc, oc=oc, wbi=wbi:
                                g.dma_start(
                                    out=wstr[:, wbi * D:wbi * D + D]
                                    .rearrange("p (c o) -> p c o", o=128),
                                    in_=dsrc[:, oc * 128:(oc + 1) * 128]
                                    .rearrange("(c p) o -> p c o", p=128)),
                                deps=[prev_w[wbi]])
                            mm = []
                            for dc in range(DCH):
                                ws = wstr[:, wbi * D + dc * 128:
                                          wbi * D + dc * 128 + 128]
                                mm.append(sc.step("tensor",
                                    lambda te, ws=ws, dc=dc, pml=pml:
                                    te.matmul(
                                        pml[:, 0:TW], ws,
                                        xnstr[:, dc * TW:(dc + 1) * TW],
                                        start=(dc == 0),
                                        stop=(dc == DCH - 1)),
                                    deps=[l_w, l_xn,
                                          mm[-1] if mm else prev_ev]))
                            prev_w[wbi] = mm[-1]
                            if nm == "q":
                                pass
                            if nm == "q":
                                qi = oc % 2
                                tgt = qtl[:, qi * TW:(qi + 1) * TW]
                                s_e0 = sc.step("scalar",
                                    lambda s, tgt=tgt, pml=pml: s.activation(
                                        tgt, pml[:, 0:TW], AF.Copy,
                                        bias=0.0, scale=0.125),
                                    deps=[mm[-1], prev_ev, prev_qst[qi]])
                                s_e = sc.dma("sync",
                                    lambda sy, b=b, oc=oc, t5=t5, tgt=tgt,
                                    TW=TW:
                                    sy.dma_start(
                                        out=self.d_qfm[b, oc, :,
                                            t5 * TW:(t5 + 1) * TW],
                                        in_=tgt), deps=[s_e0])
                                prev_qst[qi] = s_e
                                prev_ev = s_e0
                            elif nm == "k":
                                tb = kbuf[:, oc * JP + t5 * TW:
                                          oc * JP + t5 * TW + TW]
                                s_e = sc.step("scalar",
                                    lambda s, tb=tb, pml=pml:
                                    s.copy(tb, pml[:, 0:TW]),
                                    deps=[mm[-1], prev_ev, zk])
                                prev_ev = s_e
                            else:
                                s_e0 = sc.step("scalar",
                                    lambda s, pml=pml:
                                    s.copy(vstg[:, 0:TW], pml[:, 0:TW]),
                                    deps=[mm[-1], prev_ev, prev_vstg])
                                prev_ev = s_e0
                                tr_prev = s_e0
                                for jb in range(TW // 128):
                                    jc0 = t5 * (TW // 128) + jb
                                    s_vt = sc.step("tensor",
                                        lambda te, jb=jb: te.transpose(
                                            pkp[:, 512:640],
                                            vstg[:, jb * 128:(jb + 1) * 128],
                                            self.identb[:]),
                                        deps=[s_e0, tr_prev, prev_ms,
                                              self.c_ld[0]])
                                    s_vc = sc.step("scalar",
                                        lambda s, jc0=jc0, oc=oc: s.copy(
                                            vT[:, jc0 * D + oc * 128:
                                               jc0 * D + (oc + 1) * 128],
                                            pkp[:, 512:640]),
                                        deps=[s_vt, zv])
                                    tr_prev = s_vc
                                    vt_done[(oc, jc0)] = s_vc
                                prev_vstg = tr_prev
                                s_e = tr_prev
                            qkv_done[nm][(oc, t5)] = s_e
                            prev_ev2[pbi] = prev_ev
                    prev_xn = qkv_done["v"][(DCH - 1, t5)]
                for oc in range(DCH):
                    vt_done[(oc, MEMC)] = kd_extra[-1]
                qd = list(qkv_done["q"].values())
                kd = list(qkv_done["k"].values()) + kd_extra

                prev_dt = {0: prev_b, 1: prev_b}
                prev_pk = prev_ms
                pmch = {0: prev_ev2[0], 1: prev_ev2[1]}
                uctr = [0]
                prev_av = {0: prev_b, 1: prev_b}
                prev_oev = prev_b
                prev_out = prev_b
                prev_packT = prev_b
                prev_xr = {}
                prev_pmx = prev_b
                prev_qt = [None, None]
                if KATT < 2:
                    self.attn_b_done = qd[-1]
                    continue
                BTW = 2
                xrow_free = getattr(self, "_xrow_free", {})
                patg_done = getattr(self, "_patg_done", {})
                prev_bstb = getattr(self, "_prev_bstb", {})
                for t in range(TT):
                    jl_t = cg.jlist(t * 16 + 15)
                    njc = len(jl_t)
                    qbi = t % 2
                    s_qt = sc.dma("gpsimd", lambda g, b=b, t=t, qbi=qbi:
                        g.dma_start(
                            out=qtl[:, qbi * DCH * 128:(qbi + 1) * DCH * 128]
                            .rearrange("p (c w) -> p c w", c=DCH),
                            in_=self.d_qfm[b, :, :, t * 128:(t + 1) * 128]
                            .rearrange("c p w -> p c w")),
                        deps=qd + [prev_qt[qbi]])
                    xr_done = {}
                    last_bt = None
                    nblk = (njc + BTW - 1) // BTW
                    for blk in range(nblk):
                        jcs = jl_t[blk * BTW:(blk + 1) * BTW]
                        nw = len(jcs)
                        bt_ev = []
                        for bi, jc in enumerate(jcs):
                            for hg in range(4):
                                mm4 = []
                                for hx in range(4):
                                    hh = hg * 4 + hx
                                    stat = kbuf[(hh % 2) * 64:
                                                (hh % 2) * 64 + 64,
                                                (hh // 2) * JP + jc * 128:
                                                (hh // 2) * JP + jc * 128
                                                + 128]
                                    mov = qtl[(hh % 2) * 64:(hh % 2) * 64
                                              + 64,
                                              qbi * DCH * 128 +
                                              (hh // 2) * 128:
                                              qbi * DCH * 128 +
                                              (hh // 2) * 128 + 128]
                                    mm4.append(sc.step("tensor",
                                        lambda te, stat=stat, mov=mov,
                                        hx=hx:
                                        te.matmul(dtp[:, hx * 128:
                                                      hx * 128 + 128],
                                                  stat, mov,
                                                  start=(hx == 0),
                                                  stop=(hx == 3)),
                                        deps=kd + [s_qt, prev_dt[0],
                                                   mm4[-1] if mm4 else
                                                   None]))
                                s_bt = sc.step("scalar",
                                    lambda s, bi=bi, hg=hg:
                                    s.copy(
                                        scr[:, bi * 2048:(bi + 1) * 2048]
                                        .rearrange("p (i k) -> p i k", k=16)
                                        [:, :, hg * 4:hg * 4 + 4],
                                        dtp[:, 0:512].rearrange(
                                            "p (h i) -> p i h", h=4)),
                                    deps=[mm4[-1], prev_xn,
                                          prev_packT,
                                          prev_qst[0], prev_qst[1]])
                                bt_ev.append(s_bt)
                                prev_dt[0] = s_bt
                        last_bt = bt_ev[-1]
                        for gl in range(16):
                            g = t * 16 + gl
                            gli0 = blk * BTW
                            pks = []
                            evs = []
                            for si, (bi, jc) in enumerate(zip(
                                    range(nw), jcs)):
                                s_pk = sc.step("tensor",
                                    lambda te, bi=bi, gl=gl: te.transpose(
                                        pkp[:, 0:128],
                                        scr[:, bi * 2048 + gl * 128:
                                            bi * 2048 + gl * 128 + 128],
                                        self.identb[:]),
                                    deps=bt_ev + [prev_pk, self.c_ld[0],
                                                  pks[-1] if pks else None])
                                pks.append(s_pk)
                                s_pe = sc.step("vector",
                                    lambda v, si=si: v.tensor_copy(
                                        pkb[:, si * 128:si * 128 + 128],
                                        pkp[:, 0:128]),
                                    deps=[s_pk, prev_pmx])
                                evs.append(s_pe)
                                prev_pk = s_pe
                            fills = []
                            for si, jc in enumerate(jcs):
                                if jc == t:
                                    fills.append((si,
                                        fpbig[:, 120 - 8 * gl:
                                              248 - 8 * gl], s_fbig))
                                elif jc == MEMC:
                                    fills.append((si, fpad[:], s_fpad))
                            tb = uctr[0] % 2
                            uctr[0] += 1
                            tps = pm1 if tb == 0 else pob
                            s_pm = sc.step("tensor", lambda te, nw=nw,
                                tps=tps, bd=bool(fills):
                                te.matmul(tps[:, 0:nw * 128], self.prepat[:],
                                          pkb[:, 0:nw * 128], start=True,
                                          stop=not bd),
                                deps=evs + [self.s_pre, pmch[tb]])
                            prev_pmx = s_pm
                            for fi, (si, fsrc, fdep) in enumerate(fills):
                                s_pm = sc.step("tensor",
                                    lambda te, si=si, fsrc=fsrc, tps=tps,
                                    lastf=(fi == len(fills) - 1):
                                    te.matmul(tps[:, si * 128:si * 128
                                                  + 128],
                                              self.e8c[:], fsrc,
                                              start=False, stop=lastf),
                                    deps=[s_pm, fdep, self.c_ld[2]])
                            s_xr = sc.step("scalar",
                                lambda s, gl=gl, gli0=gli0, nw=nw, tps=tps:
                                s.copy(xrow[:, gl * JP + gli0 * 128:
                                            gl * JP + (gli0 + nw) * 128],
                                       tps[:, 0:nw * 128]),
                                deps=[s_pm, prev_xr.get(gl),
                                      xrow_free.get(gl)])
                            pmch[tb] = s_xr
                            prev_xr[gl] = s_xr
                            xr_done.setdefault(gl, []).append(s_xr)
                            prev_packT = pks[-1]
                    prev_qt[qbi] = last_bt
                    if KATT < 3:
                        continue
                    def bcols(j, g0=0, g1=16):
                        return bst[:].rearrange("p (g o) -> p g o",
                                                o=8)[:, g0:g1, j:j + 1]
                    s_i0 = sc.step("vector", lambda v: v.memset(
                        bcols(0), BIS_LO),
                        deps=[patg_done.get(gl) for gl in range(16)] +
                             [prev_packT])
                    prev_state = [s_i0]
                    prev_cnt2 = getattr(self, "_prev_cnt2", [None, None])
                    cnt_last = {}
                    wjt = njc * 128
                    for it in range(NBIS):
                        c_it = float(16.0 / (1 << it))
                        s_mid = sc.step("vector", lambda v, c_it=c_it:
                            v.tensor_single_scalar(bcols(2), bcols(0), c_it,
                                                   op=ALU.add),
                            deps=prev_state)
                        s_nm = sc.step("vector", lambda v, c_it=c_it:
                            v.tensor_scalar(
                                bcols(5, 8, 16), bcols(0, 8, 16), -1.0,
                                -c_it, op0=ALU.mult, op1=ALU.add),
                            deps=prev_state)
                        cv = []
                        cs = []
                        for gl in range(16):
                            if gl < 8:
                                sv = csb[:, 0:wjt]
                                s_c = sc.step("vector",
                                    lambda v, gl=gl, sv=sv, wjt=wjt:
                                    v.tensor_scalar(
                                        sv, xrow[:, gl * JP:gl * JP + wjt],
                                        bcol(gl, 2), None, op0=ALU.is_ge,
                                        op1=ALU.add, accum_out=bcol(gl, 3)),
                                    deps=[s_mid, prev_cnt2[0], last_bt] +
                                         xr_done.get(gl, []))
                                prev_cnt2[0] = s_c
                                cv.append(s_c)
                            else:
                                sv = csb[:, JP:JP + wjt]
                                s_c = sc.step("scalar",
                                    lambda s, gl=gl, sv=sv, wjt=wjt:
                                    s.activation(
                                        sv, xrow[:, gl * JP:gl * JP + wjt],
                                        AF.Sign, bias=bcol(gl, 5), scale=1.0,
                                        accum_out=bcol(gl, 3)),
                                    deps=[s_nm, prev_cnt2[1], last_bt] +
                                         xr_done.get(gl, []))
                                prev_cnt2[1] = s_c
                                cs.append(s_c)
                            cnt_last[gl] = s_c
                        s_pv = sc.step("vector", lambda v:
                            v.tensor_single_scalar(
                                bcols(4, 0, 8), bcols(3, 0, 8), TOPK,
                                op=ALU.is_ge), deps=cv)
                        s_ps = sc.step("vector", lambda v, t=t:
                            v.tensor_single_scalar(
                                bcols(4, 8, 16), bcols(3, 8, 16),
                                wjoff[:, t:t + 1], op=ALU.is_ge),
                            deps=cs)
                        s_lo = sc.step("vector", lambda v, c_it=c_it:
                            v.scalar_tensor_tensor(
                                bcols(0), bcols(4), c_it, bcols(0),
                                op0=ALU.mult, op1=ALU.add),
                            deps=[s_pv, s_ps])
                        prev_state = [s_lo]
                    self._prev_cnt2 = prev_cnt2
                    bis_fin = prev_state[0]
                    if KATT < 4:
                        continue
                    sm_done = {}
                    for gl in range(16):
                        xr = xrow[:, gl * JP:gl * JP + wjt]
                        s_pexp = sc.step("scalar", lambda s, xr=xr:
                            s.activation(xr, xr, AF.Exp,
                                         bias=self.cexp[:, 0:1],
                                         scale=1.0),
                            deps=[bis_fin, cnt_last[gl]])
                        s_lb = sc.step("vector", lambda v, gl=gl:
                            v.tensor_copy(bstb[:, gl * 2:gl * 2 + 1],
                                          bcol(gl, 0)),
                            deps=[bis_fin, prev_bstb.get(gl)])
                        s_pt0 = sc.step("scalar", lambda s, gl=gl:
                            s.activation(
                                bstb[:, gl * 2 + 1:gl * 2 + 2],
                                bstb[:, gl * 2:gl * 2 + 1], AF.Exp,
                                bias=self.cexp[:, 0:1], scale=1.0),
                            deps=[s_lb, s_pexp])
                        s_pt = sc.step("vector", lambda v, gl=gl:
                            v.tensor_copy(bcol(gl, 6),
                                          bstb[:, gl * 2 + 1:gl * 2 + 2]),
                            deps=[s_pt0])
                        prev_bstb[gl] = s_pt
                        s_z = sc.step("vector", lambda v, xr=xr, gl=gl:
                            v.scalar_tensor_tensor(
                                xr, xr, bcol(gl, 6), xr, op0=ALU.is_gt,
                                op1=ALU.mult, accum_out=bcol(gl, 3)),
                            deps=[s_pexp, s_pt])
                        s_ln = sc.step("scalar", lambda s, gl=gl: s.activation(
                            bcol(gl, 7), bcol(gl, 3), AF.Ln, bias=0.0,
                            scale=1.0), deps=[s_z])
                        s_iv = sc.step("scalar", lambda s, gl=gl: s.activation(
                            bcol(gl, 7), bcol(gl, 7), AF.Exp, bias=0.0,
                            scale=-1.0), deps=[s_ln])
                        sm_done[gl] = (s_z, s_iv)
                    self._prev_bstb = prev_bstb
                    if KATT < 5:
                        continue
                    for gl in range(16):
                        patg_done[gl] = sc.step("vector", lambda v, gl=gl:
                            v.tensor_single_scalar(
                                patgs[:, gl * 128:(gl + 1) * 128],
                                self.postpat[:], bcol(gl, 7), op=ALU.mult),
                            deps=[sm_done[gl][1], self.s_post,
                                  patg_done.get(gl), xrow_free.get(gl)])
                    av_mm = {}
                    for li, jc in enumerate(jl_t):
                        abi = li % 2
                        at_done = []
                        for gq in range(4):
                            tb = gq % 2
                            tps = pm1 if tb == 0 else pob
                            mxs = []
                            for si in range(4):
                                gl = gq * 4 + si
                                mxs.append(sc.step("tensor",
                                    lambda te, gl=gl, li=li, si=si, tps=tps:
                                    te.matmul(
                                        tps[:, si * 128:si * 128 + 128],
                                        xrow[:, gl * JP + li * 128:
                                             gl * JP + li * 128 + 128],
                                        patgs[:, gl * 128:(gl + 1) * 128],
                                        start=(si == 0), stop=(si == 3)),
                                    deps=[patg_done[gl], sm_done[gl][0],
                                          pmch[tb],
                                          mxs[-1] if mxs else None]))
                            at_ev = sc.step("vector",
                                lambda v, gq=gq, abi=abi, tps=tps:
                                v.tensor_copy(
                                    atb[:, abi * 2048:(abi + 1) * 2048]
                                    .rearrange("p (k e i) -> p k e i",
                                               e=16, i=8)
                                    [:, :, gq * 4:gq * 4 + 4, :],
                                    tps[:, 0:512].rearrange(
                                        "p (s i k) -> p k s i", s=4, i=8)),
                                deps=[mxs[-1], prev_av[abi]])
                            at_done.append(at_ev)
                            pmch[tb] = at_ev
                            if li == njc - 1:
                                for si in range(4):
                                    xrow_free[gq * 4 + si] = at_ev
                        avl = []
                        for k2 in range(H):
                            st_fl = (li == 0) and (k2 % 4 == 0)
                            sp_fl = (li == njc - 1) and (k2 % 4 == 3)
                            s_av = sc.step("tensor",
                                lambda te, k2=k2, jc=jc, abi=abi,
                                st_fl=st_fl, sp_fl=sp_fl:
                                te.matmul(
                                    ovp[0:64, k2 * 128:k2 * 128 + 128],
                                    vT[:, jc * D + k2 * 64:
                                       jc * D + k2 * 64 + 64],
                                    atb[:, abi * 2048 + k2 * 128:
                                        abi * 2048 + k2 * 128 + 128],
                                    start=st_fl, stop=sp_fl),
                                deps=at_done + [vt_done[(k2 // 2, jc)],
                                                av_mm.get((k2, li - 1)),
                                                av_mm.get((k2 - 1, li)),
                                                prev_oev if li == 0 else
                                                None])
                            av_mm[(k2, li)] = s_av
                            avl.append(s_av)
                        prev_av[abi] = avl[-1]
                    self._xrow_free = xrow_free
                    self._patg_done = patg_done
                    self._prev_bstb = prev_bstb
                    if KATT < 6:
                        continue
                    oev = []
                    av_last = [av_mm[(kk, njc - 1)] for kk in range(H)]
                    for k2 in range(H):
                        s_oe = sc.step("scalar", lambda s, k2=k2:
                            s.copy(obuf[(k2 % 2) * 64:(k2 % 2) * 64 + 64,
                                        (k2 // 2) * 128:(k2 // 2) * 128 + 128],
                                   ovp[0:64, k2 * 128:k2 * 128 + 128]),
                            deps=av_last + [prev_out,
                                            oev[-1] if oev else None])
                        oev.append(s_oe)
                    prev_oev = oev[-1]
                    s_lx = sc.dma("sync", lambda sy, b=b, t=t: sy.dma_start(
                        out=xtl[:], in_=self.d_x[b, t * 128:(t + 1) * 128, :]),
                        deps=[prev_out])
                    s_lg = sc.dma("gpsimd", lambda g, b=b, t=t: g.dma_start(
                        out=gtl[:], in_=self.d_g[b, t]),
                        deps=[self.g_done[(b, t)], prev_out])
                    cmb = []
                    for ns in range(D // 512):
                        mml = []
                        for c in range(DCH):
                            ws = wob[:, c * D + ns * 512:c * D + ns * 512 + 512]
                            mml.append(sc.step("tensor",
                                lambda te, c=c, ws=ws: te.matmul(
                                    pob[:, 0:512],
                                    obuf[:, c * 128:(c + 1) * 128], ws,
                                    start=(c == 0), stop=False),
                                deps=[s_lwo] + oev +
                                     [mml[-1] if mml else
                                      (cmb[-1] if cmb else pmch[1])]))
                        s_bomm = sc.step("tensor", lambda te, ns=ns:
                            te.matmul(pob[:, 0:512], self.ones8b[0:1, :],
                                      self.bo_rowb[0:1, ns * 512:
                                                    (ns + 1) * 512],
                                      start=False, stop=True),
                            deps=[mml[-1], self.s_bo_ld, self.c_ld[6]])
                        s_c1 = sc.step("vector", lambda v, ns=ns:
                            v.tensor_copy(otl[:, ns * 512:(ns + 1) * 512],
                                          pob[:, 0:512]),
                            deps=[s_bomm, prev_out])
                        s_c2 = sc.step("vector", lambda v, ns=ns:
                            v.tensor_tensor(otl[:, ns * 512:(ns + 1) * 512],
                                            otl[:, ns * 512:(ns + 1) * 512],
                                            gtl[:, ns * 512:(ns + 1) * 512],
                                            op=ALU.mult),
                            deps=[s_c1, s_lg])
                        s_c3 = sc.step("vector", lambda v, ns=ns:
                            v.tensor_tensor(otl[:, ns * 512:(ns + 1) * 512],
                                            otl[:, ns * 512:(ns + 1) * 512],
                                            xtl[:, ns * 512:(ns + 1) * 512],
                                            op=ALU.add),
                            deps=[s_c2, s_lx])
                        cmb.append(s_c1)
                        cmb.append(s_c3)
                    pmch[1] = cmb[-2]
                    prev_out = sc.dma("sync", lambda sy, b=b, t=t:
                        sy.dma_start(
                            out=self.d_out[b, t * 128:(t + 1) * 128, :],
                            in_=otl[:]), deps=[cmb[-1], cmb[-2]])
                self.attn_b_done = prev_out if KATT >= 6 else \
                    (self.sc.last.get("vector"))


def relayout_inputs(cfg, inputs):
    cg = cfg
    x = np.ascontiguousarray(inputs["L0_attn_x"])
    ins = {
        "wq": np.ascontiguousarray(inputs["L0_attn_Wq"]),
        "wk": np.ascontiguousarray(inputs["L0_attn_Wk"]),
        "wv": np.ascontiguousarray(inputs["L0_attn_Wv"]),
        "wo": np.ascontiguousarray(inputs["L0_attn_Wo"]),
        "pre": np.ascontiguousarray(inputs["L0_attn_pre_proj"]),
        "post": np.ascontiguousarray(inputs["L0_attn_post_proj"]),
        "memk": np.ascontiguousarray(
            inputs["L0_attn_mem_k"].transpose(0, 2, 1)),
        "memv": np.ascontiguousarray(
            inputs["L0_attn_mem_v"].transpose(0, 2, 1)),
        "bo": np.ascontiguousarray(inputs["L0_attn_bo"]).reshape(1, -1),
        "gatewT": np.ascontiguousarray(inputs["L0_ffn_gate_w"].T),
        "wgT": np.ascontiguousarray(inputs["L0_ffn_Wg"].transpose(0, 2, 1)),
        "wuT": np.ascontiguousarray(inputs["L0_ffn_Wu"].transpose(0, 2, 1)),
        "wdT": np.ascontiguousarray(inputs["L0_ffn_Wd"].transpose(0, 2, 1)),
        "wsgT": np.ascontiguousarray(inputs["L0_ffn_Ws_g"].T),
        "wsuT": np.ascontiguousarray(inputs["L0_ffn_Ws_u"].T),
        "wsdT": np.ascontiguousarray(inputs["L0_ffn_Ws_d"].T),
    }
    ins.update(host_constants(cfg))
    nb = x.shape[0]
    per_core = nb // cfg.b
    maps = []
    for c in range(per_core):
        m = dict(ins)
        m["x"] = np.ascontiguousarray(x[c * cfg.b:(c + 1) * cfg.b])
        maps.append(m)
    return maps


def kernel(**inputs):
    cfg = Cfg()
    maps = relayout_inputs(cfg, {k: np.asarray(v) for k, v in inputs.items()})
    b = Build(cfg)
    nc = b.build()
    res = run_bass_kernel_spmd(nc, maps, core_ids=list(range(len(maps))))
    outs = [r["out"] for r in res.results]
    return np.concatenate(outs, axis=0)
